# version 32 (final)
# baseline (speedup 1.0000x reference)
import numpy as np
import ml_dtypes
from contextlib import ExitStack

import concourse.bass as bass
import concourse.mybir as mybir
from concourse.bass_utils import run_bass_kernel_spmd

F32 = mybir.dt.float32
BF16 = mybir.dt.bfloat16
AF = mybir.ActivationFunctionType
ALU = mybir.AluOpType
AX = mybir.AxisListType

B, S, D = 4, 4096, 2048
H, HKV, DH = 16, 4, 128
NKVG = H // HKV
CHUNK = 128
GRP = 512
TPG = GRP // 128
EPS = 1e-5
NEXP = 9
I_IN = 1024
NEG_BIG = 1.0e30


class Cfg:
    def __init__(self, n_pre_groups=4, n_main_groups=4):
        self.n_pre = n_pre_groups
        self.n_main = n_main_groups
        self.t_pre = n_pre_groups * GRP
        self.t_main = n_main_groups * GRP
        self.t_half = self.t_main // 2
        self.blk_vals = None


def _alibi_lam():
    i = np.arange(1, H + 1, dtype=np.float32)
    return np.exp2((np.float32(-8.0) * i / np.float32(H)).astype(np.float32))


def rope_tables(p0, n):
    f32 = np.float32
    inv = (f32(1.0) / f32(10000.0) ** (np.arange(0, DH, 2, dtype=f32) / f32(DH))
           ).astype(f32)
    pos = np.arange(p0, p0 + n).astype(f32)
    fr = (pos[None, :] * inv[:, None]).astype(f32)
    cos = np.concatenate([np.cos(fr), np.cos(fr)], 0).astype(f32)
    sin = np.concatenate([np.sin(fr), -np.sin(fr)], 0).astype(f32)
    return cos, sin


def decay_tables():
    f32 = np.float32
    lam = _alibi_lam()
    r = np.arange(CHUNK, dtype=f32)
    dd = np.zeros((CHUNK, H * CHUNK), f32)
    qd = np.zeros((CHUNK, H * CHUNK), f32)
    kd = np.zeros((CHUNK, H), f32)
    bb = np.zeros((CHUNK, H), f32)
    kb = np.zeros((CHUNK, H), f32)
    for h in range(H):
        diff = (r[None, :] - r[:, None]).astype(f32)
        dd[:, h * CHUNK:(h + 1) * CHUNK] = np.where(
            diff >= 0, np.exp((-lam[h] * diff).astype(f32)), f32(0.0))
        qd[:, h * CHUNK:(h + 1) * CHUNK] = np.exp(
            (-lam[h] * (r[None, :] + f32(1.0))).astype(f32))
        kd[:, h] = np.exp((-lam[h] * (f32(CHUNK) - f32(1.0) - r)).astype(f32))
        bb[:, h] = np.exp(np.float32(-lam[h] * f32(CHUNK)))
        kb[:, h] = kd[:, h]
    blk = [float(bb[0, h]) for h in range(H)]
    return dd, qd, kd, blk, bb, kb


class Sem:
    def __init__(self, handle, name):
        self.h = handle
        self.name = name
        self.v = 0


class Buf:
    __slots__ = ("name", "last_w", "readers")
    def __init__(self, name):
        self.name = name
        self.last_w = None
        self.readers = []


class Reg:
    def __init__(self, block):
        self.cache = {}
        self.block = block
        self.pend = {n: [] for n in ("sync", "gpsimd", "tensor", "vector", "scalar")}

    def emit(self, eng_name, fn, reads=(), writes=(), inc=None, inc_n=1,
             pre_waits=()):
        need = {}
        def add(sv):
            if sv is not None and need.get(sv[0].name, (None, -1))[1] < sv[1]:
                need[sv[0].name] = sv
        for sv in pre_waits:
            add(sv)
        for b in reads:
            add(b.last_w)
        for b in writes:
            add(b.last_w)
            for r in b.readers:
                add(r)
        cache = self.cache.setdefault(eng_name, {})
        waits = []
        for s, v in need.values():
            if cache.get(s.name, -1) < v:
                waits.append((s, v))
                cache[s.name] = v
        sv = None
        if inc is not None:
            inc.v += inc_n
            sv = (inc, inc.v)
        self.pend[eng_name].append((waits, fn, inc, inc_n if inc else 0))
        for b in writes:
            if sv is not None:
                b.last_w = sv
            b.readers = []
        if sv is not None:
            for b in reads:
                b.readers.append(sv)
        return sv

    def flush(self):
        for name, items in self.pend.items():
            if not items:
                continue
            batch = tuple(items)
            items.clear()
            @getattr(self.block, name)
            def _(en, _batch=batch):
                for waits, fn, inc, n in _batch:
                    for s_, v_ in waits:
                        en.wait_ge(s_.h, v_)
                    inst = fn(en)
                    if inc is not None:
                        inst.then_inc(inc.h, n)


def build_program(cfg: Cfg):
    nc = bass.Bass("TRN2", target_bir_lowering=False, debug=False)
    blk = cfg.blk_vals
    TM, TP, THF = cfg.t_main, cfg.t_pre, cfg.t_half
    NTILE = TM // 128
    NTH = THF // 128

    dt_in = {}
    def din(name, shape, dtype=F32):
        h = nc.dram_tensor(name, list(shape), dtype, kind="ExternalInput")
        dt_in[name] = h
        return h

    x_in = din("x", (TM, D))
    xpre_in = din("xpre", (TP, D))
    wqT = din("wqT", (D, D))
    wkT = din("wkT", (D, HKV * DH))
    wvT = din("wvT", (D, HKV * DH))
    woT = din("woT", (D, D))
    gwT = din("gwT", (D, 8))
    gb_in = din("gb", (128, 8))
    w1T = din("w1T", (NEXP, D, I_IN))
    w3T = din("w3T", (NEXP, D, I_IN))
    w2T = din("w2T", (NEXP, I_IN, D))
    cosp = din("cosp", (128, TP))
    sinp = din("sinp", (128, TP))
    cosm = din("cosm", (128, TM))
    sinm = din("sinm", (128, TM))
    ddiag_in = din("ddiag", (CHUNK, H * CHUNK))
    qdec_in = din("qdec", (CHUNK, H * CHUNK))
    kdec_in = din("kdec", (CHUNK, H))
    blkbc_in = din("blkbc", (CHUNK, H))
    kdecbc_in = din("kdecbc", (CHUNK, H))
    ident_in = din("ident", (128, 128), BF16)

    out_d = nc.dram_tensor("out", [TM, D], F32, kind="ExternalOutput")
    scr_x2 = nc.dram_tensor("scr_x2", [TM, D], F32, kind="Internal")
    scr_xn2T = nc.dram_tensor("scr_xn2T", [cfg.n_main, 128, 16 * GRP], BF16, kind="Internal")
    scr_o = nc.dram_tensor("scr_o", [cfg.n_main, 128, H * GRP], BF16, kind="Internal")
    scr_wq = nc.dram_tensor("scr_wq", [8, 128, 16 * 256], BF16, kind="Internal")
    scr_wo = nc.dram_tensor("scr_wo", [4, 128, 16 * 512], BF16, kind="Internal")
    b_sx2, b_sxn, b_so = Buf("scr_x2"), Buf("scr_xn2T"), Buf("scr_o")
    b_swq, b_swo = Buf("scr_wq"), Buf("scr_wo")

    with ExitStack() as OUT:
        e = OUT.enter_context
        dma_sems = []
        def dsem(name):
            s_ = Sem(e(nc.semaphore(name)), name)
            dma_sems.append(s_)
            return s_
        s_out = dsem("s_out")
        s_pe = Sem(e(nc.semaphore("s_pe")), "s_pe")
        s_dv = Sem(e(nc.semaphore("s_dv")), "s_dv")
        s_ac = Sem(e(nc.semaphore("s_ac")), "s_ac")
        s_po = Sem(e(nc.semaphore("s_po")), "s_po")
        s_bar = Sem(e(nc.semaphore("s_bar")), "s_bar")

        ps = [e(nc.psum_tensor(f"ps{i}", [128, 512], F32)) for i in range(7)]
        psT = e(nc.psum_tensor("psT", [128, 1024], BF16))
        bps = [Buf(f"ps{i}") for i in range(7)]
        bpsT = Buf("psT")

        ident_s = e(nc.sbuf_tensor("ident_s", [128, 128], BF16))
        gwT_s = e(nc.sbuf_tensor("gwT_s", [128, 16 * 8], BF16))
        gb_s = e(nc.sbuf_tensor("gb_s", [128, 8], F32))
        dw_s = e(nc.sbuf_tensor("dw_s", [128, NTILE * 16], F32))
        mv_s = e(nc.sbuf_tensor("mv_s", [128, 256], F32))
        eps_s = e(nc.sbuf_tensor("eps_s", [128, 1], F32))
        zro_s = e(nc.sbuf_tensor("zro_s", [128, 1], F32))
        lg_s = e(nc.sbuf_tensor("lg_s", [128, 88], F32))
        b_ident, b_gwT, b_gb = Buf("ident_s"), Buf("gwT_s"), Buf("gb_s")
        b_dw = Buf("dw_s")
        b_lg = [Buf("lg0"), Buf("lg1")]
        b_mv = [Buf(f"mv{i}") for i in range(8)]
        b_eps = Buf("eps_s"); b_zro = Buf("zro_s")

        block = e(nc.Block())
        R = Reg(block)
        nbar = [0]

        def barrier():
            R.flush()
            nbar[0] += 1
            tgt = 5 * nbar[0]
            @block.sync
            def _(sy):
                for s_ in dma_sems:
                    if s_.v:
                        sy.wait_ge(s_.h, s_.v)
                sy.nop().then_inc(s_bar.h, 1)
                sy.wait_ge(s_bar.h, tgt)
            @block.gpsimd
            def _(g):
                g.wait_ge(s_po.h, s_po.v)
                g.nop().then_inc(s_bar.h, 1)
                g.wait_ge(s_bar.h, tgt)
            for name, es in (("tensor", s_pe), ("vector", s_dv), ("scalar", s_ac)):
                @getattr(block, name)
                def _(en, _es=es):
                    en.wait_ge(_es.h, _es.v)
                    en.nop().then_inc(s_bar.h, 1)
                    en.wait_ge(s_bar.h, tgt)
            for c in R.cache.values():
                for es in (s_pe, s_dv, s_ac, s_po, *dma_sems):
                    c[es.name] = max(c.get(es.name, -1), es.v)
            s_bar.v = tgt
            for c in R.cache.values():
                c[s_bar.name] = tgt

        def sync_dma(sem, dst_ap, src_ap, reads=(), writes=()):
            pw = [(sem, sem.v)] if sem.v else []
            R.emit("sync", lambda s: s.dma_start(out=dst_ap, in_=src_ap),
                   reads=list(reads), writes=list(writes), inc=sem, inc_n=16,
                   pre_waits=pw)

        def act_dma(sem, dst_ap, src_ap, reads=(), writes=()):
            pw = [(sem, sem.v)] if sem.v else []
            R.emit("scalar", lambda sc: sc.dma_start(out=dst_ap, in_=src_ap),
                   reads=list(reads), writes=list(writes), inc=sem, inc_n=16,
                   pre_waits=pw)

        def wt_dma(sem, dst_ap, src_ap, reads=(), writes=()):
            pw = [(sem, sem.v)] if sem.v else []
            R.emit("gpsimd", lambda g_: g_.dma_start(out=dst_ap, in_=src_ap),
                   reads=list(reads), writes=list(writes), inc=sem, inc_n=16,
                   pre_waits=pw)

        def mm_group(psum_ap, b_psum, pairs, reads):
            def go(t_, _pairs=pairs, _ap=psum_ap):
                inst = None
                n = len(_pairs)
                for i, (l, r_) in enumerate(_pairs):
                    inst = t_.matmul(_ap, l, r_, start=(i == 0), stop=(i == n - 1))
                return inst
            R.emit("tensor", go, reads=list(reads), writes=[b_psum], inc=s_pe)

        def vec(fn, reads=(), writes=()):
            R.emit("vector", fn, reads=list(reads), writes=list(writes), inc=s_dv)

        def act(fn, reads=(), writes=()):
            R.emit("scalar", fn, reads=list(reads), writes=list(writes), inc=s_ac)

        def pool(fn, reads=(), writes=()):
            R.emit("gpsimd", fn, reads=list(reads), writes=list(writes), inc=s_po)

        def ln_stats(src_ap, b_src, slot=0):
            o = 32 * slot
            bmv = b_mv[slot]
            nckk = D // 512
            for i in range(nckk):
                vec(lambda v, _i=i: v.bn_stats(
                    mv_s[:, o + _i * 6:o + (_i + 1) * 6],
                    src_ap[:, _i * 512:(_i + 1) * 512]), [b_src], [bmv])
            vec(lambda v: v.bn_aggr(mv_s[:, o + 24:o + 26], mv_s[:, o:o + 6 * nckk]),
                [bmv], [bmv])
            act(lambda s: s.activation(mv_s[:, o + 26:o + 27], mv_s[:, o + 25:o + 26],
                                       AF.Ln, bias=eps_s[:, 0:1]), [bmv, b_eps], [bmv])
            act(lambda s: s.activation(mv_s[:, o + 1:o + 2], mv_s[:, o + 26:o + 27],
                                       AF.Exp, scale=-0.5), [bmv], [bmv])
            vec(lambda v: v.scalar_tensor_tensor(
                mv_s[:, o + 2:o + 3], mv_s[:, o + 24:o + 25], -1.0,
                mv_s[:, o + 1:o + 2], ALU.mult, ALU.mult), [bmv], [bmv])


        def gate_chain(items):
            def step(fn_builder, eng=vec):
                for tt, sl, tile in items:
                    lo = 44 * sl
                    eng(fn_builder(lo, tile), [b_lg[sl]], [b_lg[sl]])
            step(lambda lo, _t: (lambda v, lo=lo: v.reduce_max(
                lg_s[:, lo + 40:lo + 41], lg_s[:, lo:lo + 8], axis=AX.X)))
            step(lambda lo, _t: (lambda v, lo=lo: v.tensor_scalar_mul(
                lg_s[:, lo + 41:lo + 42], lg_s[:, lo + 40:lo + 41], -1.0)))
            step(lambda lo, _t: (lambda s_, lo=lo: s_.activation(
                lg_s[:, lo:lo + 8], lg_s[:, lo:lo + 8], AF.Exp,
                bias=lg_s[:, lo + 41:lo + 42])), eng=act)
            step(lambda lo, _t: (lambda v, lo=lo: v.reduce_sum(
                lg_s[:, lo + 42:lo + 43], lg_s[:, lo:lo + 8], axis=AX.X)))
            step(lambda lo, _t: (lambda v, lo=lo: v.reciprocal(
                lg_s[:, lo + 43:lo + 44], lg_s[:, lo + 42:lo + 43])))
            step(lambda lo, _t: (lambda v, lo=lo: v.tensor_single_scalar(
                lg_s[:, lo:lo + 8], lg_s[:, lo:lo + 8],
                lg_s[:, lo + 43:lo + 44], ALU.mult)))
            for tt, sl, tile in items:
                lo = 44 * sl
                vec(lambda v, lo=lo: v.tensor_tensor(
                    lg_s[:, lo + 8:lo + 16], lg_s[:, lo:lo + 8],
                    gb_s[:, 0:8], ALU.add), [b_lg[sl], b_gb], [b_lg[sl]])
            step(lambda lo, _t: (lambda v, lo=lo: v.reduce_max(
                lg_s[:, lo + 40:lo + 41], lg_s[:, lo + 8:lo + 16], axis=AX.X)))
            step(lambda lo, _t: (lambda v, lo=lo: v.tensor_single_scalar(
                lg_s[:, lo + 16:lo + 24], lg_s[:, lo + 8:lo + 16],
                lg_s[:, lo + 40:lo + 41], ALU.is_equal)))
            step(lambda lo, _t: (lambda v, lo=lo: v.scalar_tensor_tensor(
                lg_s[:, lo + 24:lo + 32], lg_s[:, lo + 16:lo + 24], -NEG_BIG,
                lg_s[:, lo + 8:lo + 16], ALU.mult, ALU.add)))
            step(lambda lo, _t: (lambda v, lo=lo: v.reduce_max(
                lg_s[:, lo + 40:lo + 41], lg_s[:, lo + 24:lo + 32], axis=AX.X)))
            step(lambda lo, _t: (lambda v, lo=lo: v.tensor_single_scalar(
                lg_s[:, lo + 32:lo + 40], lg_s[:, lo + 24:lo + 32],
                lg_s[:, lo + 40:lo + 41], ALU.is_equal)))
            step(lambda lo, _t: (lambda v, lo=lo: v.tensor_tensor(
                lg_s[:, lo + 16:lo + 24], lg_s[:, lo + 16:lo + 24],
                lg_s[:, lo + 32:lo + 40], ALU.add)))
            for tt, sl, tile in items:
                lo = 44 * sl
                vec(lambda v, lo=lo, _t=tile: v.tensor_tensor(
                    dw_s[:, _t * 16:_t * 16 + 8], lg_s[:, lo + 16:lo + 24],
                    lg_s[:, lo:lo + 8], ALU.mult), [b_lg[sl]], [b_dw])

        def ln_stats_staged(items):
            nckk = D // 512
            for i in range(nckk):
                for src_ap, b_src, slot in items:
                    o = 32 * slot
                    vec(lambda v, _i=i, _s=src_ap, _o=o: v.bn_stats(
                        mv_s[:, _o + _i * 6:_o + (_i + 1) * 6],
                        _s[:, _i * 512:(_i + 1) * 512]), [b_src], [b_mv[slot]])
            for src_ap, b_src, slot in items:
                o = 32 * slot
                vec(lambda v, _o=o: v.bn_aggr(
                    mv_s[:, _o + 24:_o + 26], mv_s[:, _o:_o + 6 * nckk]),
                    [b_mv[slot]], [b_mv[slot]])
            for src_ap, b_src, slot in items:
                o = 32 * slot
                act(lambda s_, _o=o: s_.activation(
                    mv_s[:, _o + 26:_o + 27], mv_s[:, _o + 25:_o + 26],
                    AF.Ln, bias=eps_s[:, 0:1]), [b_mv[slot], b_eps], [b_mv[slot]])
            for src_ap, b_src, slot in items:
                o = 32 * slot
                act(lambda s_, _o=o: s_.activation(
                    mv_s[:, _o + 1:_o + 2], mv_s[:, _o + 26:_o + 27],
                    AF.Exp, scale=-0.5), [b_mv[slot]], [b_mv[slot]])
            for src_ap, b_src, slot in items:
                o = 32 * slot
                vec(lambda v, _o=o: v.scalar_tensor_tensor(
                    mv_s[:, _o + 2:_o + 3], mv_s[:, _o + 24:_o + 25], -1.0,
                    mv_s[:, _o + 1:_o + 2], ALU.mult, ALU.mult),
                    [b_mv[slot]], [b_mv[slot]])

        def ln_apply(dst_ap, src_ap, b_src, b_dst, slot=0):
            o = 32 * slot
            act(lambda s: s.activation(dst_ap, src_ap, AF.Identity,
                                       bias=mv_s[:, o + 2:o + 3],
                                       scale=mv_s[:, o + 1:o + 2]),
                [b_src, b_mv[slot]], [b_dst])

        def transpose_tile(src_ap, b_src, dstT, b_dstT, tt, ntiles):
            dv = dstT[:].rearrange("p (k t) -> p k t", k=16)
            for half in range(2):
                def go(t_, _h=half):
                    inst = None
                    for kk in range(8):
                        k = _h * 8 + kk
                        inst = t_.transpose(psT[:, kk * 128:(kk + 1) * 128],
                                            src_ap[:, k * 128:(k + 1) * 128],
                                            ident_s[:])
                    return inst
                R.emit("tensor", go, reads=[b_src, b_ident], writes=[bpsT],
                       inc=s_pe)
                vec(lambda v, _h=half: v.tensor_copy(
                    dv[:, _h * 8:(_h + 1) * 8, tt * 128:(tt + 1) * 128],
                    psT[:].rearrange("p (k t) -> p k t", k=8)),
                    [bpsT], [b_dstT])

        vec(lambda v: v.memset(eps_s[:], EPS), [], [b_eps])
        vec(lambda v: v.memset(zro_s[:], 0.0), [], [b_zro])
        d_tab = dsem("d_tab")
        d_gw = dsem("d_gw")
        sync_dma(d_tab, ident_s[:], ident_in.ap(), writes=[b_ident])
        sync_dma(d_tab, gb_s[:], gb_in.ap(), writes=[b_gb])
        wt_dma(d_gw, gwT_s[:].rearrange("p (k e) -> p k e", k=16),
               gwT.ap().rearrange("(k p) e -> p k e", p=128), writes=[b_gwT])

        with ExitStack() as PA:
            ea = PA.enter_context
            ddiag_s = ea(nc.sbuf_tensor("ddiag_s", [128, H * 128], BF16))
            qdec_s = ea(nc.sbuf_tensor("qdec_s", [128, H * 128], BF16))
            blkbc_s = ea(nc.sbuf_tensor("blkbc_s", [128, H], BF16))
            kdecbc_s = ea(nc.sbuf_tensor("kdecbc_s", [128, H], BF16))
            M_s = ea(nc.sbuf_tensor("M_s", [128, H * 128], F32))
            Mb_s = ea(nc.sbuf_tensor("Mb_s", [128, H * 128], BF16))
            wkT_s = ea(nc.sbuf_tensor("wkT_s", [128, 16 * 512], BF16))
            wvT_s = ea(nc.sbuf_tensor("wvT_s", [128, 16 * 512], BF16))
            cos_s = ea(nc.sbuf_tensor("cos_s", [128, GRP], F32))
            sin_s = ea(nc.sbuf_tensor("sin_s", [128, GRP], F32))
            x_t = [ea(nc.sbuf_tensor(f"x_t{i}", [128, D], F32)) for i in range(3)]
            xn1_t = [ea(nc.sbuf_tensor(f"xn1_t{i}", [128, D], BF16)) for i in range(4)]
            xn1T_s = ea(nc.sbuf_tensor("xn1T_s", [128, 16 * GRP], BF16))
            wq_buf = [ea(nc.sbuf_tensor(f"wq_buf{i}", [128, 16 * 256], BF16)) for i in range(2)]
            q_fm = ea(nc.sbuf_tensor("q_fm", [128, H * GRP], BF16))
            k_fm = ea(nc.sbuf_tensor("k_fm", [128, HKV * GRP], BF16))
            v_s = ea(nc.sbuf_tensor("v_s", [128, TPG * 512], BF16))
            o_fm = ea(nc.sbuf_tensor("o_fm", [128, H * GRP], BF16))
            st_s = ea(nc.sbuf_tensor("st_s", [128, H * 128], BF16))
            qt_g = ea(nc.sbuf_tensor("qt_g", [128, H * GRP], BF16))
            kt_s = ea(nc.sbuf_tensor("kt_s", [128, H * 128], BF16))
            rtmp = [[ea(nc.sbuf_tensor(f"rtmp{i}_{j}", [128, 512], F32))
                     for j in range(2)] for i in range(2)]
            k_tok_s = ea(nc.sbuf_tensor("k_tok_s", [128, TPG * 512], BF16))

            b_ddiag, b_qdec, b_kdec = Buf("dd"), Buf("qd"), Buf("kd")
            b_blkbc = Buf("bb"); b_kdecbc = Buf("kb")
            b_M, b_Mb = Buf("M"), Buf("Mb")
            b_wk, b_wv = Buf("wk"), Buf("wv")
            b_wq = [Buf("wq0"), Buf("wq1")]
            b_cos, b_sin = Buf("cos"), Buf("sin")
            b_xt = [Buf(f"xt{i}") for i in range(3)]
            b_xn1 = [Buf(f"xn1_{i}") for i in range(4)]
            b_xn1T = Buf("xn1T")
            b_q, b_k, b_v, b_o = Buf("q"), Buf("k"), Buf("v"), Buf("o")
            b_st, b_qt, b_kt = Buf("st"), Buf("qt"), Buf("kt")
            b_rt = [[Buf(f"rt{i}_{j}") for j in range(2)] for i in range(2)]
            b_ktok = Buf("ktok")

            d_cos = dsem("d_cos"); d_sin = dsem("d_sin")
            d_wqc = dsem("d_wqc"); d_wst = dsem("d_wst")
            d_x0 = dsem("d_x0"); d_x1 = dsem("d_x1")
            d_wq = dsem("d_wq"); d_so = dsem("d_so")
            wt_dma(d_gw, ddiag_s[0:CHUNK, :], ddiag_in.ap(), writes=[b_ddiag])
            wt_dma(d_gw, qdec_s[0:CHUNK, :], qdec_in.ap(), writes=[b_qdec])
            wt_dma(d_gw, blkbc_s[0:CHUNK, :], blkbc_in.ap(), writes=[b_blkbc])
            wt_dma(d_gw, kdecbc_s[0:CHUNK, :], kdecbc_in.ap(), writes=[b_kdecbc])
            wt_dma(d_gw, wkT_s[:].rearrange("p (k e) -> p k e", k=16),
                   wkT.ap().rearrange("(k p) e -> p k e", p=128), writes=[b_wk])
            wt_dma(d_gw, wvT_s[:].rearrange("p (k e) -> p k e", k=16),
                   wvT.ap().rearrange("(k p) e -> p k e", p=128), writes=[b_wv])
            vec(lambda v: v.memset(M_s[:], 0.0), [], [b_M])
            vec(lambda v: v.memset(Mb_s[:], 0.0), [], [b_Mb])
            def rope_dve(si, ps_ap, b_ps, dst_s, b_dst, col0, qdec_h=None):
                t1, u = rtmp[si][0], rtmp[si][1]
                bt1, bu = b_rt[si][0], b_rt[si][1]
                vec(lambda v: v.tensor_tensor(t1[:], ps_ap, cos_s[:], ALU.mult),
                    [b_ps, b_cos], [bt1])
                vec(lambda v: v.tensor_tensor(
                    u[0:64, :], ps_ap[64:128, :], sin_s[64:128, :], ALU.mult),
                    [b_ps, b_sin], [bu])
                vec(lambda v: v.tensor_tensor(
                    u[64:128, :], ps_ap[0:64, :], sin_s[0:64, :], ALU.mult),
                    [b_ps, b_sin], [bu])
                pool(lambda g: g.tensor_tensor(
                    dst_s[:, col0:col0 + GRP], t1[:], u[:], ALU.add),
                    [bt1, bu], [b_dst])
                if qdec_h is not None:
                    qdv = qdec_s[:, qdec_h * 128:(qdec_h + 1) * 128].rearrange(
                        "p (r i) -> p r i", r=1).broadcast_to([128, TPG, 128])
                    pool(lambda g, _qdv=qdv: g.tensor_tensor(
                        qt_g[:, col0:col0 + GRP].rearrange(
                            "p (r i) -> p r i", r=TPG),
                        dst_s[:, col0:col0 + GRP].rearrange(
                            "p (r i) -> p r i", r=TPG),
                        _qdv, ALU.mult), [b_dst, b_qdec], [b_qt])

            def ln_steps(ga2):
                is_pre2 = ga2 < cfg.n_pre
                g2 = ga2 if is_pre2 else ga2 - cfg.n_pre
                src_x2 = xpre_in if is_pre2 else x_in
                def a1_load(tt):
                    xts, bx = x_t[tt % 3], b_xt[tt % 3]
                    row0 = g2 * GRP + tt * 128
                    act_dma(d_x0 if tt % 2 == 0 else d_x1, xts[:],
                            src_x2.ap()[row0:row0 + 128, :], writes=[bx])
                def a1_apply(tt):
                    xts, bx = x_t[tt % 3], b_xt[tt % 3]
                    ln_apply(xn1_t[tt % 4][:], xts[:], bx, b_xn1[tt % 4],
                             tt % 4)
                def st(tt):
                    ln_stats_staged([(x_t[tt % 3][:], b_xt[tt % 3], tt % 4)])
                early = [
                    lambda: (a1_load(0), a1_load(1), st(0)),
                    lambda: (st(1), a1_apply(0), a1_load(2)),
                    lambda: (st(2), a1_apply(1), a1_load(3)),
                    lambda: (st(3), a1_apply(2)),
                    lambda: (a1_apply(3),),
                ]
                def late():
                    for tt in range(TPG):
                        transpose_tile(xn1_t[tt % 4][:], b_xn1[tt % 4],
                                       xn1T_s, b_xn1T, tt, TPG)
                return early, late

            def lnphase(ga2):
                early, late = ln_steps(ga2)
                for f in early:
                    f()
                late()

            pend_st = []
            for ga in range(cfg.n_pre + cfg.n_main):
                R.flush()
                is_pre = ga < cfg.n_pre
                g = ga if is_pre else ga - cfg.n_pre
                src_x = xpre_in if is_pre else x_in
                cos_in = cosp if is_pre else cosm
                sin_in = sinp if is_pre else sinm

                sync_dma(d_cos, cos_s[:], cos_in.ap()[:, g * GRP:(g + 1) * GRP], writes=[b_cos])
                sync_dma(d_sin, sin_s[:], sin_in.ap()[:, g * GRP:(g + 1) * GRP], writes=[b_sin])
                for _k, st_fn in pend_st:
                    st_fn()
                pend_st = []

                if ga == 0:
                    lnphase(0)

                if ga + 1 < cfg.n_pre + cfg.n_main:
                    ln_early, ln_late = ln_steps(ga + 1)
                else:
                    ln_early, ln_late = [lambda: None] * 5, lambda: None

                for kv in range(HKV):
                    si = kv % 2
                    bk_ = kv % 2
                    pairs = [(wkT_s[:, k * 512 + kv * 128: k * 512 + (kv + 1) * 128],
                              xn1T_s[:, k * GRP:(k + 1) * GRP]) for k in range(16)]
                    mm_group(ps[bk_][:, 0:GRP], bps[bk_], pairs, [b_wk, b_xn1T])
                    rope_dve(si, ps[bk_][:, 0:GRP], bps[bk_], k_fm, b_k, kv * GRP)
                ln_early[0]()
                for tt in range(TPG):
                    bk_ = 2 + tt % 2
                    pairs = [(xn1T_s[:, k * GRP + tt * 128: k * GRP + (tt + 1) * 128],
                              wvT_s[:, k * 512:(k + 1) * 512]) for k in range(16)]
                    mm_group(ps[bk_][:, :], bps[bk_], pairs, [b_wv, b_xn1T])
                    vec(lambda v, _tt=tt, _b=bk_: v.tensor_copy(
                        v_s[:, _tt * 512:(_tt + 1) * 512], ps[_b][:, :]),
                        [bps[bk_]], [b_v])
                ln_early[1]()
                for cpair in range(TPG // 2):
                    def goT2(t_, _cp=cpair):
                        inst = None
                        for j in range(2):
                            c_ = _cp * 2 + j
                            for kv in range(HKV):
                                inst = t_.transpose(
                                    psT[:, j * 512 + kv * 128: j * 512 + (kv + 1) * 128],
                                    k_fm[:, kv * GRP + c_ * 128: kv * GRP + (c_ + 1) * 128],
                                    ident_s[:])
                        return inst
                    R.emit("tensor", goT2, reads=[b_k, b_ident], writes=[bpsT],
                           inc=s_pe)
                    act(lambda s_, _cp=cpair: s_.activation(
                        k_tok_s[:, _cp * 1024:(_cp + 1) * 1024], psT[:, :],
                        AF.Copy), [bpsT], [b_ktok])
                ln_early[2]()
                if is_pre:
                    ln_early[3]()
                    ln_early[4]()
                if not is_pre:
                    for eighth in range(8):
                        wqb, bwq = wq_buf[eighth % 2], b_wq[eighth % 2]
                        sync_dma(d_wq, wqb[:], scr_wq.ap()[eighth],
                                 reads=[b_swq], writes=[bwq])
                        for hh in range(2):
                            m = eighth * 2 + hh
                            si = m % 2
                            bk_ = m % 2
                            pairs = [(wqb[:, k * 256 + hh * 128: k * 256 + (hh + 1) * 128],
                                      xn1T_s[:, k * GRP:(k + 1) * GRP]) for k in range(16)]
                            mm_group(ps[bk_][:, 0:GRP], bps[bk_], pairs, [bwq, b_xn1T])
                            rope_dve(si, ps[bk_][:, 0:GRP], bps[bk_], q_fm, b_q,
                                     m * GRP, qdec_h=m)
                        if eighth == 2:
                            ln_early[3]()
                        elif eighth == 5:
                            ln_early[4]()

                ln_late()

                for c in range(TPG):
                    if not is_pre:
                        for bk in range(4):
                            def goS(t_, _bk=bk, _c=c):
                                inst = None
                                for hh in range(4):
                                    h = _bk * 4 + hh
                                    inst = t_.matmul(
                                        ps[_bk][:, hh * 128:(hh + 1) * 128],
                                        k_fm[:, (h // NKVG) * GRP + _c * 128:
                                             (h // NKVG) * GRP + (_c + 1) * 128],
                                        q_fm[:, h * GRP + _c * 128:
                                             h * GRP + (_c + 1) * 128],
                                        start=True, stop=True)
                                return inst
                            R.emit("tensor", goS, reads=[b_k, b_q],
                                   writes=[bps[bk]], inc=s_pe)
                            vec(lambda v, _bk=bk: v.tensor_tensor(
                                st_s[:, _bk * 512:(_bk + 1) * 512], ps[_bk][:, :],
                                ddiag_s[:, _bk * 512:(_bk + 1) * 512], ALU.mult),
                                [bps[bk], b_ddiag], [b_st])
                        ov = o_fm[:].rearrange("p (h t) -> p h t", h=H)
                        for bk in range(4):
                            def goO(t_, _bk=bk, _c=c):
                                inst = None
                                for hh in range(4):
                                    h = _bk * 4 + hh
                                    t_.matmul(
                                        ps[_bk][:, hh * 128:(hh + 1) * 128],
                                        v_s[:, _c * 512 + (h // NKVG) * 128:
                                            _c * 512 + (h // NKVG + 1) * 128],
                                        st_s[:, h * 128:(h + 1) * 128],
                                        start=True, stop=False)
                                    inst = t_.matmul(
                                        ps[_bk][:, hh * 128:(hh + 1) * 128],
                                        Mb_s[:, h * 128:(h + 1) * 128],
                                        qt_g[:, h * GRP + _c * 128:
                                             h * GRP + (_c + 1) * 128],
                                        start=False, stop=True)
                                return inst
                            R.emit("tensor", goO,
                                   reads=[b_v, b_st, b_Mb, b_qt],
                                   writes=[bps[bk]], inc=s_pe)
                            act(lambda s_, _bk=bk, _c=c: s_.activation(
                                ov[:, _bk * 4:(_bk + 1) * 4, _c * 128:(_c + 1) * 128],
                                ps[_bk][:].rearrange("p (h t) -> p h t", h=4),
                                AF.Copy), [bps[bk]], [b_o])
                    ktr_rep = k_tok_s[:, c * 512:(c + 1) * 512].rearrange(
                        "p (g d) -> p g d", g=HKV).rearrange(
                        "p g (r d) -> p g r d", r=1).broadcast_to(
                        [128, HKV, NKVG, 128])
                    kdr = kdecbc_s[:].rearrange("p (g r) -> p g r", g=HKV)\
                        .rearrange("p g (r u) -> p g r u", u=1)\
                        .broadcast_to([128, HKV, NKVG, 128])
                    pool(lambda g, _kr=ktr_rep, _kd=kdr: g.tensor_tensor(
                        kt_s[:].rearrange("p (g r d) -> p g r d", g=HKV, r=NKVG),
                        _kr, _kd, ALU.mult), [b_ktok, b_kdecbc], [b_kt])
                    bank_sets = [(4, 0), (5, 4), (6, 8), (4, 12)]
                    for bk, h0 in bank_sets:
                        brep = blkbc_s[:, h0:h0 + 4].rearrange(
                            "p (h u) -> p h u", u=1).broadcast_to([128, 4, 128])
                        mv_ = M_s[:, h0 * 128:(h0 + 4) * 128].rearrange(
                            "p (h d) -> p h d", h=4)
                        if (h0 // 4) % 2 == 0:
                            vec(lambda v, _b=brep, _m=mv_: v.tensor_tensor(
                                _m, _m, _b, ALU.mult), [b_M, b_blkbc], [b_M])
                        else:
                            pool(lambda g, _b=brep, _m=mv_: g.tensor_tensor(
                                _m, _m, _b, ALU.mult), [b_M, b_blkbc], [b_M])
                        def goA(t_, _bk=bk, _h0=h0, _c=c):
                            inst = None
                            for hh in range(4):
                                h = _h0 + hh
                                inst = t_.matmul(
                                    ps[_bk][:, hh * 128:(hh + 1) * 128],
                                    kt_s[:, h * 128:(h + 1) * 128],
                                    v_s[:, _c * 512 + (h // NKVG) * 128:
                                        _c * 512 + (h // NKVG + 1) * 128],
                                    start=True, stop=True)
                            return inst
                        R.emit("tensor", goA, reads=[b_kt, b_v],
                               writes=[bps[bk]], inc=s_pe)
                        vec(lambda v, _h0=h0, _bk=bk: v.tensor_tensor(
                            M_s[:, _h0 * 128:(_h0 + 4) * 128],
                            M_s[:, _h0 * 128:(_h0 + 4) * 128],
                            ps[_bk][:, :], ALU.add), [b_M, bps[bk]], [b_M])
                    if (not is_pre) or (ga == cfg.n_pre - 1 and c == TPG - 1):
                        act(lambda s_: s_.activation(Mb_s[:], M_s[:], AF.Copy),
                            [b_M], [b_Mb])

                if not is_pre:
                    sync_dma(d_so, scr_o.ap()[g], o_fm[:], reads=[b_o], writes=[b_so])

                if is_pre:
                    def flush_pend(key):
                        for k_, fn in list(pend_st):
                            if k_ == key:
                                fn()
                                pend_st.remove((k_, fn))
                    for qq in range(8):
                        if qq % cfg.n_pre != ga:
                            continue
                        bi = (qq // 4) % 2
                        wqb, bwq = wq_buf[bi], b_wq[bi]
                        flush_pend(("wq", bi))
                        wt_dma(d_wqc, wqb[:].rearrange("p (k e) -> p k e", k=16),
                               wqT.ap()[:, qq * 256:(qq + 1) * 256]
                               .rearrange("(k p) e -> p k e", p=128), writes=[bwq])
                        pend_st.append((("wq", bi),
                                        lambda _q=qq, _w=wqb, _b=bwq: sync_dma(
                            d_wst, scr_wq.ap()[_q], _w[:], reads=[_b],
                            writes=[b_swq])))
                    for qq in range(4):
                        if qq % cfg.n_pre != ga:
                            continue
                        flush_pend("wo")
                        wt_dma(d_wqc, q_fm[:].rearrange("p (k e) -> p k e", k=16),
                               woT.ap()[:, qq * 512:(qq + 1) * 512]
                               .rearrange("(k p) e -> p k e", p=128), writes=[b_q])
                        pend_st.append(("wo", lambda _q=qq: sync_dma(
                            d_wst, scr_wo.ap()[_q], q_fm[:], reads=[b_q],
                            writes=[b_swo])))

        barrier()

        with ExitStack() as PA2:
            ea = PA2.enter_context
            o_g = ea(nc.sbuf_tensor("o_g", [128, H * GRP], BF16))
            wo_buf = [ea(nc.sbuf_tensor(f"wo_buf{i}", [128, 16 * 512], BF16)) for i in range(2)]
            attn_g = [ea(nc.sbuf_tensor(f"attn_g{i}", [128, TPG * D], F32)) for i in range(3)]
            x2_t = [ea(nc.sbuf_tensor(f"x2_t{i}", [128, D], F32)) for i in range(4)]
            xn2_t = [ea(nc.sbuf_tensor(f"xn2_t{i}", [128, D], BF16)) for i in range(2)]
            xn2T_s = ea(nc.sbuf_tensor("xn2T_s", [128, 16 * GRP], BF16))
            b_og = Buf("og")
            b_ag = [[Buf(f"ag{i}_{t}") for t in range(TPG)] for i in range(3)]
            b_wo = [Buf("wo0"), Buf("wo1")]
            b_x2 = [Buf(f"x2t{i}") for i in range(4)]
            b_xn2 = [Buf("xn2a"), Buf("xn2b")]
            b_xn2T_t = [Buf(f"xn2T{t}") for t in range(TPG)]
            d_og = dsem("d_og"); d_wo = dsem("d_wo")
            d_x20 = dsem("d_x20"); d_x21 = dsem("d_x21")
            d_sx2 = dsem("d_sx2"); d_sxn = dsem("d_sxn")

            def a2_outproj(g):
                ag, bag = attn_g[g % 3], b_ag[g % 3]
                sync_dma(d_og, o_g[:], scr_o.ap()[g], reads=[b_so], writes=[b_og])
                for dp in range(4):
                    wob, bwo = wo_buf[dp % 2], b_wo[dp % 2]
                    sync_dma(d_wo, wob[:], scr_wo.ap()[dp],
                             reads=[b_swo], writes=[bwo])
                    for tt in range(TPG):
                        pairs = [(o_g[:, h * GRP + tt * 128: h * GRP + (tt + 1) * 128],
                                  wob[:, h * 512:(h + 1) * 512]) for h in range(16)]
                        pp = (dp * TPG + tt) % 2
                        mm_group(ps[pp][:, :], bps[pp], pairs, [b_og, bwo])
                        act(lambda s_, _tt=tt, _dp=dp, _pp=pp, _ag=ag: s_.activation(
                            _ag[:, _tt * D + _dp * 512: _tt * D + (_dp + 1) * 512],
                            ps[_pp][:, :], AF.Copy), [bps[pp]], [bag[tt]])

            def a2_tile(g):
                ag, bag = attn_g[g % 3], b_ag[g % 3]
                items = [(ag[:, tt * D:(tt + 1) * D], bag[tt], tt % 4)
                         for tt in range(TPG)]
                for tt in range(TPG):
                    xts, bx = x2_t[tt % 4], b_x2[tt % 4]
                    row0 = g * GRP + tt * 128
                    act_dma(d_x20 if tt % 2 == 0 else d_x21, xts[:],
                            x_in.ap()[row0:row0 + 128, :], writes=[bx])
                ln_stats_staged(items)
                for tt in range(TPG):
                    a_ap = ag[:, tt * D:(tt + 1) * D]
                    ln_apply(a_ap, a_ap, bag[tt], bag[tt], tt % 4)
                    xts, bx = x2_t[tt % 4], b_x2[tt % 4]
                    vec(lambda v, _ap=a_ap, _x=xts: v.tensor_tensor(
                        _x[:], _x[:], _ap, ALU.add), [bx, bag[tt]], [bx])
                    row0 = g * GRP + tt * 128
                    sync_dma(d_sx2, scr_x2.ap()[row0:row0 + 128, :], xts[:],
                             reads=[bx], writes=[b_sx2])
                items2 = [(x2_t[tt % 4][:], b_x2[tt % 4], tt % 4)
                          for tt in range(TPG)]
                ln_stats_staged(items2)
                for tt in range(TPG):
                    xts, bx = x2_t[tt % 4], b_x2[tt % 4]
                    ln_apply(xn2_t[tt % 2][:], xts[:], bx, b_xn2[tt % 2], tt % 4)
                    transpose_tile(xn2_t[tt % 2][:], b_xn2[tt % 2], xn2T_s,
                                   b_xn2T_t[tt], tt, TPG)
                for tt in range(TPG):
                    sl = tt % 2
                    pairs = [(xn2T_s[:, k * GRP + tt * 128: k * GRP + (tt + 1) * 128],
                              gwT_s[:, k * 8:(k + 1) * 8]) for k in range(16)]
                    mm_group(ps[4 + sl][:, 0:8], bps[4 + sl], pairs,
                             [b_xn2T_t[tt], b_gwT])
                    vec(lambda v, lo=44 * sl, _sl=sl: v.tensor_copy(
                        lg_s[:, lo:lo + 8], ps[4 + _sl][:, 0:8]),
                        [bps[4 + sl]], [b_lg[sl]])
                    if sl == 1 or tt == TPG - 1:
                        t0_ = tt - sl
                        gate_chain([(t0_ + j, (t0_ + j) % 2, g * TPG + t0_ + j)
                                    for j in range(sl + 1)])
                sync_dma(d_sxn, scr_xn2T.ap()[g], xn2T_s[:],
                         reads=list(b_xn2T_t), writes=[b_sxn])


            a2_outproj(0)
            for g in range(cfg.n_main):
                R.flush()
                if g + 1 < cfg.n_main:
                    a2_outproj(g + 1)
                a2_tile(g)

        barrier()

        with ExitStack() as PB:
            eb = PB.enter_context
            xn2T_h = eb(nc.sbuf_tensor("xn2T_h", [128, 16 * THF], BF16))
            y_s = eb(nc.sbuf_tensor("y_s", [128, NTH * D], F32))
            h_s_ = [eb(nc.sbuf_tensor(f"h_s{i}", [128, 8 * THF], BF16)) for i in range(2)]
            w1_buf = [eb(nc.sbuf_tensor(f"w1_buf{i}", [128, 16 * 128], BF16)) for i in range(2)]
            w3_buf = [eb(nc.sbuf_tensor(f"w3_buf{i}", [128, 16 * 128], BF16)) for i in range(2)]
            w2_buf = [eb(nc.sbuf_tensor(f"w2_buf{i}", [128, 8 * 512], BF16)) for i in range(2)]
            silu_t = eb(nc.sbuf_tensor("silu_t", [128, 512], F32))
            x2b_t = [eb(nc.sbuf_tensor(f"x2b_t{i}", [128, D], F32)) for i in range(2)]
            b_xnh, b_y = Buf("xnh"), Buf("y")
            b_h_ = [Buf("h0"), Buf("h1")]
            b_w1 = [Buf("w1b0"), Buf("w1b1")]
            b_w3 = [Buf("w3b0"), Buf("w3b1")]
            b_w2 = [Buf("w2b0"), Buf("w2b1")]
            b_silu = Buf("silu")
            b_x2b = [Buf("x2b0"), Buf("x2b1")]
            d_xnh = dsem("d_xnh"); d_w1 = dsem("d_w1"); d_w3 = dsem("d_w3")
            d_w2 = dsem("d_w2"); d_x2b = dsem("d_x2b"); d_x2c = dsem("d_x2c")

            t_pieces = [(i * 512, min(512, THF - i * 512))
                        for i in range((THF + 511) // 512)]

            def load_xnh(hf):
                xv = xn2T_h[:].rearrange("p (k t) -> p k t", k=16)
                t0 = hf * THF
                g0, g1 = t0 // GRP, (t0 + THF - 1) // GRP
                for g in range(g0, g1 + 1):
                    lo = max(t0, g * GRP)
                    hi = min(t0 + THF, (g + 1) * GRP)
                    sync_dma(d_xnh, xv[:, :, lo - t0:hi - t0],
                             scr_xn2T.ap()[g].rearrange("p (k t) -> p k t", k=16)
                             [:, :, lo - g * GRP:hi - g * GRP],
                             reads=[b_sxn], writes=[b_xnh])

            def emit_h(hf, ex, hb, ici_order=None, skips=(), out_cb=None):
                h_s, b_h = h_s_[hb], b_h_[hb]
                R.flush()
                for ici in (ici_order if ici_order is not None else range(8)):
                    if out_cb is not None:
                        out_cb(ici)
                    w1b, bw1 = w1_buf[ici % 2], b_w1[ici % 2]
                    w3b, bw3 = w3_buf[ici % 2], b_w3[ici % 2]
                    if ici not in skips:
                        wt_dma(d_w1, w1b[:].rearrange("p (k i) -> p k i", k=16),
                               w1T.ap()[ex][:, ici * 128:(ici + 1) * 128]
                               .rearrange("(k p) i -> p k i", p=128), writes=[bw1])
                        wt_dma(d_w3, w3b[:].rearrange("p (k i) -> p k i", k=16),
                               w3T.ap()[ex][:, ici * 128:(ici + 1) * 128]
                               .rearrange("(k p) i -> p k i", p=128), writes=[bw3])
                    if True:
                        for tpi, (tp0, tpl) in enumerate(t_pieces):
                            p1 = [(w1b[:, k * 128:(k + 1) * 128],
                                   xn2T_h[:, k * THF + tp0: k * THF + tp0 + tpl])
                                  for k in range(16)]
                            p3 = [(w3b[:, k * 128:(k + 1) * 128],
                                   xn2T_h[:, k * THF + tp0: k * THF + tp0 + tpl])
                                  for k in range(16)]
                            pp = ((ici * len(t_pieces) + tpi) % 2) * 2
                            mm_group(ps[pp][:, 0:tpl], bps[pp], p1, [bw1, b_xnh])
                            mm_group(ps[pp + 1][:, 0:tpl], bps[pp + 1], p3, [bw3, b_xnh])
                            act(lambda s, _pp=pp, _l=tpl: s.activation(
                                silu_t[:, 0:_l], ps[_pp][:, 0:_l], AF.Silu),
                                [bps[pp]], [b_silu])
                            vec(lambda v, _pp=pp, _l=tpl, _i=ici, _t0=tp0:
                                v.tensor_tensor(
                                    h_s[:, _i * THF + _t0: _i * THF + _t0 + _l],
                                    silu_t[:, 0:_l], ps[_pp + 1][:, 0:_l], ALU.mult),
                                [b_silu, bps[pp + 1]], [b_h])

            def emit_y(hf, ex, hb, dq_order=None, skips=(), init=None):
                h_s, b_h = h_s_[hb], b_h_[hb]
                if init is None:
                    init = (ex == 8)
                for dq4 in (dq_order if dq_order is not None else range(4)):
                    w2b, bw2 = w2_buf[dq4 % 2], b_w2[dq4 % 2]
                    if dq4 not in skips:
                        wt_dma(d_w2, w2b[:].rearrange("p (k d) -> p k d", k=8),
                               w2T.ap()[ex][:, dq4 * 512:(dq4 + 1) * 512]
                               .rearrange("(k p) d -> p k d", p=128), writes=[bw2])
                    if True:
                        for tt in range(NTH):
                            dcol = dq4 * 512
                            pairs = [(h_s[:, ic * THF + tt * 128: ic * THF + (tt + 1) * 128],
                                      w2b[:, ic * 512:(ic + 1) * 512])
                                     for ic in range(8)]
                            pp = 4 + (dq4 * NTH + tt) % 2
                            mm_group(ps[pp][:, :], bps[pp], pairs, [b_h, bw2])
                            ysl = y_s[:, tt * D + dcol: tt * D + dcol + 512]
                            tile = hf * NTH + tt
                            if init and ex == 8:
                                act(lambda s, _y=ysl, _pp=pp: s.activation(
                                    _y, ps[_pp][:, :], AF.Copy),
                                    [bps[pp]], [b_y])
                            elif init:
                                act(lambda s, _y=ysl, _pp=pp, _t=tile, _e=ex:
                                    s.activation(
                                        _y, ps[_pp][:, :], AF.Identity,
                                        bias=zro_s[:, 0:1],
                                        scale=dw_s[:, _t * 16 + _e: _t * 16 + _e + 1]),
                                    [bps[pp], b_dw, b_zro], [b_y])
                            elif ex == 8:
                                vec(lambda v, _y=ysl, _pp=pp: v.tensor_tensor(
                                    _y, _y, ps[_pp][:, :], ALU.add),
                                    [bps[pp], b_y], [b_y])
                            else:
                                vec(lambda v, _y=ysl, _pp=pp, _t=tile, _e=ex:
                                    v.scalar_tensor_tensor(
                                        _y, ps[_pp][:, :],
                                        dw_s[:, _t * 16 + _e: _t * 16 + _e + 1],
                                        _y, ALU.mult, ALU.add),
                                    [bps[pp], b_dw, b_y], [b_y])

            def emit_out_stats(hf):
                for tt in range(2):
                    row0 = hf * THF + tt * 128
                    act_dma(d_x2b if tt % 2 == 0 else d_x2c,
                            x2b_t[tt % 2][:], scr_x2.ap()[row0:row0 + 128, :],
                            reads=[b_sx2], writes=[b_x2b[tt % 2]])
                ln_stats_staged([(y_s[:, tt * D:(tt + 1) * D], b_y, tt % 8)
                                 for tt in range(NTH)])

            def emit_out_tile(hf, tt):
                if tt >= NTH:
                    return
                row0 = hf * THF + tt * 128
                x2b, bx2b = x2b_t[tt % 2], b_x2b[tt % 2]
                o = 32 * (tt % 8)
                zbc = zro_s[:, 0:1].broadcast_to([128, D])
                vec(lambda v, _x=x2b, _b=o, _z=zbc: v.scalar_tensor_tensor(
                    _x[:], _x[:], mv_s[:, _b + 2:_b + 3], _z,
                    ALU.add, ALU.add),
                    [bx2b, b_mv[tt % 8], b_zro], [bx2b])
                vec(lambda v, _x=x2b, _t=tt, _b=o: v.scalar_tensor_tensor(
                    _x[:], y_s[:, _t * D:(_t + 1) * D],
                    mv_s[:, _b + 1:_b + 2], _x[:], ALU.mult, ALU.add),
                    [b_y, b_mv[tt % 8], bx2b], [bx2b])
                pw = [(s_out, s_out.v)] if s_out.v else []
                R.emit("sync", lambda s, _r=row0, _x=x2b: s.dma_start(
                    out=out_d.ap()[_r:_r + 128, :], in_=_x[:]),
                    reads=[bx2b], writes=[], inc=s_out, inc_n=16,
                    pre_waits=pw)
                if tt + 2 < NTH:
                    row2 = hf * THF + (tt + 2) * 128
                    act_dma(d_x2b if tt % 2 == 0 else d_x2c,
                            x2b[:], scr_x2.ap()[row2:row2 + 128, :],
                            reads=[b_sx2], writes=[bx2b])

            def emit_out(hf):
                emit_out_stats(hf)
                for tt in range(NTH):
                    emit_out_tile(hf, tt)

            load_xnh(0)
            for ei, ex in enumerate([8] + list(range(8))):
                emit_h(0, ex, ei % 2)
                emit_y(0, ex, ei % 2)
            load_xnh(1)
            emit_h(1, 7, 1, ici_order=[7, 6, 5, 4, 3, 2, 1, 0], skips=(7, 6))
            emit_out_stats(0)
            h_seq = [6, 5, 4, 3, 2, 1, 0, 8]
            emit_h(1, h_seq[0], 0, out_cb=lambda c: emit_out_tile(0, c))
            emit_y(1, 7, 1, dq_order=[3, 2, 1, 0], skips=(3, 2), init=True)
            for i in range(1, 8):
                emit_h(1, h_seq[i], i % 2)
                emit_y(1, h_seq[i - 1], (i - 1) % 2, init=False)
            emit_y(1, 8, 1, init=False)
            emit_out(1)

            R.flush()
            @block.sync
            def _(sy):
                sy.wait_ge(s_out.h, s_out.v)

    return nc


def _stage_inputs(cfg, L0_attn_x, wq, wk, wv, wo, gate_w, gate_b,
                  W1, W2, W3, Ws1, Ws2, Ws3):
    dd, qd, kd, blk, bb, kb = decay_tables()
    cfg.blk_vals = blk
    f32 = np.float32
    shared = {
        "wqT": np.ascontiguousarray(wq.T, f32),
        "wkT": np.ascontiguousarray(wk.T, f32),
        "wvT": np.ascontiguousarray(wv.T, f32),
        "woT": np.ascontiguousarray(wo.T, f32),
        "gwT": np.ascontiguousarray(gate_w.T, f32),
        "gb": np.ascontiguousarray(np.broadcast_to(gate_b, (128, 8)), f32),
        "w1T": np.ascontiguousarray(
            np.concatenate([W1.transpose(0, 2, 1), Ws1.T[None]], 0), f32),
        "w3T": np.ascontiguousarray(
            np.concatenate([W3.transpose(0, 2, 1), Ws3.T[None]], 0), f32),
        "w2T": np.ascontiguousarray(
            np.concatenate([W2.transpose(0, 2, 1), Ws2.T[None]], 0), f32),
        "ddiag": dd, "qdec": qd, "kdec": kd, "blkbc": bb, "kdecbc": kb,
        "ident": np.eye(128, dtype=ml_dtypes.bfloat16),
    }
    cos0, sin0 = rope_tables(0, cfg.t_pre)
    in_maps = []
    for c in range(8):
        b, half = c // 2, c % 2
        m0 = half * cfg.t_main
        cosm, sinm = rope_tables(m0, cfg.t_main)
        im = dict(shared)
        im["x"] = np.ascontiguousarray(L0_attn_x[b, m0:m0 + cfg.t_main, :], f32)
        im["xpre"] = (np.ascontiguousarray(L0_attn_x[b, 0:cfg.t_pre, :], f32)
                      if half == 1 else np.zeros((cfg.t_pre, D), f32))
        im["cosp"], im["sinp"] = cos0, sin0
        im["cosm"], im["sinm"] = cosm, sinm
        in_maps.append(im)
    return in_maps


def kernel(L0_attn_x, L0_attn_wq, L0_attn_wk, L0_attn_wv, L0_attn_wo,
           L0_ffn_gate_w, L0_ffn_gate_b, L0_ffn_W1, L0_ffn_W2, L0_ffn_W3,
           L0_ffn_Ws1, L0_ffn_Ws2, L0_ffn_Ws3):
    cfg = Cfg(4, 4)
    in_maps = _stage_inputs(cfg, np.asarray(L0_attn_x), np.asarray(L0_attn_wq),
                            np.asarray(L0_attn_wk), np.asarray(L0_attn_wv),
                            np.asarray(L0_attn_wo), np.asarray(L0_ffn_gate_w),
                            np.asarray(L0_ffn_gate_b), np.asarray(L0_ffn_W1),
                            np.asarray(L0_ffn_W2), np.asarray(L0_ffn_W3),
                            np.asarray(L0_ffn_Ws1), np.asarray(L0_ffn_Ws2),
                            np.asarray(L0_ffn_Ws3))
    nc = build_program(cfg)
    res = run_bass_kernel_spmd(nc, in_maps, core_ids=list(range(8)))
    out = np.empty((B, S, D), np.float32)
    for c in range(8):
        b, half = c // 2, c % 2
        out[b, half * cfg.t_main:(half + 1) * cfg.t_main, :] = \
            np.asarray(res.results[c]["out"], np.float32)
    return out


# version 33
# speedup vs baseline: 1.0456x; 1.0456x over previous
import numpy as np
import ml_dtypes
from contextlib import ExitStack

import concourse.bass as bass
import concourse.mybir as mybir
from concourse.bass_utils import run_bass_kernel_spmd

F32 = mybir.dt.float32
BF16 = mybir.dt.bfloat16
AF = mybir.ActivationFunctionType
ALU = mybir.AluOpType
AX = mybir.AxisListType

B, S, D = 4, 4096, 2048
H, HKV, DH = 16, 4, 128
NKVG = H // HKV
CHUNK = 128
GRP = 512
TPG = GRP // 128
EPS = 1e-5
NEXP = 9
I_IN = 1024
NEG_BIG = 1.0e30


class Cfg:
    def __init__(self, n_pre_groups=4, n_main_groups=4):
        self.n_pre = n_pre_groups
        self.n_main = n_main_groups
        self.t_pre = n_pre_groups * GRP
        self.t_main = n_main_groups * GRP
        self.t_half = self.t_main // 2
        self.blk_vals = None


def _alibi_lam():
    i = np.arange(1, H + 1, dtype=np.float32)
    return np.exp2((np.float32(-8.0) * i / np.float32(H)).astype(np.float32))


def rope_tables(p0, n):
    f32 = np.float32
    inv = (f32(1.0) / f32(10000.0) ** (np.arange(0, DH, 2, dtype=f32) / f32(DH))
           ).astype(f32)
    pos = np.arange(p0, p0 + n).astype(f32)
    fr = (pos[None, :] * inv[:, None]).astype(f32)
    cos = np.concatenate([np.cos(fr), np.cos(fr)], 0).astype(f32)
    sin = np.concatenate([np.sin(fr), -np.sin(fr)], 0).astype(f32)
    return cos, sin


def decay_tables():
    f32 = np.float32
    lam = _alibi_lam()
    r = np.arange(CHUNK, dtype=f32)
    dd = np.zeros((CHUNK, H * CHUNK), f32)
    qd = np.zeros((CHUNK, H * CHUNK), f32)
    kd = np.zeros((CHUNK, H), f32)
    bb = np.zeros((CHUNK, H), f32)
    kb = np.zeros((CHUNK, H), f32)
    for h in range(H):
        diff = (r[None, :] - r[:, None]).astype(f32)
        dd[:, h * CHUNK:(h + 1) * CHUNK] = np.where(
            diff >= 0, np.exp((-lam[h] * diff).astype(f32)), f32(0.0))
        qd[:, h * CHUNK:(h + 1) * CHUNK] = np.exp(
            (-lam[h] * (r[None, :] + f32(1.0))).astype(f32))
        kd[:, h] = np.exp((-lam[h] * (f32(CHUNK) - f32(1.0) - r)).astype(f32))
        bb[:, h] = np.exp(np.float32(-lam[h] * f32(CHUNK)))
        kb[:, h] = kd[:, h]
    blk = [float(bb[0, h]) for h in range(H)]
    return dd, qd, kd, blk, bb, kb


class Sem:
    def __init__(self, handle, name):
        self.h = handle
        self.name = name
        self.v = 0


class Buf:
    __slots__ = ("name", "last_w", "readers")
    def __init__(self, name):
        self.name = name
        self.last_w = None
        self.readers = []


class Reg:
    def __init__(self, block):
        self.cache = {}
        self.block = block
        self.pend = {n: [] for n in ("sync", "gpsimd", "tensor", "vector", "scalar")}

    def emit(self, eng_name, fn, reads=(), writes=(), inc=None, inc_n=1,
             pre_waits=()):
        need = {}
        def add(sv):
            if sv is not None and need.get(sv[0].name, (None, -1))[1] < sv[1]:
                need[sv[0].name] = sv
        for sv in pre_waits:
            add(sv)
        for b in reads:
            add(b.last_w)
        for b in writes:
            add(b.last_w)
            for r in b.readers:
                add(r)
        cache = self.cache.setdefault(eng_name, {})
        waits = []
        for s, v in need.values():
            if cache.get(s.name, -1) < v:
                waits.append((s, v))
                cache[s.name] = v
        sv = None
        if inc is not None:
            inc.v += inc_n
            sv = (inc, inc.v)
        self.pend[eng_name].append((waits, fn, inc, inc_n if inc else 0))
        for b in writes:
            if sv is not None:
                b.last_w = sv
            b.readers = []
        if sv is not None:
            for b in reads:
                b.readers.append(sv)
        return sv

    def flush(self):
        for name, items in self.pend.items():
            if not items:
                continue
            batch = tuple(items)
            items.clear()
            @getattr(self.block, name)
            def _(en, _batch=batch):
                for waits, fn, inc, n in _batch:
                    for s_, v_ in waits:
                        en.wait_ge(s_.h, v_)
                    inst = fn(en)
                    if inc is not None:
                        inst.then_inc(inc.h, n)


def build_program(cfg: Cfg):
    nc = bass.Bass("TRN2", target_bir_lowering=False, debug=False)
    blk = cfg.blk_vals
    TM, TP, THF = cfg.t_main, cfg.t_pre, cfg.t_half
    NTILE = TM // 128
    NTH = THF // 128

    dt_in = {}
    def din(name, shape, dtype=F32):
        h = nc.dram_tensor(name, list(shape), dtype, kind="ExternalInput")
        dt_in[name] = h
        return h

    x_in = din("x", (TM, D))
    xpre_in = din("xpre", (TP, D))
    wqT = din("wqT", (D, D))
    wkT = din("wkT", (D, HKV * DH))
    wvT = din("wvT", (D, HKV * DH))
    woT = din("woT", (D, D))
    gwT = din("gwT", (D, 8))
    gb_in = din("gb", (128, 8))
    w1T = din("w1T", (NEXP, D, I_IN))
    w3T = din("w3T", (NEXP, D, I_IN))
    w2T = din("w2T", (NEXP, I_IN, D))
    cosp = din("cosp", (128, TP))
    sinp = din("sinp", (128, TP))
    cosm = din("cosm", (128, TM))
    sinm = din("sinm", (128, TM))
    ddiag_in = din("ddiag", (CHUNK, H * CHUNK))
    qdec_in = din("qdec", (CHUNK, H * CHUNK))
    kdec_in = din("kdec", (CHUNK, H))
    blkbc_in = din("blkbc", (CHUNK, H))
    kdecbc_in = din("kdecbc", (CHUNK, H))
    ident_in = din("ident", (128, 128), BF16)

    out_d = nc.dram_tensor("out", [TM, D], F32, kind="ExternalOutput")
    scr_x2 = nc.dram_tensor("scr_x2", [TM, D], F32, kind="Internal")
    scr_xn2T = nc.dram_tensor("scr_xn2T", [cfg.n_main, 128, 16 * GRP], BF16, kind="Internal")
    scr_o = nc.dram_tensor("scr_o", [cfg.n_main, 128, H * GRP], BF16, kind="Internal")
    scr_wq = nc.dram_tensor("scr_wq", [8, 128, 16 * 256], BF16, kind="Internal")
    scr_wo = nc.dram_tensor("scr_wo", [4, 128, 16 * 512], BF16, kind="Internal")
    b_sx2, b_sxn, b_so = Buf("scr_x2"), Buf("scr_xn2T"), Buf("scr_o")
    b_swq, b_swo = Buf("scr_wq"), Buf("scr_wo")

    with ExitStack() as OUT:
        e = OUT.enter_context
        dma_sems = []
        def dsem(name):
            s_ = Sem(e(nc.semaphore(name)), name)
            dma_sems.append(s_)
            return s_
        s_out = dsem("s_out")
        s_pe = Sem(e(nc.semaphore("s_pe")), "s_pe")
        s_dv = Sem(e(nc.semaphore("s_dv")), "s_dv")
        s_ac = Sem(e(nc.semaphore("s_ac")), "s_ac")
        s_po = Sem(e(nc.semaphore("s_po")), "s_po")
        s_bar = Sem(e(nc.semaphore("s_bar")), "s_bar")

        ps = [e(nc.psum_tensor(f"ps{i}", [128, 512], F32)) for i in range(7)]
        psT = e(nc.psum_tensor("psT", [128, 1024], BF16))
        bps = [Buf(f"ps{i}") for i in range(7)]
        bpsT = Buf("psT")

        ident_s = e(nc.sbuf_tensor("ident_s", [128, 128], BF16))
        gwT_s = e(nc.sbuf_tensor("gwT_s", [128, 16 * 8], BF16))
        gb_s = e(nc.sbuf_tensor("gb_s", [128, 8], F32))
        dw_s = e(nc.sbuf_tensor("dw_s", [128, NTILE * 16], F32))
        mv_s = e(nc.sbuf_tensor("mv_s", [128, 256], F32))
        eps_s = e(nc.sbuf_tensor("eps_s", [128, 1], F32))
        zro_s = e(nc.sbuf_tensor("zro_s", [128, 1], F32))
        lg_s = e(nc.sbuf_tensor("lg_s", [128, 88], F32))
        b_ident, b_gwT, b_gb = Buf("ident_s"), Buf("gwT_s"), Buf("gb_s")
        b_dw = Buf("dw_s")
        b_lg = [Buf("lg0"), Buf("lg1")]
        b_mv = [Buf(f"mv{i}") for i in range(8)]
        b_eps = Buf("eps_s"); b_zro = Buf("zro_s")

        block = e(nc.Block())
        R = Reg(block)
        nbar = [0]

        def barrier():
            R.flush()
            nbar[0] += 1
            tgt = 5 * nbar[0]
            @block.sync
            def _(sy):
                for s_ in dma_sems:
                    if s_.v:
                        sy.wait_ge(s_.h, s_.v)
                sy.nop().then_inc(s_bar.h, 1)
                sy.wait_ge(s_bar.h, tgt)
            @block.gpsimd
            def _(g):
                g.wait_ge(s_po.h, s_po.v)
                g.nop().then_inc(s_bar.h, 1)
                g.wait_ge(s_bar.h, tgt)
            for name, es in (("tensor", s_pe), ("vector", s_dv), ("scalar", s_ac)):
                @getattr(block, name)
                def _(en, _es=es):
                    en.wait_ge(_es.h, _es.v)
                    en.nop().then_inc(s_bar.h, 1)
                    en.wait_ge(s_bar.h, tgt)
            for c in R.cache.values():
                for es in (s_pe, s_dv, s_ac, s_po, *dma_sems):
                    c[es.name] = max(c.get(es.name, -1), es.v)
            s_bar.v = tgt
            for c in R.cache.values():
                c[s_bar.name] = tgt

        def sync_dma(sem, dst_ap, src_ap, reads=(), writes=()):
            pw = [(sem, sem.v)] if sem.v else []
            R.emit("sync", lambda s: s.dma_start(out=dst_ap, in_=src_ap),
                   reads=list(reads), writes=list(writes), inc=sem, inc_n=16,
                   pre_waits=pw)

        def act_dma(sem, dst_ap, src_ap, reads=(), writes=()):
            pw = [(sem, sem.v)] if sem.v else []
            R.emit("scalar", lambda sc: sc.dma_start(out=dst_ap, in_=src_ap),
                   reads=list(reads), writes=list(writes), inc=sem, inc_n=16,
                   pre_waits=pw)

        def wt_dma(sem, dst_ap, src_ap, reads=(), writes=()):
            pw = [(sem, sem.v)] if sem.v else []
            R.emit("gpsimd", lambda g_: g_.dma_start(out=dst_ap, in_=src_ap),
                   reads=list(reads), writes=list(writes), inc=sem, inc_n=16,
                   pre_waits=pw)

        def mm_group(psum_ap, b_psum, pairs, reads):
            def go(t_, _pairs=pairs, _ap=psum_ap):
                inst = None
                n = len(_pairs)
                for i, (l, r_) in enumerate(_pairs):
                    inst = t_.matmul(_ap, l, r_, start=(i == 0), stop=(i == n - 1))
                return inst
            R.emit("tensor", go, reads=list(reads), writes=[b_psum], inc=s_pe)

        def vec(fn, reads=(), writes=()):
            R.emit("vector", fn, reads=list(reads), writes=list(writes), inc=s_dv)

        def act(fn, reads=(), writes=()):
            R.emit("scalar", fn, reads=list(reads), writes=list(writes), inc=s_ac)

        def pool(fn, reads=(), writes=()):
            R.emit("gpsimd", fn, reads=list(reads), writes=list(writes), inc=s_po)

        def ln_stats(src_ap, b_src, slot=0):
            o = 32 * slot
            bmv = b_mv[slot]
            nckk = D // 512
            for i in range(nckk):
                vec(lambda v, _i=i: v.bn_stats(
                    mv_s[:, o + _i * 6:o + (_i + 1) * 6],
                    src_ap[:, _i * 512:(_i + 1) * 512]), [b_src], [bmv])
            vec(lambda v: v.bn_aggr(mv_s[:, o + 24:o + 26], mv_s[:, o:o + 6 * nckk]),
                [bmv], [bmv])
            act(lambda s: s.activation(mv_s[:, o + 26:o + 27], mv_s[:, o + 25:o + 26],
                                       AF.Ln, bias=eps_s[:, 0:1]), [bmv, b_eps], [bmv])
            act(lambda s: s.activation(mv_s[:, o + 1:o + 2], mv_s[:, o + 26:o + 27],
                                       AF.Exp, scale=-0.5), [bmv], [bmv])
            vec(lambda v: v.scalar_tensor_tensor(
                mv_s[:, o + 2:o + 3], mv_s[:, o + 24:o + 25], -1.0,
                mv_s[:, o + 1:o + 2], ALU.mult, ALU.mult), [bmv], [bmv])


        def gate_chain(items):
            def step(fn_builder, eng=vec):
                for tt, sl, tile in items:
                    lo = 44 * sl
                    eng(fn_builder(lo, tile), [b_lg[sl]], [b_lg[sl]])
            step(lambda lo, _t: (lambda v, lo=lo: v.reduce_max(
                lg_s[:, lo + 40:lo + 41], lg_s[:, lo:lo + 8], axis=AX.X)))
            step(lambda lo, _t: (lambda v, lo=lo: v.tensor_scalar_mul(
                lg_s[:, lo + 41:lo + 42], lg_s[:, lo + 40:lo + 41], -1.0)))
            step(lambda lo, _t: (lambda s_, lo=lo: s_.activation(
                lg_s[:, lo:lo + 8], lg_s[:, lo:lo + 8], AF.Exp,
                bias=lg_s[:, lo + 41:lo + 42])), eng=act)
            step(lambda lo, _t: (lambda v, lo=lo: v.reduce_sum(
                lg_s[:, lo + 42:lo + 43], lg_s[:, lo:lo + 8], axis=AX.X)))
            step(lambda lo, _t: (lambda v, lo=lo: v.reciprocal(
                lg_s[:, lo + 43:lo + 44], lg_s[:, lo + 42:lo + 43])))
            step(lambda lo, _t: (lambda v, lo=lo: v.tensor_single_scalar(
                lg_s[:, lo:lo + 8], lg_s[:, lo:lo + 8],
                lg_s[:, lo + 43:lo + 44], ALU.mult)))
            for tt, sl, tile in items:
                lo = 44 * sl
                vec(lambda v, lo=lo: v.tensor_tensor(
                    lg_s[:, lo + 8:lo + 16], lg_s[:, lo:lo + 8],
                    gb_s[:, 0:8], ALU.add), [b_lg[sl], b_gb], [b_lg[sl]])
            step(lambda lo, _t: (lambda v, lo=lo: v.reduce_max(
                lg_s[:, lo + 40:lo + 41], lg_s[:, lo + 8:lo + 16], axis=AX.X)))
            step(lambda lo, _t: (lambda v, lo=lo: v.tensor_single_scalar(
                lg_s[:, lo + 16:lo + 24], lg_s[:, lo + 8:lo + 16],
                lg_s[:, lo + 40:lo + 41], ALU.is_equal)))
            step(lambda lo, _t: (lambda v, lo=lo: v.scalar_tensor_tensor(
                lg_s[:, lo + 24:lo + 32], lg_s[:, lo + 16:lo + 24], -NEG_BIG,
                lg_s[:, lo + 8:lo + 16], ALU.mult, ALU.add)))
            step(lambda lo, _t: (lambda v, lo=lo: v.reduce_max(
                lg_s[:, lo + 40:lo + 41], lg_s[:, lo + 24:lo + 32], axis=AX.X)))
            step(lambda lo, _t: (lambda v, lo=lo: v.tensor_single_scalar(
                lg_s[:, lo + 32:lo + 40], lg_s[:, lo + 24:lo + 32],
                lg_s[:, lo + 40:lo + 41], ALU.is_equal)))
            step(lambda lo, _t: (lambda v, lo=lo: v.tensor_tensor(
                lg_s[:, lo + 16:lo + 24], lg_s[:, lo + 16:lo + 24],
                lg_s[:, lo + 32:lo + 40], ALU.add)))
            for tt, sl, tile in items:
                lo = 44 * sl
                vec(lambda v, lo=lo, _t=tile: v.tensor_tensor(
                    dw_s[:, _t * 16:_t * 16 + 8], lg_s[:, lo + 16:lo + 24],
                    lg_s[:, lo:lo + 8], ALU.mult), [b_lg[sl]], [b_dw])

        def ln_stats_staged(items):
            nckk = D // 512
            for i in range(nckk):
                for src_ap, b_src, slot in items:
                    o = 32 * slot
                    vec(lambda v, _i=i, _s=src_ap, _o=o: v.bn_stats(
                        mv_s[:, _o + _i * 6:_o + (_i + 1) * 6],
                        _s[:, _i * 512:(_i + 1) * 512]), [b_src], [b_mv[slot]])
            for src_ap, b_src, slot in items:
                o = 32 * slot
                vec(lambda v, _o=o: v.bn_aggr(
                    mv_s[:, _o + 24:_o + 26], mv_s[:, _o:_o + 6 * nckk]),
                    [b_mv[slot]], [b_mv[slot]])
            for src_ap, b_src, slot in items:
                o = 32 * slot
                act(lambda s_, _o=o: s_.activation(
                    mv_s[:, _o + 26:_o + 27], mv_s[:, _o + 25:_o + 26],
                    AF.Ln, bias=eps_s[:, 0:1]), [b_mv[slot], b_eps], [b_mv[slot]])
            for src_ap, b_src, slot in items:
                o = 32 * slot
                act(lambda s_, _o=o: s_.activation(
                    mv_s[:, _o + 1:_o + 2], mv_s[:, _o + 26:_o + 27],
                    AF.Exp, scale=-0.5), [b_mv[slot]], [b_mv[slot]])
            for src_ap, b_src, slot in items:
                o = 32 * slot
                vec(lambda v, _o=o: v.scalar_tensor_tensor(
                    mv_s[:, _o + 2:_o + 3], mv_s[:, _o + 24:_o + 25], -1.0,
                    mv_s[:, _o + 1:_o + 2], ALU.mult, ALU.mult),
                    [b_mv[slot]], [b_mv[slot]])

        def ln_apply(dst_ap, src_ap, b_src, b_dst, slot=0):
            o = 32 * slot
            act(lambda s: s.activation(dst_ap, src_ap, AF.Identity,
                                       bias=mv_s[:, o + 2:o + 3],
                                       scale=mv_s[:, o + 1:o + 2]),
                [b_src, b_mv[slot]], [b_dst])

        def transpose_tile(src_ap, b_src, dstT, b_dstT, tt, ntiles):
            dv = dstT[:].rearrange("p (k t) -> p k t", k=16)
            for half in range(2):
                def go(t_, _h=half):
                    inst = None
                    for kk in range(8):
                        k = _h * 8 + kk
                        inst = t_.transpose(psT[:, kk * 128:(kk + 1) * 128],
                                            src_ap[:, k * 128:(k + 1) * 128],
                                            ident_s[:])
                    return inst
                R.emit("tensor", go, reads=[b_src, b_ident], writes=[bpsT],
                       inc=s_pe)
                act(lambda s_, _h=half: s_.activation(
                    dv[:, _h * 8:(_h + 1) * 8, tt * 128:(tt + 1) * 128],
                    psT[:].rearrange("p (k t) -> p k t", k=8), AF.Copy),
                    [bpsT], [b_dstT])

        vec(lambda v: v.memset(eps_s[:], EPS), [], [b_eps])
        vec(lambda v: v.memset(zro_s[:], 0.0), [], [b_zro])
        d_tab = dsem("d_tab")
        d_gw = dsem("d_gw")
        sync_dma(d_tab, ident_s[:], ident_in.ap(), writes=[b_ident])
        sync_dma(d_tab, gb_s[:], gb_in.ap(), writes=[b_gb])
        wt_dma(d_gw, gwT_s[:].rearrange("p (k e) -> p k e", k=16),
               gwT.ap().rearrange("(k p) e -> p k e", p=128), writes=[b_gwT])

        with ExitStack() as PA:
            ea = PA.enter_context
            ddiag_s = ea(nc.sbuf_tensor("ddiag_s", [128, H * 128], BF16))
            qdec_s = ea(nc.sbuf_tensor("qdec_s", [128, H * 128], BF16))
            blkbc_s = ea(nc.sbuf_tensor("blkbc_s", [128, H], BF16))
            kdecbc_s = ea(nc.sbuf_tensor("kdecbc_s", [128, H], BF16))
            M_s = ea(nc.sbuf_tensor("M_s", [128, H * 128], F32))
            Mb_s = ea(nc.sbuf_tensor("Mb_s", [128, H * 128], BF16))
            wkT_s = ea(nc.sbuf_tensor("wkT_s", [128, 16 * 512], BF16))
            wvT_s = ea(nc.sbuf_tensor("wvT_s", [128, 16 * 512], BF16))
            cos_s = ea(nc.sbuf_tensor("cos_s", [128, GRP], F32))
            sin_s = ea(nc.sbuf_tensor("sin_s", [128, GRP], F32))
            x_t = [ea(nc.sbuf_tensor(f"x_t{i}", [128, D], F32)) for i in range(3)]
            xn1_t = [ea(nc.sbuf_tensor(f"xn1_t{i}", [128, D], BF16)) for i in range(4)]
            xn1T_s = ea(nc.sbuf_tensor("xn1T_s", [128, 16 * GRP], BF16))
            wq_buf = [ea(nc.sbuf_tensor(f"wq_buf{i}", [128, 16 * 256], BF16)) for i in range(2)]
            q_fm = ea(nc.sbuf_tensor("q_fm", [128, H * GRP], BF16))
            k_fm = ea(nc.sbuf_tensor("k_fm", [128, HKV * GRP], BF16))
            v_s = ea(nc.sbuf_tensor("v_s", [128, TPG * 512], BF16))
            o_fm = ea(nc.sbuf_tensor("o_fm", [128, H * GRP], BF16))
            st_s = ea(nc.sbuf_tensor("st_s", [128, H * 128], BF16))
            qt_g = ea(nc.sbuf_tensor("qt_g", [128, H * GRP], BF16))
            kt_s = ea(nc.sbuf_tensor("kt_s", [128, H * 128], BF16))
            rtmp = [[ea(nc.sbuf_tensor(f"rtmp{i}_{j}", [128, 512], F32))
                     for j in range(2)] for i in range(2)]
            k_tok_s = ea(nc.sbuf_tensor("k_tok_s", [128, TPG * 512], BF16))

            b_ddiag, b_qdec, b_kdec = Buf("dd"), Buf("qd"), Buf("kd")
            b_blkbc = Buf("bb"); b_kdecbc = Buf("kb")
            b_M, b_Mb = Buf("M"), Buf("Mb")
            b_wk, b_wv = Buf("wk"), Buf("wv")
            b_wq = [Buf(f"wq{i}") for i in range(2)]
            b_cos, b_sin = Buf("cos"), Buf("sin")
            b_xt = [Buf(f"xt{i}") for i in range(3)]
            b_xn1 = [Buf(f"xn1_{i}") for i in range(4)]
            b_xn1T = Buf("xn1T")
            b_q, b_k, b_v, b_o = Buf("q"), Buf("k"), Buf("v"), Buf("o")
            b_st, b_qt, b_kt = Buf("st"), Buf("qt"), Buf("kt")
            b_rt = [[Buf(f"rt{i}_{j}") for j in range(2)] for i in range(2)]
            b_ktok = Buf("ktok")

            d_cos = dsem("d_cos"); d_sin = dsem("d_sin")
            d_wqc = dsem("d_wqc"); d_wst = dsem("d_wst")
            d_x0 = dsem("d_x0"); d_x1 = dsem("d_x1")
            d_wq = dsem("d_wq"); d_so = dsem("d_so")
            wt_dma(d_gw, ddiag_s[0:CHUNK, :], ddiag_in.ap(), writes=[b_ddiag])
            wt_dma(d_gw, qdec_s[0:CHUNK, :], qdec_in.ap(), writes=[b_qdec])
            wt_dma(d_gw, blkbc_s[0:CHUNK, :], blkbc_in.ap(), writes=[b_blkbc])
            wt_dma(d_gw, kdecbc_s[0:CHUNK, :], kdecbc_in.ap(), writes=[b_kdecbc])
            wt_dma(d_gw, wkT_s[:].rearrange("p (k e) -> p k e", k=16),
                   wkT.ap().rearrange("(k p) e -> p k e", p=128), writes=[b_wk])
            wt_dma(d_gw, wvT_s[:].rearrange("p (k e) -> p k e", k=16),
                   wvT.ap().rearrange("(k p) e -> p k e", p=128), writes=[b_wv])
            vec(lambda v: v.memset(M_s[:], 0.0), [], [b_M])
            vec(lambda v: v.memset(Mb_s[:], 0.0), [], [b_Mb])
            def rope_dve(si, ps_ap, b_ps, dst_s, b_dst, col0, qdec_h=None):
                t1, u = rtmp[si][0], rtmp[si][1]
                bt1, bu = b_rt[si][0], b_rt[si][1]
                vec(lambda v: v.tensor_tensor(t1[:], ps_ap, cos_s[:], ALU.mult),
                    [b_ps, b_cos], [bt1])
                vec(lambda v: v.tensor_tensor(
                    u[0:64, :], ps_ap[64:128, :], sin_s[64:128, :], ALU.mult),
                    [b_ps, b_sin], [bu])
                vec(lambda v: v.tensor_tensor(
                    u[64:128, :], ps_ap[0:64, :], sin_s[0:64, :], ALU.mult),
                    [b_ps, b_sin], [bu])
                pool(lambda g: g.tensor_tensor(
                    dst_s[:, col0:col0 + GRP], t1[:], u[:], ALU.add),
                    [bt1, bu], [b_dst])
                if qdec_h is not None:
                    qdv = qdec_s[:, qdec_h * 128:(qdec_h + 1) * 128].rearrange(
                        "p (r i) -> p r i", r=1).broadcast_to([128, TPG, 128])
                    pool(lambda g, _qdv=qdv: g.tensor_tensor(
                        qt_g[:, col0:col0 + GRP].rearrange(
                            "p (r i) -> p r i", r=TPG),
                        dst_s[:, col0:col0 + GRP].rearrange(
                            "p (r i) -> p r i", r=TPG),
                        _qdv, ALU.mult), [b_dst, b_qdec], [b_qt])

            def ln_steps(ga2):
                is_pre2 = ga2 < cfg.n_pre
                g2 = ga2 if is_pre2 else ga2 - cfg.n_pre
                src_x2 = xpre_in if is_pre2 else x_in
                def a1_load(tt):
                    xts, bx = x_t[tt % 3], b_xt[tt % 3]
                    row0 = g2 * GRP + tt * 128
                    act_dma(d_x0 if tt % 2 == 0 else d_x1, xts[:],
                            src_x2.ap()[row0:row0 + 128, :], writes=[bx])
                def a1_apply(tt):
                    xts, bx = x_t[tt % 3], b_xt[tt % 3]
                    ln_apply(xn1_t[tt % 4][:], xts[:], bx, b_xn1[tt % 4],
                             tt % 4)
                def st(tt):
                    ln_stats_staged([(x_t[tt % 3][:], b_xt[tt % 3], tt % 4)])
                early = [
                    lambda: (a1_load(0), a1_load(1), st(0)),
                    lambda: (st(1), a1_apply(0), a1_load(2)),
                    lambda: (st(2), a1_apply(1), a1_load(3)),
                    lambda: (st(3), a1_apply(2)),
                    lambda: (a1_apply(3),),
                ]
                def late():
                    for tt in range(TPG):
                        transpose_tile(xn1_t[tt % 4][:], b_xn1[tt % 4],
                                       xn1T_s, b_xn1T, tt, TPG)
                return early, late

            def lnphase(ga2):
                early, late = ln_steps(ga2)
                for f in early:
                    f()
                late()

            pend_st = []
            for ga in range(cfg.n_pre + cfg.n_main):
                R.flush()
                is_pre = ga < cfg.n_pre
                g = ga if is_pre else ga - cfg.n_pre
                src_x = xpre_in if is_pre else x_in
                cos_in = cosp if is_pre else cosm
                sin_in = sinp if is_pre else sinm

                sync_dma(d_cos, cos_s[:], cos_in.ap()[:, g * GRP:(g + 1) * GRP], writes=[b_cos])
                sync_dma(d_sin, sin_s[:], sin_in.ap()[:, g * GRP:(g + 1) * GRP], writes=[b_sin])
                for _k, st_fn in pend_st:
                    st_fn()
                pend_st = []

                if ga == 0:
                    lnphase(0)

                if ga + 1 < cfg.n_pre + cfg.n_main:
                    ln_early, ln_late = ln_steps(ga + 1)
                else:
                    ln_early, ln_late = [lambda: None] * 5, lambda: None

                for kv in range(HKV):
                    si = kv % 2
                    bk_ = kv % 2
                    pairs = [(wkT_s[:, k * 512 + kv * 128: k * 512 + (kv + 1) * 128],
                              xn1T_s[:, k * GRP:(k + 1) * GRP]) for k in range(16)]
                    mm_group(ps[bk_][:, 0:GRP], bps[bk_], pairs, [b_wk, b_xn1T])
                    rope_dve(si, ps[bk_][:, 0:GRP], bps[bk_], k_fm, b_k, kv * GRP)
                ln_early[0]()
                for tt in range(TPG):
                    bk_ = 2 + tt % 2
                    pairs = [(xn1T_s[:, k * GRP + tt * 128: k * GRP + (tt + 1) * 128],
                              wvT_s[:, k * 512:(k + 1) * 512]) for k in range(16)]
                    mm_group(ps[bk_][:, :], bps[bk_], pairs, [b_wv, b_xn1T])
                    vec(lambda v, _tt=tt, _b=bk_: v.tensor_copy(
                        v_s[:, _tt * 512:(_tt + 1) * 512], ps[_b][:, :]),
                        [bps[bk_]], [b_v])
                ln_early[1]()
                for cpair in range(TPG // 2):
                    def goT2(t_, _cp=cpair):
                        inst = None
                        for j in range(2):
                            c_ = _cp * 2 + j
                            for kv in range(HKV):
                                inst = t_.transpose(
                                    psT[:, j * 512 + kv * 128: j * 512 + (kv + 1) * 128],
                                    k_fm[:, kv * GRP + c_ * 128: kv * GRP + (c_ + 1) * 128],
                                    ident_s[:])
                        return inst
                    R.emit("tensor", goT2, reads=[b_k, b_ident], writes=[bpsT],
                           inc=s_pe)
                    act(lambda s_, _cp=cpair: s_.activation(
                        k_tok_s[:, _cp * 1024:(_cp + 1) * 1024], psT[:, :],
                        AF.Copy), [bpsT], [b_ktok])
                ln_early[2]()
                if is_pre:
                    ln_early[3]()
                    ln_early[4]()
                if not is_pre:
                    for eighth in range(8):
                        wqb, bwq = wq_buf[eighth % 2], b_wq[eighth % 2]
                        sync_dma(d_wq, wqb[:], scr_wq.ap()[eighth],
                                 reads=[b_swq], writes=[bwq])
                        for hh in range(2):
                            m = eighth * 2 + hh
                            si = m % 2
                            bk_ = m % 2
                            pairs = [(wqb[:, k * 256 + hh * 128: k * 256 + (hh + 1) * 128],
                                      xn1T_s[:, k * GRP:(k + 1) * GRP]) for k in range(16)]
                            mm_group(ps[bk_][:, 0:GRP], bps[bk_], pairs, [bwq, b_xn1T])
                            rope_dve(si, ps[bk_][:, 0:GRP], bps[bk_], q_fm, b_q,
                                     m * GRP, qdec_h=m)
                        if eighth == 2:
                            ln_early[3]()
                        elif eighth == 5:
                            ln_early[4]()

                ln_late()

                for c in range(TPG):
                    if not is_pre:
                        for bk in range(4):
                            def goS(t_, _bk=bk, _c=c):
                                inst = None
                                for hh in range(4):
                                    h = _bk * 4 + hh
                                    inst = t_.matmul(
                                        ps[_bk][:, hh * 128:(hh + 1) * 128],
                                        k_fm[:, (h // NKVG) * GRP + _c * 128:
                                             (h // NKVG) * GRP + (_c + 1) * 128],
                                        q_fm[:, h * GRP + _c * 128:
                                             h * GRP + (_c + 1) * 128],
                                        start=True, stop=True)
                                return inst
                            R.emit("tensor", goS, reads=[b_k, b_q],
                                   writes=[bps[bk]], inc=s_pe)
                            vec(lambda v, _bk=bk: v.tensor_tensor(
                                st_s[:, _bk * 512:(_bk + 1) * 512], ps[_bk][:, :],
                                ddiag_s[:, _bk * 512:(_bk + 1) * 512], ALU.mult),
                                [bps[bk], b_ddiag], [b_st])
                        ov = o_fm[:].rearrange("p (h t) -> p h t", h=H)
                        for bk in range(4):
                            def goO(t_, _bk=bk, _c=c):
                                inst = None
                                for hh in range(4):
                                    h = _bk * 4 + hh
                                    t_.matmul(
                                        ps[_bk][:, hh * 128:(hh + 1) * 128],
                                        v_s[:, _c * 512 + (h // NKVG) * 128:
                                            _c * 512 + (h // NKVG + 1) * 128],
                                        st_s[:, h * 128:(h + 1) * 128],
                                        start=True, stop=False)
                                    inst = t_.matmul(
                                        ps[_bk][:, hh * 128:(hh + 1) * 128],
                                        Mb_s[:, h * 128:(h + 1) * 128],
                                        qt_g[:, h * GRP + _c * 128:
                                             h * GRP + (_c + 1) * 128],
                                        start=False, stop=True)
                                return inst
                            R.emit("tensor", goO,
                                   reads=[b_v, b_st, b_Mb, b_qt],
                                   writes=[bps[bk]], inc=s_pe)
                            act(lambda s_, _bk=bk, _c=c: s_.activation(
                                ov[:, _bk * 4:(_bk + 1) * 4, _c * 128:(_c + 1) * 128],
                                ps[_bk][:].rearrange("p (h t) -> p h t", h=4),
                                AF.Copy), [bps[bk]], [b_o])
                    ktr_rep = k_tok_s[:, c * 512:(c + 1) * 512].rearrange(
                        "p (g d) -> p g d", g=HKV).rearrange(
                        "p g (r d) -> p g r d", r=1).broadcast_to(
                        [128, HKV, NKVG, 128])
                    kdr = kdecbc_s[:].rearrange("p (g r) -> p g r", g=HKV)\
                        .rearrange("p g (r u) -> p g r u", u=1)\
                        .broadcast_to([128, HKV, NKVG, 128])
                    pool(lambda g, _kr=ktr_rep, _kd=kdr: g.tensor_tensor(
                        kt_s[:].rearrange("p (g r d) -> p g r d", g=HKV, r=NKVG),
                        _kr, _kd, ALU.mult), [b_ktok, b_kdecbc], [b_kt])
                    bank_sets = [(4, 0), (5, 4), (6, 8), (4, 12)]
                    for bk, h0 in bank_sets:
                        brep = blkbc_s[:, h0:h0 + 4].rearrange(
                            "p (h u) -> p h u", u=1).broadcast_to([128, 4, 128])
                        mv_ = M_s[:, h0 * 128:(h0 + 4) * 128].rearrange(
                            "p (h d) -> p h d", h=4)
                        if (h0 // 4) % 2 == 0:
                            vec(lambda v, _b=brep, _m=mv_: v.tensor_tensor(
                                _m, _m, _b, ALU.mult), [b_M, b_blkbc], [b_M])
                        else:
                            pool(lambda g, _b=brep, _m=mv_: g.tensor_tensor(
                                _m, _m, _b, ALU.mult), [b_M, b_blkbc], [b_M])
                        def goA(t_, _bk=bk, _h0=h0, _c=c):
                            inst = None
                            for hh in range(4):
                                h = _h0 + hh
                                inst = t_.matmul(
                                    ps[_bk][:, hh * 128:(hh + 1) * 128],
                                    kt_s[:, h * 128:(h + 1) * 128],
                                    v_s[:, _c * 512 + (h // NKVG) * 128:
                                        _c * 512 + (h // NKVG + 1) * 128],
                                    start=True, stop=True)
                            return inst
                        R.emit("tensor", goA, reads=[b_kt, b_v],
                               writes=[bps[bk]], inc=s_pe)
                        vec(lambda v, _h0=h0, _bk=bk: v.tensor_tensor(
                            M_s[:, _h0 * 128:(_h0 + 4) * 128],
                            M_s[:, _h0 * 128:(_h0 + 4) * 128],
                            ps[_bk][:, :], ALU.add), [b_M, bps[bk]], [b_M])
                    if (not is_pre) or (ga == cfg.n_pre - 1 and c == TPG - 1):
                        act(lambda s_: s_.activation(Mb_s[:], M_s[:], AF.Copy),
                            [b_M], [b_Mb])

                if not is_pre:
                    sync_dma(d_so, scr_o.ap()[g], o_fm[:], reads=[b_o], writes=[b_so])

                if is_pre:
                    def flush_pend(key):
                        for k_, fn in list(pend_st):
                            if k_ == key:
                                fn()
                                pend_st.remove((k_, fn))
                    for qq in range(8):
                        if qq % cfg.n_pre != ga:
                            continue
                        bi = (qq // 4) % 2
                        wqb, bwq = wq_buf[bi], b_wq[bi]
                        flush_pend(("wq", bi))
                        wt_dma(d_wqc, wqb[:].rearrange("p (k e) -> p k e", k=16),
                               wqT.ap()[:, qq * 256:(qq + 1) * 256]
                               .rearrange("(k p) e -> p k e", p=128), writes=[bwq])
                        pend_st.append((("wq", bi),
                                        lambda _q=qq, _w=wqb, _b=bwq: sync_dma(
                            d_wst, scr_wq.ap()[_q], _w[:], reads=[_b],
                            writes=[b_swq])))
                    for qq in range(4):
                        if qq % cfg.n_pre != ga:
                            continue
                        flush_pend("wo")
                        wt_dma(d_wqc, q_fm[:].rearrange("p (k e) -> p k e", k=16),
                               woT.ap()[:, qq * 512:(qq + 1) * 512]
                               .rearrange("(k p) e -> p k e", p=128), writes=[b_q])
                        pend_st.append(("wo", lambda _q=qq: sync_dma(
                            d_wst, scr_wo.ap()[_q], q_fm[:], reads=[b_q],
                            writes=[b_swo])))

        barrier()

        with ExitStack() as PA2:
            ea = PA2.enter_context
            o_g = [ea(nc.sbuf_tensor(f"o_g{i}", [128, H * GRP], BF16)) for i in range(2)]
            wo_buf = [ea(nc.sbuf_tensor(f"wo_buf{i}", [128, 16 * 512], BF16)) for i in range(2)]
            attn_g = [ea(nc.sbuf_tensor(f"attn_g{i}", [128, TPG * D], F32)) for i in range(2)]
            x2_t = [ea(nc.sbuf_tensor(f"x2_t{i}", [128, D], F32)) for i in range(4)]
            xn2_t = [ea(nc.sbuf_tensor(f"xn2_t{i}", [128, D], BF16)) for i in range(2)]
            xn2T_s = ea(nc.sbuf_tensor("xn2T_s", [128, 16 * GRP], BF16))
            b_og = [Buf("og0"), Buf("og1")]
            b_ag = [[Buf(f"ag{i}_{t}") for t in range(TPG)] for i in range(2)]
            b_wo = [Buf("wo0"), Buf("wo1")]
            b_x2 = [Buf(f"x2t{i}") for i in range(4)]
            b_xn2 = [Buf("xn2a"), Buf("xn2b")]
            b_xn2T_t = [Buf(f"xn2T{t}") for t in range(TPG)]
            d_og = dsem("d_og"); d_wo = dsem("d_wo")
            d_x20 = dsem("d_x20"); d_x21 = dsem("d_x21")
            d_sx2 = dsem("d_sx2"); d_sxn = dsem("d_sxn")

            def a2_outproj(g):
                ag, bag = attn_g[g % 2], b_ag[g % 2]
                og, bog = o_g[g % 2], b_og[g % 2]
                sync_dma(d_og, og[:], scr_o.ap()[g], reads=[b_so], writes=[bog])
                for dp in range(4):
                    wob, bwo = wo_buf[dp % 2], b_wo[dp % 2]
                    sync_dma(d_wo, wob[:], scr_wo.ap()[dp],
                             reads=[b_swo], writes=[bwo])
                    for tt in range(TPG):
                        pairs = [(og[:, h * GRP + tt * 128: h * GRP + (tt + 1) * 128],
                                  wob[:, h * 512:(h + 1) * 512]) for h in range(16)]
                        pp = (dp * TPG + tt) % 2
                        mm_group(ps[pp][:, :], bps[pp], pairs, [bog, bwo])
                        act(lambda s_, _tt=tt, _dp=dp, _pp=pp, _ag=ag: s_.activation(
                            _ag[:, _tt * D + _dp * 512: _tt * D + (_dp + 1) * 512],
                            ps[_pp][:, :], AF.Copy), [bps[pp]], [bag[tt]])

            def a2_tile(g):
                ag, bag = attn_g[g % 2], b_ag[g % 2]
                items = [(ag[:, tt * D:(tt + 1) * D], bag[tt], tt % 4)
                         for tt in range(TPG)]
                for tt in range(TPG):
                    xts, bx = x2_t[tt % 4], b_x2[tt % 4]
                    row0 = g * GRP + tt * 128
                    act_dma(d_x20 if tt % 2 == 0 else d_x21, xts[:],
                            x_in.ap()[row0:row0 + 128, :], writes=[bx])
                ln_stats_staged(items)
                for tt in range(TPG):
                    a_ap = ag[:, tt * D:(tt + 1) * D]
                    ln_apply(a_ap, a_ap, bag[tt], bag[tt], tt % 4)
                    xts, bx = x2_t[tt % 4], b_x2[tt % 4]
                    vec(lambda v, _ap=a_ap, _x=xts: v.tensor_tensor(
                        _x[:], _x[:], _ap, ALU.add), [bx, bag[tt]], [bx])
                    row0 = g * GRP + tt * 128
                    sync_dma(d_sx2, scr_x2.ap()[row0:row0 + 128, :], xts[:],
                             reads=[bx], writes=[b_sx2])
                items2 = [(x2_t[tt % 4][:], b_x2[tt % 4], tt % 4)
                          for tt in range(TPG)]
                ln_stats_staged(items2)
                for tt in range(TPG):
                    xts, bx = x2_t[tt % 4], b_x2[tt % 4]
                    ln_apply(xn2_t[tt % 2][:], xts[:], bx, b_xn2[tt % 2], tt % 4)
                    transpose_tile(xn2_t[tt % 2][:], b_xn2[tt % 2], xn2T_s,
                                   b_xn2T_t[tt], tt, TPG)
                for tt in range(TPG):
                    sl = tt % 2
                    pairs = [(xn2T_s[:, k * GRP + tt * 128: k * GRP + (tt + 1) * 128],
                              gwT_s[:, k * 8:(k + 1) * 8]) for k in range(16)]
                    mm_group(ps[4 + sl][:, 0:8], bps[4 + sl], pairs,
                             [b_xn2T_t[tt], b_gwT])
                    vec(lambda v, lo=44 * sl, _sl=sl: v.tensor_copy(
                        lg_s[:, lo:lo + 8], ps[4 + _sl][:, 0:8]),
                        [bps[4 + sl]], [b_lg[sl]])
                    if sl == 1 or tt == TPG - 1:
                        t0_ = tt - sl
                        gate_chain([(t0_ + j, (t0_ + j) % 2, g * TPG + t0_ + j)
                                    for j in range(sl + 1)])
                sync_dma(d_sxn, scr_xn2T.ap()[g], xn2T_s[:],
                         reads=list(b_xn2T_t), writes=[b_sxn])


            a2_outproj(0)
            for g in range(cfg.n_main):
                R.flush()
                if g + 1 < cfg.n_main:
                    a2_outproj(g + 1)
                a2_tile(g)

        barrier()

        with ExitStack() as PB:
            eb = PB.enter_context
            xn2T_h = eb(nc.sbuf_tensor("xn2T_h", [128, 16 * THF], BF16))
            y_s = eb(nc.sbuf_tensor("y_s", [128, NTH * D], F32))
            h_s_ = [eb(nc.sbuf_tensor(f"h_s{i}", [128, 8 * THF], BF16)) for i in range(2)]
            w1_buf = [eb(nc.sbuf_tensor(f"w1_buf{i}", [128, 16 * 128], BF16)) for i in range(2)]
            w3_buf = [eb(nc.sbuf_tensor(f"w3_buf{i}", [128, 16 * 128], BF16)) for i in range(2)]
            w2_buf = [eb(nc.sbuf_tensor(f"w2_buf{i}", [128, 8 * 512], BF16)) for i in range(2)]
            silu_t = eb(nc.sbuf_tensor("silu_t", [128, 512], F32))
            x2b_t = [eb(nc.sbuf_tensor(f"x2b_t{i}", [128, D], F32)) for i in range(2)]
            b_xnh, b_y = Buf("xnh"), Buf("y")
            b_h_ = [Buf("h0"), Buf("h1")]
            b_w1 = [Buf("w1b0"), Buf("w1b1")]
            b_w3 = [Buf("w3b0"), Buf("w3b1")]
            b_w2 = [Buf("w2b0"), Buf("w2b1")]
            b_silu = Buf("silu")
            b_x2b = [Buf("x2b0"), Buf("x2b1")]
            d_xnh = dsem("d_xnh"); d_w1 = dsem("d_w1"); d_w3 = dsem("d_w3")
            d_w2 = dsem("d_w2"); d_x2b = dsem("d_x2b"); d_x2c = dsem("d_x2c")

            t_pieces = [(i * 512, min(512, THF - i * 512))
                        for i in range((THF + 511) // 512)]

            def load_xnh(hf):
                xv = xn2T_h[:].rearrange("p (k t) -> p k t", k=16)
                t0 = hf * THF
                g0, g1 = t0 // GRP, (t0 + THF - 1) // GRP
                for g in range(g0, g1 + 1):
                    lo = max(t0, g * GRP)
                    hi = min(t0 + THF, (g + 1) * GRP)
                    sync_dma(d_xnh, xv[:, :, lo - t0:hi - t0],
                             scr_xn2T.ap()[g].rearrange("p (k t) -> p k t", k=16)
                             [:, :, lo - g * GRP:hi - g * GRP],
                             reads=[b_sxn], writes=[b_xnh])

            def emit_h(hf, ex, hb, ici_order=None, skips=(), out_cb=None):
                h_s, b_h = h_s_[hb], b_h_[hb]
                R.flush()
                for ici in (ici_order if ici_order is not None else range(8)):
                    if out_cb is not None:
                        out_cb(ici)
                    w1b, bw1 = w1_buf[ici % 2], b_w1[ici % 2]
                    w3b, bw3 = w3_buf[ici % 2], b_w3[ici % 2]
                    if ici not in skips:
                        wt_dma(d_w1, w1b[:].rearrange("p (k i) -> p k i", k=16),
                               w1T.ap()[ex][:, ici * 128:(ici + 1) * 128]
                               .rearrange("(k p) i -> p k i", p=128), writes=[bw1])
                        wt_dma(d_w3, w3b[:].rearrange("p (k i) -> p k i", k=16),
                               w3T.ap()[ex][:, ici * 128:(ici + 1) * 128]
                               .rearrange("(k p) i -> p k i", p=128), writes=[bw3])
                    if True:
                        for tpi, (tp0, tpl) in enumerate(t_pieces):
                            p1 = [(w1b[:, k * 128:(k + 1) * 128],
                                   xn2T_h[:, k * THF + tp0: k * THF + tp0 + tpl])
                                  for k in range(16)]
                            p3 = [(w3b[:, k * 128:(k + 1) * 128],
                                   xn2T_h[:, k * THF + tp0: k * THF + tp0 + tpl])
                                  for k in range(16)]
                            pp = ((ici * len(t_pieces) + tpi) % 2) * 2
                            mm_group(ps[pp][:, 0:tpl], bps[pp], p1, [bw1, b_xnh])
                            mm_group(ps[pp + 1][:, 0:tpl], bps[pp + 1], p3, [bw3, b_xnh])
                            act(lambda s, _pp=pp, _l=tpl: s.activation(
                                silu_t[:, 0:_l], ps[_pp][:, 0:_l], AF.Silu),
                                [bps[pp]], [b_silu])
                            vec(lambda v, _pp=pp, _l=tpl, _i=ici, _t0=tp0:
                                v.tensor_tensor(
                                    h_s[:, _i * THF + _t0: _i * THF + _t0 + _l],
                                    silu_t[:, 0:_l], ps[_pp + 1][:, 0:_l], ALU.mult),
                                [b_silu, bps[pp + 1]], [b_h])

            def emit_y(hf, ex, hb, dq_order=None, skips=(), init=None):
                h_s, b_h = h_s_[hb], b_h_[hb]
                if init is None:
                    init = (ex == 8)
                for dq4 in (dq_order if dq_order is not None else range(4)):
                    w2b, bw2 = w2_buf[dq4 % 2], b_w2[dq4 % 2]
                    if dq4 not in skips:
                        wt_dma(d_w2, w2b[:].rearrange("p (k d) -> p k d", k=8),
                               w2T.ap()[ex][:, dq4 * 512:(dq4 + 1) * 512]
                               .rearrange("(k p) d -> p k d", p=128), writes=[bw2])
                    if True:
                        for tt in range(NTH):
                            dcol = dq4 * 512
                            pairs = [(h_s[:, ic * THF + tt * 128: ic * THF + (tt + 1) * 128],
                                      w2b[:, ic * 512:(ic + 1) * 512])
                                     for ic in range(8)]
                            pp = 4 + (dq4 * NTH + tt) % 2
                            mm_group(ps[pp][:, :], bps[pp], pairs, [b_h, bw2])
                            ysl = y_s[:, tt * D + dcol: tt * D + dcol + 512]
                            tile = hf * NTH + tt
                            if init and ex == 8:
                                act(lambda s, _y=ysl, _pp=pp: s.activation(
                                    _y, ps[_pp][:, :], AF.Copy),
                                    [bps[pp]], [b_y])
                            elif init:
                                act(lambda s, _y=ysl, _pp=pp, _t=tile, _e=ex:
                                    s.activation(
                                        _y, ps[_pp][:, :], AF.Identity,
                                        bias=zro_s[:, 0:1],
                                        scale=dw_s[:, _t * 16 + _e: _t * 16 + _e + 1]),
                                    [bps[pp], b_dw, b_zro], [b_y])
                            elif ex == 8:
                                vec(lambda v, _y=ysl, _pp=pp: v.tensor_tensor(
                                    _y, _y, ps[_pp][:, :], ALU.add),
                                    [bps[pp], b_y], [b_y])
                            else:
                                vec(lambda v, _y=ysl, _pp=pp, _t=tile, _e=ex:
                                    v.scalar_tensor_tensor(
                                        _y, ps[_pp][:, :],
                                        dw_s[:, _t * 16 + _e: _t * 16 + _e + 1],
                                        _y, ALU.mult, ALU.add),
                                    [bps[pp], b_dw, b_y], [b_y])

            def emit_out_stats(hf):
                for tt in range(2):
                    row0 = hf * THF + tt * 128
                    act_dma(d_x2b if tt % 2 == 0 else d_x2c,
                            x2b_t[tt % 2][:], scr_x2.ap()[row0:row0 + 128, :],
                            reads=[b_sx2], writes=[b_x2b[tt % 2]])
                ln_stats_staged([(y_s[:, tt * D:(tt + 1) * D], b_y, tt % 8)
                                 for tt in range(NTH)])

            def emit_out_tile(hf, tt):
                if tt >= NTH:
                    return
                row0 = hf * THF + tt * 128
                x2b, bx2b = x2b_t[tt % 2], b_x2b[tt % 2]
                o = 32 * (tt % 8)
                zbc = zro_s[:, 0:1].broadcast_to([128, D])
                vec(lambda v, _x=x2b, _b=o, _z=zbc: v.scalar_tensor_tensor(
                    _x[:], _x[:], mv_s[:, _b + 2:_b + 3], _z,
                    ALU.add, ALU.add),
                    [bx2b, b_mv[tt % 8], b_zro], [bx2b])
                vec(lambda v, _x=x2b, _t=tt, _b=o: v.scalar_tensor_tensor(
                    _x[:], y_s[:, _t * D:(_t + 1) * D],
                    mv_s[:, _b + 1:_b + 2], _x[:], ALU.mult, ALU.add),
                    [b_y, b_mv[tt % 8], bx2b], [bx2b])
                pw = [(s_out, s_out.v)] if s_out.v else []
                R.emit("sync", lambda s, _r=row0, _x=x2b: s.dma_start(
                    out=out_d.ap()[_r:_r + 128, :], in_=_x[:]),
                    reads=[bx2b], writes=[], inc=s_out, inc_n=16,
                    pre_waits=pw)
                if tt + 2 < NTH:
                    row2 = hf * THF + (tt + 2) * 128
                    act_dma(d_x2b if tt % 2 == 0 else d_x2c,
                            x2b[:], scr_x2.ap()[row2:row2 + 128, :],
                            reads=[b_sx2], writes=[bx2b])

            def emit_out(hf):
                emit_out_stats(hf)
                for tt in range(NTH):
                    emit_out_tile(hf, tt)

            load_xnh(0)
            for ei, ex in enumerate([8] + list(range(8))):
                emit_h(0, ex, ei % 2)
                emit_y(0, ex, ei % 2)
            load_xnh(1)
            emit_h(1, 7, 1, ici_order=[7, 6, 5, 4, 3, 2, 1, 0], skips=(7, 6))
            emit_out_stats(0)
            h_seq = [6, 5, 4, 3, 2, 1, 0, 8]
            emit_h(1, h_seq[0], 0, out_cb=lambda c: emit_out_tile(0, c))
            emit_y(1, 7, 1, dq_order=[3, 2, 1, 0], skips=(3, 2), init=True)
            for i in range(1, 8):
                emit_h(1, h_seq[i], i % 2)
                emit_y(1, h_seq[i - 1], (i - 1) % 2, init=False)
            emit_y(1, 8, 1, init=False)
            emit_out(1)

            R.flush()
            @block.sync
            def _(sy):
                sy.wait_ge(s_out.h, s_out.v)

    return nc


def _stage_inputs(cfg, L0_attn_x, wq, wk, wv, wo, gate_w, gate_b,
                  W1, W2, W3, Ws1, Ws2, Ws3):
    dd, qd, kd, blk, bb, kb = decay_tables()
    cfg.blk_vals = blk
    f32 = np.float32
    shared = {
        "wqT": np.ascontiguousarray(wq.T, f32),
        "wkT": np.ascontiguousarray(wk.T, f32),
        "wvT": np.ascontiguousarray(wv.T, f32),
        "woT": np.ascontiguousarray(wo.T, f32),
        "gwT": np.ascontiguousarray(gate_w.T, f32),
        "gb": np.ascontiguousarray(np.broadcast_to(gate_b, (128, 8)), f32),
        "w1T": np.ascontiguousarray(
            np.concatenate([W1.transpose(0, 2, 1), Ws1.T[None]], 0), f32),
        "w3T": np.ascontiguousarray(
            np.concatenate([W3.transpose(0, 2, 1), Ws3.T[None]], 0), f32),
        "w2T": np.ascontiguousarray(
            np.concatenate([W2.transpose(0, 2, 1), Ws2.T[None]], 0), f32),
        "ddiag": dd, "qdec": qd, "kdec": kd, "blkbc": bb, "kdecbc": kb,
        "ident": np.eye(128, dtype=ml_dtypes.bfloat16),
    }
    cos0, sin0 = rope_tables(0, cfg.t_pre)
    in_maps = []
    for c in range(8):
        b, half = c // 2, c % 2
        m0 = half * cfg.t_main
        cosm, sinm = rope_tables(m0, cfg.t_main)
        im = dict(shared)
        im["x"] = np.ascontiguousarray(L0_attn_x[b, m0:m0 + cfg.t_main, :], f32)
        im["xpre"] = (np.ascontiguousarray(L0_attn_x[b, 0:cfg.t_pre, :], f32)
                      if half == 1 else np.zeros((cfg.t_pre, D), f32))
        im["cosp"], im["sinp"] = cos0, sin0
        im["cosm"], im["sinm"] = cosm, sinm
        in_maps.append(im)
    return in_maps


def kernel(L0_attn_x, L0_attn_wq, L0_attn_wk, L0_attn_wv, L0_attn_wo,
           L0_ffn_gate_w, L0_ffn_gate_b, L0_ffn_W1, L0_ffn_W2, L0_ffn_W3,
           L0_ffn_Ws1, L0_ffn_Ws2, L0_ffn_Ws3):
    cfg = Cfg(4, 4)
    in_maps = _stage_inputs(cfg, np.asarray(L0_attn_x), np.asarray(L0_attn_wq),
                            np.asarray(L0_attn_wk), np.asarray(L0_attn_wv),
                            np.asarray(L0_attn_wo), np.asarray(L0_ffn_gate_w),
                            np.asarray(L0_ffn_gate_b), np.asarray(L0_ffn_W1),
                            np.asarray(L0_ffn_W2), np.asarray(L0_ffn_W3),
                            np.asarray(L0_ffn_Ws1), np.asarray(L0_ffn_Ws2),
                            np.asarray(L0_ffn_Ws3))
    nc = build_program(cfg)
    res = run_bass_kernel_spmd(nc, in_maps, core_ids=list(range(8)))
    out = np.empty((B, S, D), np.float32)
    for c in range(8):
        b, half = c // 2, c % 2
        out[b, half * cfg.t_main:(half + 1) * cfg.t_main, :] = \
            np.asarray(res.results[c]["out"], np.float32)
    return out


# version 34
# speedup vs baseline: 1.0585x; 1.0123x over previous
import numpy as np
import ml_dtypes
from contextlib import ExitStack

import concourse.bass as bass
import concourse.mybir as mybir
from concourse.bass_utils import run_bass_kernel_spmd

F32 = mybir.dt.float32
BF16 = mybir.dt.bfloat16
AF = mybir.ActivationFunctionType
ALU = mybir.AluOpType
AX = mybir.AxisListType

B, S, D = 4, 4096, 2048
H, HKV, DH = 16, 4, 128
NKVG = H // HKV
CHUNK = 128
GRP = 512
TPG = GRP // 128
EPS = 1e-5
NEXP = 9
I_IN = 1024
NEG_BIG = 1.0e30


class Cfg:
    def __init__(self, n_pre_groups=4, n_main_groups=4):
        self.n_pre = n_pre_groups
        self.n_main = n_main_groups
        self.t_pre = n_pre_groups * GRP
        self.t_main = n_main_groups * GRP
        self.t_half = self.t_main // 2
        self.blk_vals = None


def _alibi_lam():
    i = np.arange(1, H + 1, dtype=np.float32)
    return np.exp2((np.float32(-8.0) * i / np.float32(H)).astype(np.float32))


def rope_tables(p0, n):
    f32 = np.float32
    inv = (f32(1.0) / f32(10000.0) ** (np.arange(0, DH, 2, dtype=f32) / f32(DH))
           ).astype(f32)
    pos = np.arange(p0, p0 + n).astype(f32)
    fr = (pos[None, :] * inv[:, None]).astype(f32)
    cos = np.concatenate([np.cos(fr), np.cos(fr)], 0).astype(f32)
    sin = np.concatenate([np.sin(fr), -np.sin(fr)], 0).astype(f32)
    return cos, sin


def decay_tables():
    f32 = np.float32
    lam = _alibi_lam()
    r = np.arange(CHUNK, dtype=f32)
    dd = np.zeros((CHUNK, H * CHUNK), f32)
    qd = np.zeros((CHUNK, H * CHUNK), f32)
    kd = np.zeros((CHUNK, H), f32)
    bb = np.zeros((CHUNK, H), f32)
    kb = np.zeros((CHUNK, H), f32)
    for h in range(H):
        diff = (r[None, :] - r[:, None]).astype(f32)
        dd[:, h * CHUNK:(h + 1) * CHUNK] = np.where(
            diff >= 0, np.exp((-lam[h] * diff).astype(f32)), f32(0.0))
        qd[:, h * CHUNK:(h + 1) * CHUNK] = np.exp(
            (-lam[h] * (r[None, :] + f32(1.0))).astype(f32))
        kd[:, h] = np.exp((-lam[h] * (f32(CHUNK) - f32(1.0) - r)).astype(f32))
        bb[:, h] = np.exp(np.float32(-lam[h] * f32(CHUNK)))
        kb[:, h] = kd[:, h]
    blk = [float(bb[0, h]) for h in range(H)]
    return dd, qd, kd, blk, bb, kb


class Sem:
    def __init__(self, handle, name):
        self.h = handle
        self.name = name
        self.v = 0


class Buf:
    __slots__ = ("name", "last_w", "readers")
    def __init__(self, name):
        self.name = name
        self.last_w = None
        self.readers = []


class Reg:
    def __init__(self, block):
        self.cache = {}
        self.block = block
        self.pend = {n: [] for n in ("sync", "gpsimd", "tensor", "vector", "scalar")}

    def emit(self, eng_name, fn, reads=(), writes=(), inc=None, inc_n=1,
             pre_waits=()):
        need = {}
        def add(sv):
            if sv is not None and need.get(sv[0].name, (None, -1))[1] < sv[1]:
                need[sv[0].name] = sv
        for sv in pre_waits:
            add(sv)
        for b in reads:
            add(b.last_w)
        for b in writes:
            add(b.last_w)
            for r in b.readers:
                add(r)
        cache = self.cache.setdefault(eng_name, {})
        waits = []
        for s, v in need.values():
            if cache.get(s.name, -1) < v:
                waits.append((s, v))
                cache[s.name] = v
        sv = None
        if inc is not None:
            inc.v += inc_n
            sv = (inc, inc.v)
        self.pend[eng_name].append((waits, fn, inc, inc_n if inc else 0))
        for b in writes:
            if sv is not None:
                b.last_w = sv
            b.readers = []
        if sv is not None:
            for b in reads:
                b.readers.append(sv)
        return sv

    def flush(self):
        for name, items in self.pend.items():
            if not items:
                continue
            batch = tuple(items)
            items.clear()
            @getattr(self.block, name)
            def _(en, _batch=batch):
                for waits, fn, inc, n in _batch:
                    for s_, v_ in waits:
                        en.wait_ge(s_.h, v_)
                    inst = fn(en)
                    if inc is not None:
                        inst.then_inc(inc.h, n)


def build_program(cfg: Cfg):
    nc = bass.Bass("TRN2", target_bir_lowering=False, debug=False)
    blk = cfg.blk_vals
    TM, TP, THF = cfg.t_main, cfg.t_pre, cfg.t_half
    NTILE = TM // 128
    NTH = THF // 128

    dt_in = {}
    def din(name, shape, dtype=F32):
        h = nc.dram_tensor(name, list(shape), dtype, kind="ExternalInput")
        dt_in[name] = h
        return h

    x_in = din("x", (TM, D))
    xpre_in = din("xpre", (TP, D))
    wqT = din("wqT", (D, D))
    wkT = din("wkT", (D, HKV * DH))
    wvT = din("wvT", (D, HKV * DH))
    woT = din("woT", (D, D))
    gwT = din("gwT", (D, 8))
    gb_in = din("gb", (128, 8))
    w1T = din("w1T", (NEXP, D, I_IN))
    w3T = din("w3T", (NEXP, D, I_IN))
    w2T = din("w2T", (NEXP, I_IN, D))
    cosp = din("cosp", (128, TP))
    sinp = din("sinp", (128, TP))
    cosm = din("cosm", (128, TM))
    sinm = din("sinm", (128, TM))
    ddiag_in = din("ddiag", (CHUNK, H * CHUNK))
    qdec_in = din("qdec", (CHUNK, H * CHUNK))
    kdec_in = din("kdec", (CHUNK, H))
    blkbc_in = din("blkbc", (CHUNK, H))
    kdecbc_in = din("kdecbc", (CHUNK, H))
    ident_in = din("ident", (128, 128), BF16)

    out_d = nc.dram_tensor("out", [TM, D], F32, kind="ExternalOutput")
    scr_x2 = nc.dram_tensor("scr_x2", [TM, D], F32, kind="Internal")
    scr_xn2T = nc.dram_tensor("scr_xn2T", [cfg.n_main, 128, 16 * GRP], BF16, kind="Internal")
    scr_o = nc.dram_tensor("scr_o", [cfg.n_main, 128, H * GRP], BF16, kind="Internal")
    scr_wq = nc.dram_tensor("scr_wq", [8, 128, 16 * 256], BF16, kind="Internal")
    scr_wo = nc.dram_tensor("scr_wo", [4, 128, 16 * 512], BF16, kind="Internal")
    b_sx2, b_sxn, b_so = Buf("scr_x2"), Buf("scr_xn2T"), Buf("scr_o")
    b_swq, b_swo = Buf("scr_wq"), Buf("scr_wo")

    with ExitStack() as OUT:
        e = OUT.enter_context
        dma_sems = []
        def dsem(name):
            s_ = Sem(e(nc.semaphore(name)), name)
            dma_sems.append(s_)
            return s_
        s_out = dsem("s_out")
        s_pe = Sem(e(nc.semaphore("s_pe")), "s_pe")
        s_dv = Sem(e(nc.semaphore("s_dv")), "s_dv")
        s_ac = Sem(e(nc.semaphore("s_ac")), "s_ac")
        s_po = Sem(e(nc.semaphore("s_po")), "s_po")
        s_bar = Sem(e(nc.semaphore("s_bar")), "s_bar")

        ps = [e(nc.psum_tensor(f"ps{i}", [128, 512], F32)) for i in range(7)]
        psT = e(nc.psum_tensor("psT", [128, 1024], BF16))
        bps = [Buf(f"ps{i}") for i in range(7)]
        bpsT = Buf("psT")

        ident_s = e(nc.sbuf_tensor("ident_s", [128, 128], BF16))
        gwT_s = e(nc.sbuf_tensor("gwT_s", [128, 16 * 8], BF16))
        gb_s = e(nc.sbuf_tensor("gb_s", [128, 8], F32))
        dw_s = e(nc.sbuf_tensor("dw_s", [128, NTILE * 16], F32))
        mv_s = e(nc.sbuf_tensor("mv_s", [128, 256], F32))
        eps_s = e(nc.sbuf_tensor("eps_s", [128, 1], F32))
        zro_s = e(nc.sbuf_tensor("zro_s", [128, 1], F32))
        lg_s = e(nc.sbuf_tensor("lg_s", [128, 88], F32))
        b_ident, b_gwT, b_gb = Buf("ident_s"), Buf("gwT_s"), Buf("gb_s")
        b_dw = Buf("dw_s")
        b_lg = [Buf("lg0"), Buf("lg1")]
        b_mv = [Buf(f"mv{i}") for i in range(8)]
        b_eps = Buf("eps_s"); b_zro = Buf("zro_s")

        block = e(nc.Block())
        R = Reg(block)
        nbar = [0]

        def barrier():
            R.flush()
            nbar[0] += 1
            tgt = 5 * nbar[0]
            @block.sync
            def _(sy):
                for s_ in dma_sems:
                    if s_.v:
                        sy.wait_ge(s_.h, s_.v)
                sy.nop().then_inc(s_bar.h, 1)
                sy.wait_ge(s_bar.h, tgt)
            @block.gpsimd
            def _(g):
                g.wait_ge(s_po.h, s_po.v)
                g.nop().then_inc(s_bar.h, 1)
                g.wait_ge(s_bar.h, tgt)
            for name, es in (("tensor", s_pe), ("vector", s_dv), ("scalar", s_ac)):
                @getattr(block, name)
                def _(en, _es=es):
                    en.wait_ge(_es.h, _es.v)
                    en.nop().then_inc(s_bar.h, 1)
                    en.wait_ge(s_bar.h, tgt)
            for c in R.cache.values():
                for es in (s_pe, s_dv, s_ac, s_po, *dma_sems):
                    c[es.name] = max(c.get(es.name, -1), es.v)
            s_bar.v = tgt
            for c in R.cache.values():
                c[s_bar.name] = tgt

        def sync_dma(sem, dst_ap, src_ap, reads=(), writes=()):
            pw = [(sem, sem.v)] if sem.v else []
            R.emit("sync", lambda s: s.dma_start(out=dst_ap, in_=src_ap),
                   reads=list(reads), writes=list(writes), inc=sem, inc_n=16,
                   pre_waits=pw)

        def act_dma(sem, dst_ap, src_ap, reads=(), writes=()):
            pw = [(sem, sem.v)] if sem.v else []
            R.emit("scalar", lambda sc: sc.dma_start(out=dst_ap, in_=src_ap),
                   reads=list(reads), writes=list(writes), inc=sem, inc_n=16,
                   pre_waits=pw)

        def wt_dma(sem, dst_ap, src_ap, reads=(), writes=()):
            pw = [(sem, sem.v)] if sem.v else []
            R.emit("gpsimd", lambda g_: g_.dma_start(out=dst_ap, in_=src_ap),
                   reads=list(reads), writes=list(writes), inc=sem, inc_n=16,
                   pre_waits=pw)

        def mm_group(psum_ap, b_psum, pairs, reads):
            def go(t_, _pairs=pairs, _ap=psum_ap):
                inst = None
                n = len(_pairs)
                for i, (l, r_) in enumerate(_pairs):
                    inst = t_.matmul(_ap, l, r_, start=(i == 0), stop=(i == n - 1))
                return inst
            R.emit("tensor", go, reads=list(reads), writes=[b_psum], inc=s_pe)

        def vec(fn, reads=(), writes=()):
            R.emit("vector", fn, reads=list(reads), writes=list(writes), inc=s_dv)

        def act(fn, reads=(), writes=()):
            R.emit("scalar", fn, reads=list(reads), writes=list(writes), inc=s_ac)

        def pool(fn, reads=(), writes=()):
            R.emit("gpsimd", fn, reads=list(reads), writes=list(writes), inc=s_po)

        def ln_stats(src_ap, b_src, slot=0):
            o = 32 * slot
            bmv = b_mv[slot]
            nckk = D // 512
            for i in range(nckk):
                vec(lambda v, _i=i: v.bn_stats(
                    mv_s[:, o + _i * 6:o + (_i + 1) * 6],
                    src_ap[:, _i * 512:(_i + 1) * 512]), [b_src], [bmv])
            vec(lambda v: v.bn_aggr(mv_s[:, o + 24:o + 26], mv_s[:, o:o + 6 * nckk]),
                [bmv], [bmv])
            act(lambda s: s.activation(mv_s[:, o + 26:o + 27], mv_s[:, o + 25:o + 26],
                                       AF.Ln, bias=eps_s[:, 0:1]), [bmv, b_eps], [bmv])
            act(lambda s: s.activation(mv_s[:, o + 1:o + 2], mv_s[:, o + 26:o + 27],
                                       AF.Exp, scale=-0.5), [bmv], [bmv])
            vec(lambda v: v.scalar_tensor_tensor(
                mv_s[:, o + 2:o + 3], mv_s[:, o + 24:o + 25], -1.0,
                mv_s[:, o + 1:o + 2], ALU.mult, ALU.mult), [bmv], [bmv])


        def gate_chain(items):
            def step(fn_builder, eng=vec):
                for tt, sl, tile in items:
                    lo = 44 * sl
                    eng(fn_builder(lo, tile), [b_lg[sl]], [b_lg[sl]])
            step(lambda lo, _t: (lambda v, lo=lo: v.reduce_max(
                lg_s[:, lo + 40:lo + 41], lg_s[:, lo:lo + 8], axis=AX.X)))
            step(lambda lo, _t: (lambda v, lo=lo: v.tensor_scalar_mul(
                lg_s[:, lo + 41:lo + 42], lg_s[:, lo + 40:lo + 41], -1.0)))
            step(lambda lo, _t: (lambda s_, lo=lo: s_.activation(
                lg_s[:, lo:lo + 8], lg_s[:, lo:lo + 8], AF.Exp,
                bias=lg_s[:, lo + 41:lo + 42])), eng=act)
            step(lambda lo, _t: (lambda v, lo=lo: v.reduce_sum(
                lg_s[:, lo + 42:lo + 43], lg_s[:, lo:lo + 8], axis=AX.X)))
            step(lambda lo, _t: (lambda v, lo=lo: v.reciprocal(
                lg_s[:, lo + 43:lo + 44], lg_s[:, lo + 42:lo + 43])))
            step(lambda lo, _t: (lambda v, lo=lo: v.tensor_single_scalar(
                lg_s[:, lo:lo + 8], lg_s[:, lo:lo + 8],
                lg_s[:, lo + 43:lo + 44], ALU.mult)))
            for tt, sl, tile in items:
                lo = 44 * sl
                vec(lambda v, lo=lo: v.tensor_tensor(
                    lg_s[:, lo + 8:lo + 16], lg_s[:, lo:lo + 8],
                    gb_s[:, 0:8], ALU.add), [b_lg[sl], b_gb], [b_lg[sl]])
            step(lambda lo, _t: (lambda v, lo=lo: v.reduce_max(
                lg_s[:, lo + 40:lo + 41], lg_s[:, lo + 8:lo + 16], axis=AX.X)))
            step(lambda lo, _t: (lambda v, lo=lo: v.tensor_single_scalar(
                lg_s[:, lo + 16:lo + 24], lg_s[:, lo + 8:lo + 16],
                lg_s[:, lo + 40:lo + 41], ALU.is_equal)))
            step(lambda lo, _t: (lambda v, lo=lo: v.scalar_tensor_tensor(
                lg_s[:, lo + 24:lo + 32], lg_s[:, lo + 16:lo + 24], -NEG_BIG,
                lg_s[:, lo + 8:lo + 16], ALU.mult, ALU.add)))
            step(lambda lo, _t: (lambda v, lo=lo: v.reduce_max(
                lg_s[:, lo + 40:lo + 41], lg_s[:, lo + 24:lo + 32], axis=AX.X)))
            step(lambda lo, _t: (lambda v, lo=lo: v.tensor_single_scalar(
                lg_s[:, lo + 32:lo + 40], lg_s[:, lo + 24:lo + 32],
                lg_s[:, lo + 40:lo + 41], ALU.is_equal)))
            step(lambda lo, _t: (lambda v, lo=lo: v.tensor_tensor(
                lg_s[:, lo + 16:lo + 24], lg_s[:, lo + 16:lo + 24],
                lg_s[:, lo + 32:lo + 40], ALU.add)))
            for tt, sl, tile in items:
                lo = 44 * sl
                vec(lambda v, lo=lo, _t=tile: v.tensor_tensor(
                    dw_s[:, _t * 16:_t * 16 + 8], lg_s[:, lo + 16:lo + 24],
                    lg_s[:, lo:lo + 8], ALU.mult), [b_lg[sl]], [b_dw])

        def ln_stats_staged(items):
            nckk = D // 512
            for i in range(nckk):
                for src_ap, b_src, slot in items:
                    o = 32 * slot
                    vec(lambda v, _i=i, _s=src_ap, _o=o: v.bn_stats(
                        mv_s[:, _o + _i * 6:_o + (_i + 1) * 6],
                        _s[:, _i * 512:(_i + 1) * 512]), [b_src], [b_mv[slot]])
            for src_ap, b_src, slot in items:
                o = 32 * slot
                vec(lambda v, _o=o: v.bn_aggr(
                    mv_s[:, _o + 24:_o + 26], mv_s[:, _o:_o + 6 * nckk]),
                    [b_mv[slot]], [b_mv[slot]])
            for src_ap, b_src, slot in items:
                o = 32 * slot
                act(lambda s_, _o=o: s_.activation(
                    mv_s[:, _o + 26:_o + 27], mv_s[:, _o + 25:_o + 26],
                    AF.Ln, bias=eps_s[:, 0:1]), [b_mv[slot], b_eps], [b_mv[slot]])
            for src_ap, b_src, slot in items:
                o = 32 * slot
                act(lambda s_, _o=o: s_.activation(
                    mv_s[:, _o + 1:_o + 2], mv_s[:, _o + 26:_o + 27],
                    AF.Exp, scale=-0.5), [b_mv[slot]], [b_mv[slot]])
            for src_ap, b_src, slot in items:
                o = 32 * slot
                vec(lambda v, _o=o: v.scalar_tensor_tensor(
                    mv_s[:, _o + 2:_o + 3], mv_s[:, _o + 24:_o + 25], -1.0,
                    mv_s[:, _o + 1:_o + 2], ALU.mult, ALU.mult),
                    [b_mv[slot]], [b_mv[slot]])

        def ln_apply(dst_ap, src_ap, b_src, b_dst, slot=0):
            o = 32 * slot
            act(lambda s: s.activation(dst_ap, src_ap, AF.Identity,
                                       bias=mv_s[:, o + 2:o + 3],
                                       scale=mv_s[:, o + 1:o + 2]),
                [b_src, b_mv[slot]], [b_dst])

        def transpose_tile(src_ap, b_src, dstT, b_dstT, tt, ntiles):
            dv = dstT[:].rearrange("p (k t) -> p k t", k=16)
            for half in range(2):
                def go(t_, _h=half):
                    inst = None
                    for kk in range(8):
                        k = _h * 8 + kk
                        inst = t_.transpose(psT[:, kk * 128:(kk + 1) * 128],
                                            src_ap[:, k * 128:(k + 1) * 128],
                                            ident_s[:])
                    return inst
                R.emit("tensor", go, reads=[b_src, b_ident], writes=[bpsT],
                       inc=s_pe)
                vec(lambda v, _h=half: v.tensor_copy(
                    dv[:, _h * 8:(_h + 1) * 8, tt * 128:(tt + 1) * 128],
                    psT[:].rearrange("p (k t) -> p k t", k=8)),
                    [bpsT], [b_dstT])

        vec(lambda v: v.memset(eps_s[:], EPS), [], [b_eps])
        vec(lambda v: v.memset(zro_s[:], 0.0), [], [b_zro])
        d_tab = dsem("d_tab")
        d_gw = dsem("d_gw")
        sync_dma(d_tab, ident_s[:], ident_in.ap(), writes=[b_ident])
        sync_dma(d_tab, gb_s[:], gb_in.ap(), writes=[b_gb])
        wt_dma(d_gw, gwT_s[:].rearrange("p (k e) -> p k e", k=16),
               gwT.ap().rearrange("(k p) e -> p k e", p=128), writes=[b_gwT])

        with ExitStack() as PA:
            ea = PA.enter_context
            ddiag_s = ea(nc.sbuf_tensor("ddiag_s", [128, H * 128], BF16))
            qdec_s = ea(nc.sbuf_tensor("qdec_s", [128, H * 128], BF16))
            blkbc_s = ea(nc.sbuf_tensor("blkbc_s", [128, H], BF16))
            kdecbc_s = ea(nc.sbuf_tensor("kdecbc_s", [128, H], BF16))
            M_s = ea(nc.sbuf_tensor("M_s", [128, H * 128], F32))
            Mb_s = ea(nc.sbuf_tensor("Mb_s", [128, H * 128], BF16))
            wkT_s = ea(nc.sbuf_tensor("wkT_s", [128, 16 * 512], BF16))
            wvT_s = ea(nc.sbuf_tensor("wvT_s", [128, 16 * 512], BF16))
            cos_s = ea(nc.sbuf_tensor("cos_s", [128, GRP], F32))
            sin_s = ea(nc.sbuf_tensor("sin_s", [128, GRP], F32))
            x_t = [ea(nc.sbuf_tensor(f"x_t{i}", [128, D], F32)) for i in range(3)]
            xn1_t = [ea(nc.sbuf_tensor(f"xn1_t{i}", [128, D], BF16)) for i in range(4)]
            xn1T_s = ea(nc.sbuf_tensor("xn1T_s", [128, 16 * GRP], BF16))
            wq_buf = [ea(nc.sbuf_tensor(f"wq_buf{i}", [128, 16 * 256], BF16)) for i in range(2)]
            q_fm = ea(nc.sbuf_tensor("q_fm", [128, H * GRP], BF16))
            k_fm = ea(nc.sbuf_tensor("k_fm", [128, HKV * GRP], BF16))
            v_s = ea(nc.sbuf_tensor("v_s", [128, TPG * 512], BF16))
            o_fm = ea(nc.sbuf_tensor("o_fm", [128, H * GRP], BF16))
            st_s = ea(nc.sbuf_tensor("st_s", [128, H * 128], BF16))
            qt_g = ea(nc.sbuf_tensor("qt_g", [128, H * GRP], BF16))
            kt_s = ea(nc.sbuf_tensor("kt_s", [128, H * 128], BF16))
            rtmp = [[ea(nc.sbuf_tensor(f"rtmp{i}_{j}", [128, 512], F32))
                     for j in range(2)] for i in range(2)]
            k_tok_s = ea(nc.sbuf_tensor("k_tok_s", [128, TPG * 512], BF16))

            b_ddiag, b_qdec, b_kdec = Buf("dd"), Buf("qd"), Buf("kd")
            b_blkbc = Buf("bb"); b_kdecbc = Buf("kb")
            b_M, b_Mb = Buf("M"), Buf("Mb")
            b_wk, b_wv = Buf("wk"), Buf("wv")
            b_wq = [Buf(f"wq{i}") for i in range(2)]
            b_cos, b_sin = Buf("cos"), Buf("sin")
            b_xt = [Buf(f"xt{i}") for i in range(3)]
            b_xn1 = [Buf(f"xn1_{i}") for i in range(4)]
            b_xn1T = Buf("xn1T")
            b_q, b_k, b_v, b_o = Buf("q"), Buf("k"), Buf("v"), Buf("o")
            b_st, b_qt, b_kt = Buf("st"), Buf("qt"), Buf("kt")
            b_rt = [[Buf(f"rt{i}_{j}") for j in range(2)] for i in range(2)]
            b_ktok = Buf("ktok")

            d_cos = dsem("d_cos"); d_sin = dsem("d_sin")
            d_wqc = dsem("d_wqc"); d_wst = dsem("d_wst")
            d_x0 = dsem("d_x0"); d_x1 = dsem("d_x1")
            d_wq = dsem("d_wq"); d_so = dsem("d_so")
            wt_dma(d_gw, ddiag_s[0:CHUNK, :], ddiag_in.ap(), writes=[b_ddiag])
            wt_dma(d_gw, qdec_s[0:CHUNK, :], qdec_in.ap(), writes=[b_qdec])
            wt_dma(d_gw, blkbc_s[0:CHUNK, :], blkbc_in.ap(), writes=[b_blkbc])
            wt_dma(d_gw, kdecbc_s[0:CHUNK, :], kdecbc_in.ap(), writes=[b_kdecbc])
            wt_dma(d_gw, wkT_s[:].rearrange("p (k e) -> p k e", k=16),
                   wkT.ap().rearrange("(k p) e -> p k e", p=128), writes=[b_wk])
            wt_dma(d_gw, wvT_s[:].rearrange("p (k e) -> p k e", k=16),
                   wvT.ap().rearrange("(k p) e -> p k e", p=128), writes=[b_wv])
            vec(lambda v: v.memset(M_s[:], 0.0), [], [b_M])
            vec(lambda v: v.memset(Mb_s[:], 0.0), [], [b_Mb])
            def rope_dve(si, ps_ap, b_ps, dst_s, b_dst, col0, qdec_h=None):
                t1, u = rtmp[si][0], rtmp[si][1]
                bt1, bu = b_rt[si][0], b_rt[si][1]
                vec(lambda v: v.tensor_tensor(t1[:], ps_ap, cos_s[:], ALU.mult),
                    [b_ps, b_cos], [bt1])
                vec(lambda v: v.tensor_tensor(
                    u[0:64, :], ps_ap[64:128, :], sin_s[64:128, :], ALU.mult),
                    [b_ps, b_sin], [bu])
                vec(lambda v: v.tensor_tensor(
                    u[64:128, :], ps_ap[0:64, :], sin_s[0:64, :], ALU.mult),
                    [b_ps, b_sin], [bu])
                pool(lambda g: g.tensor_tensor(
                    dst_s[:, col0:col0 + GRP], t1[:], u[:], ALU.add),
                    [bt1, bu], [b_dst])
                if qdec_h is not None:
                    qdv = qdec_s[:, qdec_h * 128:(qdec_h + 1) * 128].rearrange(
                        "p (r i) -> p r i", r=1).broadcast_to([128, TPG, 128])
                    pool(lambda g, _qdv=qdv: g.tensor_tensor(
                        qt_g[:, col0:col0 + GRP].rearrange(
                            "p (r i) -> p r i", r=TPG),
                        dst_s[:, col0:col0 + GRP].rearrange(
                            "p (r i) -> p r i", r=TPG),
                        _qdv, ALU.mult), [b_dst, b_qdec], [b_qt])

            def ln_steps(ga2):
                is_pre2 = ga2 < cfg.n_pre
                g2 = ga2 if is_pre2 else ga2 - cfg.n_pre
                src_x2 = xpre_in if is_pre2 else x_in
                def a1_load(tt):
                    xts, bx = x_t[tt % 3], b_xt[tt % 3]
                    row0 = g2 * GRP + tt * 128
                    act_dma(d_x0 if tt % 2 == 0 else d_x1, xts[:],
                            src_x2.ap()[row0:row0 + 128, :], writes=[bx])
                def a1_apply(tt):
                    xts, bx = x_t[tt % 3], b_xt[tt % 3]
                    ln_apply(xn1_t[tt % 4][:], xts[:], bx, b_xn1[tt % 4],
                             tt % 4)
                def st(tt):
                    ln_stats_staged([(x_t[tt % 3][:], b_xt[tt % 3], tt % 4)])
                early = [
                    lambda: (a1_load(0), a1_load(1), st(0)),
                    lambda: (st(1), a1_apply(0), a1_load(2)),
                    lambda: (st(2), a1_apply(1), a1_load(3)),
                    lambda: (st(3), a1_apply(2)),
                    lambda: (a1_apply(3),),
                ]
                def late():
                    for tt in range(TPG):
                        transpose_tile(xn1_t[tt % 4][:], b_xn1[tt % 4],
                                       xn1T_s, b_xn1T, tt, TPG)
                return early, late

            def lnphase(ga2):
                early, late = ln_steps(ga2)
                for f in early:
                    f()
                late()

            pend_st = []
            for ga in range(cfg.n_pre + cfg.n_main):
                R.flush()
                is_pre = ga < cfg.n_pre
                g = ga if is_pre else ga - cfg.n_pre
                src_x = xpre_in if is_pre else x_in
                cos_in = cosp if is_pre else cosm
                sin_in = sinp if is_pre else sinm

                sync_dma(d_cos, cos_s[:], cos_in.ap()[:, g * GRP:(g + 1) * GRP], writes=[b_cos])
                sync_dma(d_sin, sin_s[:], sin_in.ap()[:, g * GRP:(g + 1) * GRP], writes=[b_sin])
                for _k, st_fn in pend_st:
                    st_fn()
                pend_st = []

                if ga == 0:
                    lnphase(0)

                if ga + 1 < cfg.n_pre + cfg.n_main:
                    ln_early, ln_late = ln_steps(ga + 1)
                else:
                    ln_early, ln_late = [lambda: None] * 5, lambda: None

                for kv in range(HKV):
                    si = kv % 2
                    bk_ = kv % 2
                    pairs = [(wkT_s[:, k * 512 + kv * 128: k * 512 + (kv + 1) * 128],
                              xn1T_s[:, k * GRP:(k + 1) * GRP]) for k in range(16)]
                    mm_group(ps[bk_][:, 0:GRP], bps[bk_], pairs, [b_wk, b_xn1T])
                    rope_dve(si, ps[bk_][:, 0:GRP], bps[bk_], k_fm, b_k, kv * GRP)
                ln_early[0]()
                for tt in range(TPG):
                    bk_ = 2 + tt % 2
                    pairs = [(xn1T_s[:, k * GRP + tt * 128: k * GRP + (tt + 1) * 128],
                              wvT_s[:, k * 512:(k + 1) * 512]) for k in range(16)]
                    mm_group(ps[bk_][:, :], bps[bk_], pairs, [b_wv, b_xn1T])
                    vec(lambda v, _tt=tt, _b=bk_: v.tensor_copy(
                        v_s[:, _tt * 512:(_tt + 1) * 512], ps[_b][:, :]),
                        [bps[bk_]], [b_v])
                ln_early[1]()
                for cpair in range(TPG // 2):
                    def goT2(t_, _cp=cpair):
                        inst = None
                        for j in range(2):
                            c_ = _cp * 2 + j
                            for kv in range(HKV):
                                inst = t_.transpose(
                                    psT[:, j * 512 + kv * 128: j * 512 + (kv + 1) * 128],
                                    k_fm[:, kv * GRP + c_ * 128: kv * GRP + (c_ + 1) * 128],
                                    ident_s[:])
                        return inst
                    R.emit("tensor", goT2, reads=[b_k, b_ident], writes=[bpsT],
                           inc=s_pe)
                    act(lambda s_, _cp=cpair: s_.activation(
                        k_tok_s[:, _cp * 1024:(_cp + 1) * 1024], psT[:, :],
                        AF.Copy), [bpsT], [b_ktok])
                ln_early[2]()
                if is_pre:
                    ln_early[3]()
                    ln_early[4]()
                if not is_pre:
                    for eighth in range(8):
                        wqb, bwq = wq_buf[eighth % 2], b_wq[eighth % 2]
                        sync_dma(d_wq, wqb[:], scr_wq.ap()[eighth],
                                 reads=[b_swq], writes=[bwq])
                        for hh in range(2):
                            m = eighth * 2 + hh
                            si = m % 2
                            bk_ = m % 2
                            pairs = [(wqb[:, k * 256 + hh * 128: k * 256 + (hh + 1) * 128],
                                      xn1T_s[:, k * GRP:(k + 1) * GRP]) for k in range(16)]
                            mm_group(ps[bk_][:, 0:GRP], bps[bk_], pairs, [bwq, b_xn1T])
                            rope_dve(si, ps[bk_][:, 0:GRP], bps[bk_], q_fm, b_q,
                                     m * GRP, qdec_h=m)
                        if eighth == 2:
                            ln_early[3]()
                        elif eighth == 5:
                            ln_early[4]()

                ln_late()

                for c in range(TPG):
                    if not is_pre:
                        for bk in range(4):
                            def goS(t_, _bk=bk, _c=c):
                                inst = None
                                for hh in range(4):
                                    h = _bk * 4 + hh
                                    inst = t_.matmul(
                                        ps[_bk][:, hh * 128:(hh + 1) * 128],
                                        k_fm[:, (h // NKVG) * GRP + _c * 128:
                                             (h // NKVG) * GRP + (_c + 1) * 128],
                                        q_fm[:, h * GRP + _c * 128:
                                             h * GRP + (_c + 1) * 128],
                                        start=True, stop=True)
                                return inst
                            R.emit("tensor", goS, reads=[b_k, b_q],
                                   writes=[bps[bk]], inc=s_pe)
                            vec(lambda v, _bk=bk: v.tensor_tensor(
                                st_s[:, _bk * 512:(_bk + 1) * 512], ps[_bk][:, :],
                                ddiag_s[:, _bk * 512:(_bk + 1) * 512], ALU.mult),
                                [bps[bk], b_ddiag], [b_st])
                        ov = o_fm[:].rearrange("p (h t) -> p h t", h=H)
                        for bk in range(4):
                            def goO(t_, _bk=bk, _c=c):
                                inst = None
                                for hh in range(4):
                                    h = _bk * 4 + hh
                                    t_.matmul(
                                        ps[_bk][:, hh * 128:(hh + 1) * 128],
                                        v_s[:, _c * 512 + (h // NKVG) * 128:
                                            _c * 512 + (h // NKVG + 1) * 128],
                                        st_s[:, h * 128:(h + 1) * 128],
                                        start=True, stop=False)
                                    inst = t_.matmul(
                                        ps[_bk][:, hh * 128:(hh + 1) * 128],
                                        Mb_s[:, h * 128:(h + 1) * 128],
                                        qt_g[:, h * GRP + _c * 128:
                                             h * GRP + (_c + 1) * 128],
                                        start=False, stop=True)
                                return inst
                            R.emit("tensor", goO,
                                   reads=[b_v, b_st, b_Mb, b_qt],
                                   writes=[bps[bk]], inc=s_pe)
                            act(lambda s_, _bk=bk, _c=c: s_.activation(
                                ov[:, _bk * 4:(_bk + 1) * 4, _c * 128:(_c + 1) * 128],
                                ps[_bk][:].rearrange("p (h t) -> p h t", h=4),
                                AF.Copy), [bps[bk]], [b_o])
                    ktr_rep = k_tok_s[:, c * 512:(c + 1) * 512].rearrange(
                        "p (g d) -> p g d", g=HKV).rearrange(
                        "p g (r d) -> p g r d", r=1).broadcast_to(
                        [128, HKV, NKVG, 128])
                    kdr = kdecbc_s[:].rearrange("p (g r) -> p g r", g=HKV)\
                        .rearrange("p g (r u) -> p g r u", u=1)\
                        .broadcast_to([128, HKV, NKVG, 128])
                    pool(lambda g, _kr=ktr_rep, _kd=kdr: g.tensor_tensor(
                        kt_s[:].rearrange("p (g r d) -> p g r d", g=HKV, r=NKVG),
                        _kr, _kd, ALU.mult), [b_ktok, b_kdecbc], [b_kt])
                    bank_sets = [(4, 0), (5, 4), (6, 8), (4, 12)]
                    for bk, h0 in bank_sets:
                        brep = blkbc_s[:, h0:h0 + 4].rearrange(
                            "p (h u) -> p h u", u=1).broadcast_to([128, 4, 128])
                        mv_ = M_s[:, h0 * 128:(h0 + 4) * 128].rearrange(
                            "p (h d) -> p h d", h=4)
                        if (h0 // 4) % 2 == 0:
                            vec(lambda v, _b=brep, _m=mv_: v.tensor_tensor(
                                _m, _m, _b, ALU.mult), [b_M, b_blkbc], [b_M])
                        else:
                            pool(lambda g, _b=brep, _m=mv_: g.tensor_tensor(
                                _m, _m, _b, ALU.mult), [b_M, b_blkbc], [b_M])
                        def goA(t_, _bk=bk, _h0=h0, _c=c):
                            inst = None
                            for hh in range(4):
                                h = _h0 + hh
                                inst = t_.matmul(
                                    ps[_bk][:, hh * 128:(hh + 1) * 128],
                                    kt_s[:, h * 128:(h + 1) * 128],
                                    v_s[:, _c * 512 + (h // NKVG) * 128:
                                        _c * 512 + (h // NKVG + 1) * 128],
                                    start=True, stop=True)
                            return inst
                        R.emit("tensor", goA, reads=[b_kt, b_v],
                               writes=[bps[bk]], inc=s_pe)
                        vec(lambda v, _h0=h0, _bk=bk: v.tensor_tensor(
                            M_s[:, _h0 * 128:(_h0 + 4) * 128],
                            M_s[:, _h0 * 128:(_h0 + 4) * 128],
                            ps[_bk][:, :], ALU.add), [b_M, bps[bk]], [b_M])
                    if (not is_pre) or (ga == cfg.n_pre - 1 and c == TPG - 1):
                        act(lambda s_: s_.activation(Mb_s[:], M_s[:], AF.Copy),
                            [b_M], [b_Mb])

                if not is_pre:
                    sync_dma(d_so, scr_o.ap()[g], o_fm[:], reads=[b_o], writes=[b_so])

                if is_pre:
                    def flush_pend(key):
                        for k_, fn in list(pend_st):
                            if k_ == key:
                                fn()
                                pend_st.remove((k_, fn))
                    for qq in range(8):
                        if qq % cfg.n_pre != ga:
                            continue
                        bi = (qq // 4) % 2
                        wqb, bwq = wq_buf[bi], b_wq[bi]
                        flush_pend(("wq", bi))
                        wt_dma(d_wqc, wqb[:].rearrange("p (k e) -> p k e", k=16),
                               wqT.ap()[:, qq * 256:(qq + 1) * 256]
                               .rearrange("(k p) e -> p k e", p=128), writes=[bwq])
                        pend_st.append((("wq", bi),
                                        lambda _q=qq, _w=wqb, _b=bwq: sync_dma(
                            d_wst, scr_wq.ap()[_q], _w[:], reads=[_b],
                            writes=[b_swq])))
                    for qq in range(4):
                        if qq % cfg.n_pre != ga:
                            continue
                        flush_pend("wo")
                        wt_dma(d_wqc, q_fm[:].rearrange("p (k e) -> p k e", k=16),
                               woT.ap()[:, qq * 512:(qq + 1) * 512]
                               .rearrange("(k p) e -> p k e", p=128), writes=[b_q])
                        pend_st.append(("wo", lambda _q=qq: sync_dma(
                            d_wst, scr_wo.ap()[_q], q_fm[:], reads=[b_q],
                            writes=[b_swo])))

        barrier()

        with ExitStack() as PA2:
            ea = PA2.enter_context
            o_g = [ea(nc.sbuf_tensor(f"o_g{i}", [128, H * GRP], BF16)) for i in range(2)]
            wo_buf = [ea(nc.sbuf_tensor(f"wo_buf{i}", [128, 16 * 512], BF16)) for i in range(2)]
            attn_g = [ea(nc.sbuf_tensor(f"attn_g{i}", [128, TPG * D], F32)) for i in range(2)]
            x2_t = [ea(nc.sbuf_tensor(f"x2_t{i}", [128, D], F32)) for i in range(4)]
            xn2_t = [ea(nc.sbuf_tensor(f"xn2_t{i}", [128, D], BF16)) for i in range(2)]
            xn2T_s = ea(nc.sbuf_tensor("xn2T_s", [128, 16 * GRP], BF16))
            b_og = [Buf("og0"), Buf("og1")]
            b_ag = [[Buf(f"ag{i}_{t}") for t in range(TPG)] for i in range(2)]
            b_wo = [Buf("wo0"), Buf("wo1")]
            b_x2 = [Buf(f"x2t{i}") for i in range(4)]
            b_xn2 = [Buf("xn2a"), Buf("xn2b")]
            b_xn2T_t = [Buf(f"xn2T{t}") for t in range(TPG)]
            d_og = dsem("d_og"); d_wo = dsem("d_wo")
            d_x20 = dsem("d_x20"); d_x21 = dsem("d_x21")
            d_sx2 = dsem("d_sx2"); d_sxn = dsem("d_sxn")

            def a2_outproj(g):
                ag, bag = attn_g[g % 2], b_ag[g % 2]
                og, bog = o_g[g % 2], b_og[g % 2]
                sync_dma(d_og, og[:], scr_o.ap()[g], reads=[b_so], writes=[bog])
                for dp in range(4):
                    wob, bwo = wo_buf[dp % 2], b_wo[dp % 2]
                    sync_dma(d_wo, wob[:], scr_wo.ap()[dp],
                             reads=[b_swo], writes=[bwo])
                    for tt in range(TPG):
                        pairs = [(og[:, h * GRP + tt * 128: h * GRP + (tt + 1) * 128],
                                  wob[:, h * 512:(h + 1) * 512]) for h in range(16)]
                        pp = (dp * TPG + tt) % 2
                        mm_group(ps[pp][:, :], bps[pp], pairs, [bog, bwo])
                        act(lambda s_, _tt=tt, _dp=dp, _pp=pp, _ag=ag: s_.activation(
                            _ag[:, _tt * D + _dp * 512: _tt * D + (_dp + 1) * 512],
                            ps[_pp][:, :], AF.Copy), [bps[pp]], [bag[tt]])

            def a2_tile(g):
                ag, bag = attn_g[g % 2], b_ag[g % 2]
                items = [(ag[:, tt * D:(tt + 1) * D], bag[tt], tt % 4)
                         for tt in range(TPG)]
                for tt in range(TPG):
                    xts, bx = x2_t[tt % 4], b_x2[tt % 4]
                    row0 = g * GRP + tt * 128
                    act_dma(d_x20 if tt % 2 == 0 else d_x21, xts[:],
                            x_in.ap()[row0:row0 + 128, :], writes=[bx])
                ln_stats_staged(items)
                for tt in range(TPG):
                    a_ap = ag[:, tt * D:(tt + 1) * D]
                    ln_apply(a_ap, a_ap, bag[tt], bag[tt], tt % 4)
                    xts, bx = x2_t[tt % 4], b_x2[tt % 4]
                    vec(lambda v, _ap=a_ap, _x=xts: v.tensor_tensor(
                        _x[:], _x[:], _ap, ALU.add), [bx, bag[tt]], [bx])
                    row0 = g * GRP + tt * 128
                    sync_dma(d_sx2, scr_x2.ap()[row0:row0 + 128, :], xts[:],
                             reads=[bx], writes=[b_sx2])
                items2 = [(x2_t[tt % 4][:], b_x2[tt % 4], tt % 4)
                          for tt in range(TPG)]
                ln_stats_staged(items2)
                for tt in range(TPG):
                    xts, bx = x2_t[tt % 4], b_x2[tt % 4]
                    ln_apply(xn2_t[tt % 2][:], xts[:], bx, b_xn2[tt % 2], tt % 4)
                    transpose_tile(xn2_t[tt % 2][:], b_xn2[tt % 2], xn2T_s,
                                   b_xn2T_t[tt], tt, TPG)
                for tt in range(TPG):
                    sl = tt % 2
                    pairs = [(xn2T_s[:, k * GRP + tt * 128: k * GRP + (tt + 1) * 128],
                              gwT_s[:, k * 8:(k + 1) * 8]) for k in range(16)]
                    mm_group(ps[4 + sl][:, 0:8], bps[4 + sl], pairs,
                             [b_xn2T_t[tt], b_gwT])
                    vec(lambda v, lo=44 * sl, _sl=sl: v.tensor_copy(
                        lg_s[:, lo:lo + 8], ps[4 + _sl][:, 0:8]),
                        [bps[4 + sl]], [b_lg[sl]])
                    if sl == 1 or tt == TPG - 1:
                        t0_ = tt - sl
                        gate_chain([(t0_ + j, (t0_ + j) % 2, g * TPG + t0_ + j)
                                    for j in range(sl + 1)])
                sync_dma(d_sxn, scr_xn2T.ap()[g], xn2T_s[:],
                         reads=list(b_xn2T_t), writes=[b_sxn])


            a2_outproj(0)
            for g in range(cfg.n_main):
                R.flush()
                if g + 1 < cfg.n_main:
                    a2_outproj(g + 1)
                a2_tile(g)

        barrier()

        with ExitStack() as PB:
            eb = PB.enter_context
            xn2T_h = eb(nc.sbuf_tensor("xn2T_h", [128, 16 * THF], BF16))
            y_s = eb(nc.sbuf_tensor("y_s", [128, NTH * D], F32))
            h_s_ = [eb(nc.sbuf_tensor(f"h_s{i}", [128, 8 * THF], BF16)) for i in range(2)]
            w1_buf = [eb(nc.sbuf_tensor(f"w1_buf{i}", [128, 16 * 128], BF16)) for i in range(2)]
            w3_buf = [eb(nc.sbuf_tensor(f"w3_buf{i}", [128, 16 * 128], BF16)) for i in range(2)]
            w2_buf = [eb(nc.sbuf_tensor(f"w2_buf{i}", [128, 8 * 512], BF16)) for i in range(2)]
            silu_t = eb(nc.sbuf_tensor("silu_t", [128, 512], F32))
            x2b_t = [eb(nc.sbuf_tensor(f"x2b_t{i}", [128, D], F32)) for i in range(2)]
            b_xnh, b_y = Buf("xnh"), Buf("y")
            b_h_ = [Buf("h0"), Buf("h1")]
            b_w1 = [Buf("w1b0"), Buf("w1b1")]
            b_w3 = [Buf("w3b0"), Buf("w3b1")]
            b_w2 = [Buf("w2b0"), Buf("w2b1")]
            b_silu = Buf("silu")
            b_x2b = [Buf("x2b0"), Buf("x2b1")]
            d_xnh = dsem("d_xnh"); d_w1 = dsem("d_w1"); d_w3 = dsem("d_w3")
            d_w2 = dsem("d_w2"); d_x2b = dsem("d_x2b"); d_x2c = dsem("d_x2c")

            t_pieces = [(i * 512, min(512, THF - i * 512))
                        for i in range((THF + 511) // 512)]

            def load_xnh(hf):
                xv = xn2T_h[:].rearrange("p (k t) -> p k t", k=16)
                t0 = hf * THF
                g0, g1 = t0 // GRP, (t0 + THF - 1) // GRP
                for g in range(g0, g1 + 1):
                    lo = max(t0, g * GRP)
                    hi = min(t0 + THF, (g + 1) * GRP)
                    sync_dma(d_xnh, xv[:, :, lo - t0:hi - t0],
                             scr_xn2T.ap()[g].rearrange("p (k t) -> p k t", k=16)
                             [:, :, lo - g * GRP:hi - g * GRP],
                             reads=[b_sxn], writes=[b_xnh])

            def emit_h(hf, ex, hb, ici_order=None, skips=(), out_cb=None):
                h_s, b_h = h_s_[hb], b_h_[hb]
                R.flush()
                for ici in (ici_order if ici_order is not None else range(8)):
                    if out_cb is not None:
                        out_cb(ici)
                    w1b, bw1 = w1_buf[ici % 2], b_w1[ici % 2]
                    w3b, bw3 = w3_buf[ici % 2], b_w3[ici % 2]
                    if ici not in skips:
                        wt_dma(d_w1, w1b[:].rearrange("p (k i) -> p k i", k=16),
                               w1T.ap()[ex][:, ici * 128:(ici + 1) * 128]
                               .rearrange("(k p) i -> p k i", p=128), writes=[bw1])
                        wt_dma(d_w3, w3b[:].rearrange("p (k i) -> p k i", k=16),
                               w3T.ap()[ex][:, ici * 128:(ici + 1) * 128]
                               .rearrange("(k p) i -> p k i", p=128), writes=[bw3])
                    if True:
                        for tpi, (tp0, tpl) in enumerate(t_pieces):
                            p1 = [(w1b[:, k * 128:(k + 1) * 128],
                                   xn2T_h[:, k * THF + tp0: k * THF + tp0 + tpl])
                                  for k in range(16)]
                            p3 = [(w3b[:, k * 128:(k + 1) * 128],
                                   xn2T_h[:, k * THF + tp0: k * THF + tp0 + tpl])
                                  for k in range(16)]
                            pp = ((ici * len(t_pieces) + tpi) % 2) * 2
                            mm_group(ps[pp][:, 0:tpl], bps[pp], p1, [bw1, b_xnh])
                            mm_group(ps[pp + 1][:, 0:tpl], bps[pp + 1], p3, [bw3, b_xnh])
                            act(lambda s, _pp=pp, _l=tpl: s.activation(
                                silu_t[:, 0:_l], ps[_pp][:, 0:_l], AF.Silu),
                                [bps[pp]], [b_silu])
                            vec(lambda v, _pp=pp, _l=tpl, _i=ici, _t0=tp0:
                                v.tensor_tensor(
                                    h_s[:, _i * THF + _t0: _i * THF + _t0 + _l],
                                    silu_t[:, 0:_l], ps[_pp + 1][:, 0:_l], ALU.mult),
                                [b_silu, bps[pp + 1]], [b_h])

            def emit_y(hf, ex, hb, dq_order=None, skips=(), init=None):
                h_s, b_h = h_s_[hb], b_h_[hb]
                if init is None:
                    init = (ex == 8)
                for dq4 in (dq_order if dq_order is not None else range(4)):
                    w2b, bw2 = w2_buf[dq4 % 2], b_w2[dq4 % 2]
                    if dq4 not in skips:
                        wt_dma(d_w2, w2b[:].rearrange("p (k d) -> p k d", k=8),
                               w2T.ap()[ex][:, dq4 * 512:(dq4 + 1) * 512]
                               .rearrange("(k p) d -> p k d", p=128), writes=[bw2])
                    if True:
                        for tt in range(NTH):
                            dcol = dq4 * 512
                            pairs = [(h_s[:, ic * THF + tt * 128: ic * THF + (tt + 1) * 128],
                                      w2b[:, ic * 512:(ic + 1) * 512])
                                     for ic in range(8)]
                            pp = 4 + (dq4 * NTH + tt) % 2
                            mm_group(ps[pp][:, :], bps[pp], pairs, [b_h, bw2])
                            ysl = y_s[:, tt * D + dcol: tt * D + dcol + 512]
                            tile = hf * NTH + tt
                            if init and ex == 8:
                                act(lambda s, _y=ysl, _pp=pp: s.activation(
                                    _y, ps[_pp][:, :], AF.Copy),
                                    [bps[pp]], [b_y])
                            elif init:
                                act(lambda s, _y=ysl, _pp=pp, _t=tile, _e=ex:
                                    s.activation(
                                        _y, ps[_pp][:, :], AF.Identity,
                                        bias=zro_s[:, 0:1],
                                        scale=dw_s[:, _t * 16 + _e: _t * 16 + _e + 1]),
                                    [bps[pp], b_dw, b_zro], [b_y])
                            elif ex == 8:
                                vec(lambda v, _y=ysl, _pp=pp: v.tensor_tensor(
                                    _y, _y, ps[_pp][:, :], ALU.add),
                                    [bps[pp], b_y], [b_y])
                            else:
                                vec(lambda v, _y=ysl, _pp=pp, _t=tile, _e=ex:
                                    v.scalar_tensor_tensor(
                                        _y, ps[_pp][:, :],
                                        dw_s[:, _t * 16 + _e: _t * 16 + _e + 1],
                                        _y, ALU.mult, ALU.add),
                                    [bps[pp], b_dw, b_y], [b_y])

            def emit_out_stats(hf):
                for tt in range(2):
                    row0 = hf * THF + tt * 128
                    act_dma(d_x2b if tt % 2 == 0 else d_x2c,
                            x2b_t[tt % 2][:], scr_x2.ap()[row0:row0 + 128, :],
                            reads=[b_sx2], writes=[b_x2b[tt % 2]])
                ln_stats_staged([(y_s[:, tt * D:(tt + 1) * D], b_y, tt % 8)
                                 for tt in range(NTH)])

            def emit_out_tile(hf, tt):
                if tt >= NTH:
                    return
                row0 = hf * THF + tt * 128
                x2b, bx2b = x2b_t[tt % 2], b_x2b[tt % 2]
                o = 32 * (tt % 8)
                zbc = zro_s[:, 0:1].broadcast_to([128, D])
                vec(lambda v, _x=x2b, _b=o, _z=zbc: v.scalar_tensor_tensor(
                    _x[:], _x[:], mv_s[:, _b + 2:_b + 3], _z,
                    ALU.add, ALU.add),
                    [bx2b, b_mv[tt % 8], b_zro], [bx2b])
                vec(lambda v, _x=x2b, _t=tt, _b=o: v.scalar_tensor_tensor(
                    _x[:], y_s[:, _t * D:(_t + 1) * D],
                    mv_s[:, _b + 1:_b + 2], _x[:], ALU.mult, ALU.add),
                    [b_y, b_mv[tt % 8], bx2b], [bx2b])
                pw = [(s_out, s_out.v)] if s_out.v else []
                R.emit("sync", lambda s, _r=row0, _x=x2b: s.dma_start(
                    out=out_d.ap()[_r:_r + 128, :], in_=_x[:]),
                    reads=[bx2b], writes=[], inc=s_out, inc_n=16,
                    pre_waits=pw)
                if tt + 2 < NTH:
                    row2 = hf * THF + (tt + 2) * 128
                    act_dma(d_x2b if tt % 2 == 0 else d_x2c,
                            x2b[:], scr_x2.ap()[row2:row2 + 128, :],
                            reads=[b_sx2], writes=[bx2b])

            def emit_out(hf):
                emit_out_stats(hf)
                for tt in range(NTH):
                    emit_out_tile(hf, tt)

            load_xnh(0)
            for ei, ex in enumerate([8] + list(range(8))):
                emit_h(0, ex, ei % 2)
                emit_y(0, ex, ei % 2)
            load_xnh(1)
            emit_h(1, 7, 1, ici_order=[7, 6, 5, 4, 3, 2, 1, 0], skips=(7, 6))
            emit_out_stats(0)
            h_seq = [6, 5, 4, 3, 2, 1, 0, 8]
            emit_h(1, h_seq[0], 0, out_cb=lambda c: emit_out_tile(0, c))
            emit_y(1, 7, 1, dq_order=[3, 2, 1, 0], skips=(3, 2), init=True)
            for i in range(1, 8):
                emit_h(1, h_seq[i], i % 2)
                emit_y(1, h_seq[i - 1], (i - 1) % 2, init=False)
            emit_y(1, 8, 1, init=False)
            emit_out(1)

            R.flush()
            @block.sync
            def _(sy):
                sy.wait_ge(s_out.h, s_out.v)

    return nc


def _stage_inputs(cfg, L0_attn_x, wq, wk, wv, wo, gate_w, gate_b,
                  W1, W2, W3, Ws1, Ws2, Ws3):
    dd, qd, kd, blk, bb, kb = decay_tables()
    cfg.blk_vals = blk
    f32 = np.float32
    shared = {
        "wqT": np.ascontiguousarray(wq.T, f32),
        "wkT": np.ascontiguousarray(wk.T, f32),
        "wvT": np.ascontiguousarray(wv.T, f32),
        "woT": np.ascontiguousarray(wo.T, f32),
        "gwT": np.ascontiguousarray(gate_w.T, f32),
        "gb": np.ascontiguousarray(np.broadcast_to(gate_b, (128, 8)), f32),
        "w1T": np.ascontiguousarray(
            np.concatenate([W1.transpose(0, 2, 1), Ws1.T[None]], 0), f32),
        "w3T": np.ascontiguousarray(
            np.concatenate([W3.transpose(0, 2, 1), Ws3.T[None]], 0), f32),
        "w2T": np.ascontiguousarray(
            np.concatenate([W2.transpose(0, 2, 1), Ws2.T[None]], 0), f32),
        "ddiag": dd, "qdec": qd, "kdec": kd, "blkbc": bb, "kdecbc": kb,
        "ident": np.eye(128, dtype=ml_dtypes.bfloat16),
    }
    cos0, sin0 = rope_tables(0, cfg.t_pre)
    in_maps = []
    for c in range(8):
        b, half = c // 2, c % 2
        m0 = half * cfg.t_main
        cosm, sinm = rope_tables(m0, cfg.t_main)
        im = dict(shared)
        im["x"] = np.ascontiguousarray(L0_attn_x[b, m0:m0 + cfg.t_main, :], f32)
        im["xpre"] = (np.ascontiguousarray(L0_attn_x[b, 0:cfg.t_pre, :], f32)
                      if half == 1 else np.zeros((cfg.t_pre, D), f32))
        im["cosp"], im["sinp"] = cos0, sin0
        im["cosm"], im["sinm"] = cosm, sinm
        in_maps.append(im)
    return in_maps


def kernel(L0_attn_x, L0_attn_wq, L0_attn_wk, L0_attn_wv, L0_attn_wo,
           L0_ffn_gate_w, L0_ffn_gate_b, L0_ffn_W1, L0_ffn_W2, L0_ffn_W3,
           L0_ffn_Ws1, L0_ffn_Ws2, L0_ffn_Ws3):
    cfg = Cfg(4, 4)
    in_maps = _stage_inputs(cfg, np.asarray(L0_attn_x), np.asarray(L0_attn_wq),
                            np.asarray(L0_attn_wk), np.asarray(L0_attn_wv),
                            np.asarray(L0_attn_wo), np.asarray(L0_ffn_gate_w),
                            np.asarray(L0_ffn_gate_b), np.asarray(L0_ffn_W1),
                            np.asarray(L0_ffn_W2), np.asarray(L0_ffn_W3),
                            np.asarray(L0_ffn_Ws1), np.asarray(L0_ffn_Ws2),
                            np.asarray(L0_ffn_Ws3))
    nc = build_program(cfg)
    res = run_bass_kernel_spmd(nc, in_maps, core_ids=list(range(8)))
    out = np.empty((B, S, D), np.float32)
    for c in range(8):
        b, half = c // 2, c % 2
        out[b, half * cfg.t_main:(half + 1) * cfg.t_main, :] = \
            np.asarray(res.results[c]["out"], np.float32)
    return out


# version 35
# speedup vs baseline: 1.0600x; 1.0014x over previous
import numpy as np
import ml_dtypes
from contextlib import ExitStack

import concourse.bass as bass
import concourse.mybir as mybir
from concourse.bass_utils import run_bass_kernel_spmd

F32 = mybir.dt.float32
BF16 = mybir.dt.bfloat16
AF = mybir.ActivationFunctionType
ALU = mybir.AluOpType
AX = mybir.AxisListType

B, S, D = 4, 4096, 2048
H, HKV, DH = 16, 4, 128
NKVG = H // HKV
CHUNK = 128
GRP = 512
TPG = GRP // 128
EPS = 1e-5
NEXP = 9
I_IN = 1024
NEG_BIG = 1.0e30


class Cfg:
    def __init__(self, n_pre_groups=4, n_main_groups=4):
        self.n_pre = n_pre_groups
        self.n_main = n_main_groups
        self.t_pre = n_pre_groups * GRP
        self.t_main = n_main_groups * GRP
        self.t_half = self.t_main // 2
        self.blk_vals = None


def _alibi_lam():
    i = np.arange(1, H + 1, dtype=np.float32)
    return np.exp2((np.float32(-8.0) * i / np.float32(H)).astype(np.float32))


def rope_tables(p0, n):
    f32 = np.float32
    inv = (f32(1.0) / f32(10000.0) ** (np.arange(0, DH, 2, dtype=f32) / f32(DH))
           ).astype(f32)
    pos = np.arange(p0, p0 + n).astype(f32)
    fr = (pos[None, :] * inv[:, None]).astype(f32)
    cos = np.concatenate([np.cos(fr), np.cos(fr)], 0).astype(f32)
    sin = np.concatenate([np.sin(fr), -np.sin(fr)], 0).astype(f32)
    return cos, sin


def decay_tables():
    f32 = np.float32
    lam = _alibi_lam()
    r = np.arange(CHUNK, dtype=f32)
    dd = np.zeros((CHUNK, H * CHUNK), f32)
    qd = np.zeros((CHUNK, H * CHUNK), f32)
    kd = np.zeros((CHUNK, H), f32)
    bb = np.zeros((CHUNK, H), f32)
    kb = np.zeros((CHUNK, H), f32)
    for h in range(H):
        diff = (r[None, :] - r[:, None]).astype(f32)
        dd[:, h * CHUNK:(h + 1) * CHUNK] = np.where(
            diff >= 0, np.exp((-lam[h] * diff).astype(f32)), f32(0.0))
        qd[:, h * CHUNK:(h + 1) * CHUNK] = np.exp(
            (-lam[h] * (r[None, :] + f32(1.0))).astype(f32))
        kd[:, h] = np.exp((-lam[h] * (f32(CHUNK) - f32(1.0) - r)).astype(f32))
        bb[:, h] = np.exp(np.float32(-lam[h] * f32(CHUNK)))
        kb[:, h] = kd[:, h]
    blk = [float(bb[0, h]) for h in range(H)]
    return dd, qd, kd, blk, bb, kb


class Sem:
    def __init__(self, handle, name):
        self.h = handle
        self.name = name
        self.v = 0


class Buf:
    __slots__ = ("name", "last_w", "readers")
    def __init__(self, name):
        self.name = name
        self.last_w = None
        self.readers = []


class Reg:
    def __init__(self, block):
        self.cache = {}
        self.block = block
        self.pend = {n: [] for n in ("sync", "gpsimd", "tensor", "vector", "scalar")}

    def emit(self, eng_name, fn, reads=(), writes=(), inc=None, inc_n=1,
             pre_waits=()):
        need = {}
        def add(sv):
            if sv is not None and need.get(sv[0].name, (None, -1))[1] < sv[1]:
                need[sv[0].name] = sv
        for sv in pre_waits:
            add(sv)
        for b in reads:
            add(b.last_w)
        for b in writes:
            add(b.last_w)
            for r in b.readers:
                add(r)
        cache = self.cache.setdefault(eng_name, {})
        waits = []
        for s, v in need.values():
            if cache.get(s.name, -1) < v:
                waits.append((s, v))
                cache[s.name] = v
        sv = None
        if inc is not None:
            inc.v += inc_n
            sv = (inc, inc.v)
        self.pend[eng_name].append((waits, fn, inc, inc_n if inc else 0))
        for b in writes:
            if sv is not None:
                b.last_w = sv
            b.readers = []
        if sv is not None:
            for b in reads:
                b.readers.append(sv)
        return sv

    def flush(self):
        for name, items in self.pend.items():
            if not items:
                continue
            batch = tuple(items)
            items.clear()
            @getattr(self.block, name)
            def _(en, _batch=batch):
                for waits, fn, inc, n in _batch:
                    for s_, v_ in waits:
                        en.wait_ge(s_.h, v_)
                    inst = fn(en)
                    if inc is not None:
                        inst.then_inc(inc.h, n)


def build_program(cfg: Cfg):
    nc = bass.Bass("TRN2", target_bir_lowering=False, debug=False)
    blk = cfg.blk_vals
    TM, TP, THF = cfg.t_main, cfg.t_pre, cfg.t_half
    NTILE = TM // 128
    NTH = THF // 128

    dt_in = {}
    def din(name, shape, dtype=F32):
        h = nc.dram_tensor(name, list(shape), dtype, kind="ExternalInput")
        dt_in[name] = h
        return h

    x_in = din("x", (TM, D))
    xpre_in = din("xpre", (TP, D))
    wqT = din("wqT", (D, D))
    wkT = din("wkT", (D, HKV * DH))
    wvT = din("wvT", (D, HKV * DH))
    woT = din("woT", (D, D))
    gwT = din("gwT", (D, 8))
    gb_in = din("gb", (128, 8))
    w1T = din("w1T", (NEXP, D, I_IN))
    w3T = din("w3T", (NEXP, D, I_IN))
    w2T = din("w2T", (NEXP, I_IN, D))
    cosp = din("cosp", (128, TP))
    sinp = din("sinp", (128, TP))
    cosm = din("cosm", (128, TM))
    sinm = din("sinm", (128, TM))
    ddiag_in = din("ddiag", (CHUNK, H * CHUNK))
    qdec_in = din("qdec", (CHUNK, H * CHUNK))
    kdec_in = din("kdec", (CHUNK, H))
    blkbc_in = din("blkbc", (CHUNK, H))
    kdecbc_in = din("kdecbc", (CHUNK, H))
    ident_in = din("ident", (128, 128), BF16)

    out_d = nc.dram_tensor("out", [TM, D], F32, kind="ExternalOutput")
    scr_x2 = nc.dram_tensor("scr_x2", [TM, D], F32, kind="Internal")
    scr_xn2T = nc.dram_tensor("scr_xn2T", [cfg.n_main, 128, 16 * GRP], BF16, kind="Internal")
    scr_o = nc.dram_tensor("scr_o", [cfg.n_main, 128, H * GRP], BF16, kind="Internal")
    scr_wq = nc.dram_tensor("scr_wq", [8, 128, 16 * 256], BF16, kind="Internal")
    scr_wo = nc.dram_tensor("scr_wo", [4, 128, 16 * 512], BF16, kind="Internal")
    b_sx2, b_sxn, b_so = Buf("scr_x2"), Buf("scr_xn2T"), Buf("scr_o")
    b_swq, b_swo = Buf("scr_wq"), Buf("scr_wo")

    with ExitStack() as OUT:
        e = OUT.enter_context
        dma_sems = []
        def dsem(name):
            s_ = Sem(e(nc.semaphore(name)), name)
            dma_sems.append(s_)
            return s_
        s_out = dsem("s_out")
        s_pe = Sem(e(nc.semaphore("s_pe")), "s_pe")
        s_dv = Sem(e(nc.semaphore("s_dv")), "s_dv")
        s_ac = Sem(e(nc.semaphore("s_ac")), "s_ac")
        s_po = Sem(e(nc.semaphore("s_po")), "s_po")
        s_bar = Sem(e(nc.semaphore("s_bar")), "s_bar")

        ps = [e(nc.psum_tensor(f"ps{i}", [128, 512], F32)) for i in range(7)]
        psT = e(nc.psum_tensor("psT", [128, 1024], BF16))
        bps = [Buf(f"ps{i}") for i in range(7)]
        bpsT = Buf("psT")

        ident_s = e(nc.sbuf_tensor("ident_s", [128, 128], BF16))
        gwT_s = e(nc.sbuf_tensor("gwT_s", [128, 16 * 8], BF16))
        gb_s = e(nc.sbuf_tensor("gb_s", [128, 8], F32))
        dw_s = e(nc.sbuf_tensor("dw_s", [128, NTILE * 16], F32))
        mv_s = e(nc.sbuf_tensor("mv_s", [128, 256], F32))
        eps_s = e(nc.sbuf_tensor("eps_s", [128, 1], F32))
        zro_s = e(nc.sbuf_tensor("zro_s", [128, 1], F32))
        lg_s = e(nc.sbuf_tensor("lg_s", [128, 88], F32))
        b_ident, b_gwT, b_gb = Buf("ident_s"), Buf("gwT_s"), Buf("gb_s")
        b_dw = Buf("dw_s")
        b_lg = [Buf("lg0"), Buf("lg1")]
        b_mv = [Buf(f"mv{i}") for i in range(8)]
        b_eps = Buf("eps_s"); b_zro = Buf("zro_s")

        block = e(nc.Block())
        R = Reg(block)
        nbar = [0]

        def barrier():
            R.flush()
            nbar[0] += 1
            tgt = 5 * nbar[0]
            @block.sync
            def _(sy):
                for s_ in dma_sems:
                    if s_.v:
                        sy.wait_ge(s_.h, s_.v)
                sy.nop().then_inc(s_bar.h, 1)
                sy.wait_ge(s_bar.h, tgt)
            @block.gpsimd
            def _(g):
                g.wait_ge(s_po.h, s_po.v)
                g.nop().then_inc(s_bar.h, 1)
                g.wait_ge(s_bar.h, tgt)
            for name, es in (("tensor", s_pe), ("vector", s_dv), ("scalar", s_ac)):
                @getattr(block, name)
                def _(en, _es=es):
                    en.wait_ge(_es.h, _es.v)
                    en.nop().then_inc(s_bar.h, 1)
                    en.wait_ge(s_bar.h, tgt)
            for c in R.cache.values():
                for es in (s_pe, s_dv, s_ac, s_po, *dma_sems):
                    c[es.name] = max(c.get(es.name, -1), es.v)
            s_bar.v = tgt
            for c in R.cache.values():
                c[s_bar.name] = tgt

        def sync_dma(sem, dst_ap, src_ap, reads=(), writes=()):
            pw = [(sem, sem.v)] if sem.v else []
            R.emit("sync", lambda s: s.dma_start(out=dst_ap, in_=src_ap),
                   reads=list(reads), writes=list(writes), inc=sem, inc_n=16,
                   pre_waits=pw)

        def act_dma(sem, dst_ap, src_ap, reads=(), writes=()):
            pw = [(sem, sem.v)] if sem.v else []
            R.emit("scalar", lambda sc: sc.dma_start(out=dst_ap, in_=src_ap),
                   reads=list(reads), writes=list(writes), inc=sem, inc_n=16,
                   pre_waits=pw)

        def wt_dma(sem, dst_ap, src_ap, reads=(), writes=()):
            pw = [(sem, sem.v)] if sem.v else []
            R.emit("gpsimd", lambda g_: g_.dma_start(out=dst_ap, in_=src_ap),
                   reads=list(reads), writes=list(writes), inc=sem, inc_n=16,
                   pre_waits=pw)

        def mm_group(psum_ap, b_psum, pairs, reads):
            def go(t_, _pairs=pairs, _ap=psum_ap):
                inst = None
                n = len(_pairs)
                for i, (l, r_) in enumerate(_pairs):
                    inst = t_.matmul(_ap, l, r_, start=(i == 0), stop=(i == n - 1))
                return inst
            R.emit("tensor", go, reads=list(reads), writes=[b_psum], inc=s_pe)

        def vec(fn, reads=(), writes=()):
            R.emit("vector", fn, reads=list(reads), writes=list(writes), inc=s_dv)

        def act(fn, reads=(), writes=()):
            R.emit("scalar", fn, reads=list(reads), writes=list(writes), inc=s_ac)

        def pool(fn, reads=(), writes=()):
            R.emit("gpsimd", fn, reads=list(reads), writes=list(writes), inc=s_po)

        def ln_stats(src_ap, b_src, slot=0):
            o = 32 * slot
            bmv = b_mv[slot]
            nckk = D // 512
            for i in range(nckk):
                vec(lambda v, _i=i: v.bn_stats(
                    mv_s[:, o + _i * 6:o + (_i + 1) * 6],
                    src_ap[:, _i * 512:(_i + 1) * 512]), [b_src], [bmv])
            vec(lambda v: v.bn_aggr(mv_s[:, o + 24:o + 26], mv_s[:, o:o + 6 * nckk]),
                [bmv], [bmv])
            act(lambda s: s.activation(mv_s[:, o + 26:o + 27], mv_s[:, o + 25:o + 26],
                                       AF.Ln, bias=eps_s[:, 0:1]), [bmv, b_eps], [bmv])
            act(lambda s: s.activation(mv_s[:, o + 1:o + 2], mv_s[:, o + 26:o + 27],
                                       AF.Exp, scale=-0.5), [bmv], [bmv])
            vec(lambda v: v.scalar_tensor_tensor(
                mv_s[:, o + 2:o + 3], mv_s[:, o + 24:o + 25], -1.0,
                mv_s[:, o + 1:o + 2], ALU.mult, ALU.mult), [bmv], [bmv])


        def gate_chain(items):
            def step(fn_builder, eng=vec):
                for tt, sl, tile in items:
                    lo = 44 * sl
                    eng(fn_builder(lo, tile), [b_lg[sl]], [b_lg[sl]])
            step(lambda lo, _t: (lambda v, lo=lo: v.reduce_max(
                lg_s[:, lo + 40:lo + 41], lg_s[:, lo:lo + 8], axis=AX.X)))
            step(lambda lo, _t: (lambda v, lo=lo: v.tensor_scalar_mul(
                lg_s[:, lo + 41:lo + 42], lg_s[:, lo + 40:lo + 41], -1.0)))
            step(lambda lo, _t: (lambda s_, lo=lo: s_.activation(
                lg_s[:, lo:lo + 8], lg_s[:, lo:lo + 8], AF.Exp,
                bias=lg_s[:, lo + 41:lo + 42])), eng=act)
            step(lambda lo, _t: (lambda v, lo=lo: v.reduce_sum(
                lg_s[:, lo + 42:lo + 43], lg_s[:, lo:lo + 8], axis=AX.X)))
            step(lambda lo, _t: (lambda v, lo=lo: v.reciprocal(
                lg_s[:, lo + 43:lo + 44], lg_s[:, lo + 42:lo + 43])))
            step(lambda lo, _t: (lambda v, lo=lo: v.tensor_single_scalar(
                lg_s[:, lo:lo + 8], lg_s[:, lo:lo + 8],
                lg_s[:, lo + 43:lo + 44], ALU.mult)))
            for tt, sl, tile in items:
                lo = 44 * sl
                vec(lambda v, lo=lo: v.tensor_tensor(
                    lg_s[:, lo + 8:lo + 16], lg_s[:, lo:lo + 8],
                    gb_s[:, 0:8], ALU.add), [b_lg[sl], b_gb], [b_lg[sl]])
            step(lambda lo, _t: (lambda v, lo=lo: v.reduce_max(
                lg_s[:, lo + 40:lo + 41], lg_s[:, lo + 8:lo + 16], axis=AX.X)))
            step(lambda lo, _t: (lambda v, lo=lo: v.tensor_single_scalar(
                lg_s[:, lo + 16:lo + 24], lg_s[:, lo + 8:lo + 16],
                lg_s[:, lo + 40:lo + 41], ALU.is_equal)))
            step(lambda lo, _t: (lambda v, lo=lo: v.scalar_tensor_tensor(
                lg_s[:, lo + 24:lo + 32], lg_s[:, lo + 16:lo + 24], -NEG_BIG,
                lg_s[:, lo + 8:lo + 16], ALU.mult, ALU.add)))
            step(lambda lo, _t: (lambda v, lo=lo: v.reduce_max(
                lg_s[:, lo + 40:lo + 41], lg_s[:, lo + 24:lo + 32], axis=AX.X)))
            step(lambda lo, _t: (lambda v, lo=lo: v.tensor_single_scalar(
                lg_s[:, lo + 32:lo + 40], lg_s[:, lo + 24:lo + 32],
                lg_s[:, lo + 40:lo + 41], ALU.is_equal)))
            step(lambda lo, _t: (lambda v, lo=lo: v.tensor_tensor(
                lg_s[:, lo + 16:lo + 24], lg_s[:, lo + 16:lo + 24],
                lg_s[:, lo + 32:lo + 40], ALU.add)))
            for tt, sl, tile in items:
                lo = 44 * sl
                vec(lambda v, lo=lo, _t=tile: v.tensor_tensor(
                    dw_s[:, _t * 16:_t * 16 + 8], lg_s[:, lo + 16:lo + 24],
                    lg_s[:, lo:lo + 8], ALU.mult), [b_lg[sl]], [b_dw])

        def ln_stats_staged(items):
            nckk = D // 512
            for i in range(nckk):
                for src_ap, b_src, slot in items:
                    o = 32 * slot
                    vec(lambda v, _i=i, _s=src_ap, _o=o: v.bn_stats(
                        mv_s[:, _o + _i * 6:_o + (_i + 1) * 6],
                        _s[:, _i * 512:(_i + 1) * 512]), [b_src], [b_mv[slot]])
            for src_ap, b_src, slot in items:
                o = 32 * slot
                vec(lambda v, _o=o: v.bn_aggr(
                    mv_s[:, _o + 24:_o + 26], mv_s[:, _o:_o + 6 * nckk]),
                    [b_mv[slot]], [b_mv[slot]])
            for src_ap, b_src, slot in items:
                o = 32 * slot
                act(lambda s_, _o=o: s_.activation(
                    mv_s[:, _o + 26:_o + 27], mv_s[:, _o + 25:_o + 26],
                    AF.Ln, bias=eps_s[:, 0:1]), [b_mv[slot], b_eps], [b_mv[slot]])
            for src_ap, b_src, slot in items:
                o = 32 * slot
                act(lambda s_, _o=o: s_.activation(
                    mv_s[:, _o + 1:_o + 2], mv_s[:, _o + 26:_o + 27],
                    AF.Exp, scale=-0.5), [b_mv[slot]], [b_mv[slot]])
            for src_ap, b_src, slot in items:
                o = 32 * slot
                vec(lambda v, _o=o: v.scalar_tensor_tensor(
                    mv_s[:, _o + 2:_o + 3], mv_s[:, _o + 24:_o + 25], -1.0,
                    mv_s[:, _o + 1:_o + 2], ALU.mult, ALU.mult),
                    [b_mv[slot]], [b_mv[slot]])

        def ln_apply(dst_ap, src_ap, b_src, b_dst, slot=0):
            o = 32 * slot
            act(lambda s: s.activation(dst_ap, src_ap, AF.Identity,
                                       bias=mv_s[:, o + 2:o + 3],
                                       scale=mv_s[:, o + 1:o + 2]),
                [b_src, b_mv[slot]], [b_dst])

        def transpose_tile(src_ap, b_src, dstT, b_dstT, tt, ntiles):
            dv = dstT[:].rearrange("p (k t) -> p k t", k=16)
            for half in range(2):
                def go(t_, _h=half):
                    inst = None
                    for kk in range(8):
                        k = _h * 8 + kk
                        inst = t_.transpose(psT[:, kk * 128:(kk + 1) * 128],
                                            src_ap[:, k * 128:(k + 1) * 128],
                                            ident_s[:])
                    return inst
                R.emit("tensor", go, reads=[b_src, b_ident], writes=[bpsT],
                       inc=s_pe)
                vec(lambda v, _h=half: v.tensor_copy(
                    dv[:, _h * 8:(_h + 1) * 8, tt * 128:(tt + 1) * 128],
                    psT[:].rearrange("p (k t) -> p k t", k=8)),
                    [bpsT], [b_dstT])

        vec(lambda v: v.memset(eps_s[:], EPS), [], [b_eps])
        vec(lambda v: v.memset(zro_s[:], 0.0), [], [b_zro])
        d_tab = dsem("d_tab")
        d_gw = dsem("d_gw")
        sync_dma(d_tab, ident_s[:], ident_in.ap(), writes=[b_ident])
        sync_dma(d_tab, gb_s[:], gb_in.ap(), writes=[b_gb])
        wt_dma(d_gw, gwT_s[:].rearrange("p (k e) -> p k e", k=16),
               gwT.ap().rearrange("(k p) e -> p k e", p=128), writes=[b_gwT])

        with ExitStack() as PA:
            ea = PA.enter_context
            ddiag_s = ea(nc.sbuf_tensor("ddiag_s", [128, H * 128], BF16))
            qdec_s = ea(nc.sbuf_tensor("qdec_s", [128, H * 128], BF16))
            blkbc_s = ea(nc.sbuf_tensor("blkbc_s", [128, H], BF16))
            kdecbc_s = ea(nc.sbuf_tensor("kdecbc_s", [128, H], BF16))
            M_s = ea(nc.sbuf_tensor("M_s", [128, H * 128], F32))
            Mb_s = ea(nc.sbuf_tensor("Mb_s", [128, H * 128], BF16))
            wkT_s = ea(nc.sbuf_tensor("wkT_s", [128, 16 * 512], BF16))
            wvT_s = ea(nc.sbuf_tensor("wvT_s", [128, 16 * 512], BF16))
            cos_s = ea(nc.sbuf_tensor("cos_s", [128, GRP], F32))
            sin_s = ea(nc.sbuf_tensor("sin_s", [128, GRP], F32))
            x_t = [ea(nc.sbuf_tensor(f"x_t{i}", [128, D], F32)) for i in range(3)]
            xn1_t = [ea(nc.sbuf_tensor(f"xn1_t{i}", [128, D], BF16)) for i in range(4)]
            xn1T_s = ea(nc.sbuf_tensor("xn1T_s", [128, 16 * GRP], BF16))
            wq_buf = [ea(nc.sbuf_tensor(f"wq_buf{i}", [128, 16 * 256], BF16)) for i in range(2)]
            q_fm = ea(nc.sbuf_tensor("q_fm", [128, H * GRP], BF16))
            k_fm = ea(nc.sbuf_tensor("k_fm", [128, HKV * GRP], BF16))
            v_s = ea(nc.sbuf_tensor("v_s", [128, TPG * 512], BF16))
            o_fm = ea(nc.sbuf_tensor("o_fm", [128, H * GRP], BF16))
            st_s = ea(nc.sbuf_tensor("st_s", [128, H * 128], BF16))
            qt_g = ea(nc.sbuf_tensor("qt_g", [128, H * GRP], BF16))
            kt_s = ea(nc.sbuf_tensor("kt_s", [128, H * 128], BF16))
            rtmp = [[ea(nc.sbuf_tensor(f"rtmp{i}_{j}", [128, 512], F32))
                     for j in range(2)] for i in range(2)]
            k_tok_s = ea(nc.sbuf_tensor("k_tok_s", [128, TPG * 512], BF16))

            b_ddiag, b_qdec, b_kdec = Buf("dd"), Buf("qd"), Buf("kd")
            b_blkbc = Buf("bb"); b_kdecbc = Buf("kb")
            b_M, b_Mb = Buf("M"), Buf("Mb")
            b_wk, b_wv = Buf("wk"), Buf("wv")
            b_wq = [Buf("wq0"), Buf("wq1")]
            b_cos, b_sin = Buf("cos"), Buf("sin")
            b_xt = [Buf(f"xt{i}") for i in range(3)]
            b_xn1 = [Buf(f"xn1_{i}") for i in range(4)]
            b_xn1T = Buf("xn1T")
            b_q, b_k, b_v, b_o = Buf("q"), Buf("k"), Buf("v"), Buf("o")
            b_st, b_qt, b_kt = Buf("st"), Buf("qt"), Buf("kt")
            b_rt = [[Buf(f"rt{i}_{j}") for j in range(2)] for i in range(2)]
            b_ktok = Buf("ktok")

            d_cos = dsem("d_cos"); d_sin = dsem("d_sin")
            d_wqc = dsem("d_wqc"); d_wst = dsem("d_wst")
            d_x0 = dsem("d_x0"); d_x1 = dsem("d_x1")
            d_wq = dsem("d_wq"); d_so = dsem("d_so")
            wt_dma(d_gw, ddiag_s[0:CHUNK, :], ddiag_in.ap(), writes=[b_ddiag])
            wt_dma(d_gw, qdec_s[0:CHUNK, :], qdec_in.ap(), writes=[b_qdec])
            wt_dma(d_gw, blkbc_s[0:CHUNK, :], blkbc_in.ap(), writes=[b_blkbc])
            wt_dma(d_gw, kdecbc_s[0:CHUNK, :], kdecbc_in.ap(), writes=[b_kdecbc])
            wt_dma(d_gw, wkT_s[:].rearrange("p (k e) -> p k e", k=16),
                   wkT.ap().rearrange("(k p) e -> p k e", p=128), writes=[b_wk])
            wt_dma(d_gw, wvT_s[:].rearrange("p (k e) -> p k e", k=16),
                   wvT.ap().rearrange("(k p) e -> p k e", p=128), writes=[b_wv])
            vec(lambda v: v.memset(M_s[:], 0.0), [], [b_M])
            vec(lambda v: v.memset(Mb_s[:], 0.0), [], [b_Mb])
            def rope_dve(si, ps_ap, b_ps, dst_s, b_dst, col0, qdec_h=None):
                t1, u = rtmp[si][0], rtmp[si][1]
                bt1, bu = b_rt[si][0], b_rt[si][1]
                vec(lambda v: v.tensor_tensor(t1[:], ps_ap, cos_s[:], ALU.mult),
                    [b_ps, b_cos], [bt1])
                vec(lambda v: v.tensor_tensor(
                    u[0:64, :], ps_ap[64:128, :], sin_s[64:128, :], ALU.mult),
                    [b_ps, b_sin], [bu])
                vec(lambda v: v.tensor_tensor(
                    u[64:128, :], ps_ap[0:64, :], sin_s[0:64, :], ALU.mult),
                    [b_ps, b_sin], [bu])
                pool(lambda g: g.tensor_tensor(
                    dst_s[:, col0:col0 + GRP], t1[:], u[:], ALU.add),
                    [bt1, bu], [b_dst])
                if qdec_h is not None:
                    qdv = qdec_s[:, qdec_h * 128:(qdec_h + 1) * 128].rearrange(
                        "p (r i) -> p r i", r=1).broadcast_to([128, TPG, 128])
                    pool(lambda g, _qdv=qdv: g.tensor_tensor(
                        qt_g[:, col0:col0 + GRP].rearrange(
                            "p (r i) -> p r i", r=TPG),
                        dst_s[:, col0:col0 + GRP].rearrange(
                            "p (r i) -> p r i", r=TPG),
                        _qdv, ALU.mult), [b_dst, b_qdec], [b_qt])

            def ln_steps(ga2):
                is_pre2 = ga2 < cfg.n_pre
                g2 = ga2 if is_pre2 else ga2 - cfg.n_pre
                src_x2 = xpre_in if is_pre2 else x_in
                def a1_load(tt):
                    xts, bx = x_t[tt % 3], b_xt[tt % 3]
                    row0 = g2 * GRP + tt * 128
                    act_dma(d_x0 if tt % 2 == 0 else d_x1, xts[:],
                            src_x2.ap()[row0:row0 + 128, :], writes=[bx])
                def a1_apply(tt):
                    xts, bx = x_t[tt % 3], b_xt[tt % 3]
                    ln_apply(xn1_t[tt % 4][:], xts[:], bx, b_xn1[tt % 4],
                             tt % 4)
                def st(tt):
                    ln_stats_staged([(x_t[tt % 3][:], b_xt[tt % 3], tt % 4)])
                early = [
                    lambda: (a1_load(0), a1_load(1), st(0)),
                    lambda: (st(1), a1_apply(0), a1_load(2)),
                    lambda: (st(2), a1_apply(1), a1_load(3)),
                    lambda: (st(3), a1_apply(2)),
                    lambda: (a1_apply(3),),
                ]
                def late():
                    for tt in range(TPG):
                        transpose_tile(xn1_t[tt % 4][:], b_xn1[tt % 4],
                                       xn1T_s, b_xn1T, tt, TPG)
                return early, late

            def lnphase(ga2):
                early, late = ln_steps(ga2)
                for f in early:
                    f()
                late()

            pend_st = []
            for ga in range(cfg.n_pre + cfg.n_main):
                R.flush()
                is_pre = ga < cfg.n_pre
                g = ga if is_pre else ga - cfg.n_pre
                src_x = xpre_in if is_pre else x_in
                cos_in = cosp if is_pre else cosm
                sin_in = sinp if is_pre else sinm

                sync_dma(d_cos, cos_s[:], cos_in.ap()[:, g * GRP:(g + 1) * GRP], writes=[b_cos])
                sync_dma(d_sin, sin_s[:], sin_in.ap()[:, g * GRP:(g + 1) * GRP], writes=[b_sin])
                for _k, st_fn in pend_st:
                    st_fn()
                pend_st = []

                if ga == 0:
                    lnphase(0)

                if ga + 1 < cfg.n_pre + cfg.n_main:
                    ln_early, ln_late = ln_steps(ga + 1)
                else:
                    ln_early, ln_late = [lambda: None] * 5, lambda: None

                for kv in range(HKV):
                    si = kv % 2
                    bk_ = kv % 2
                    pairs = [(wkT_s[:, k * 512 + kv * 128: k * 512 + (kv + 1) * 128],
                              xn1T_s[:, k * GRP:(k + 1) * GRP]) for k in range(16)]
                    mm_group(ps[bk_][:, 0:GRP], bps[bk_], pairs, [b_wk, b_xn1T])
                    rope_dve(si, ps[bk_][:, 0:GRP], bps[bk_], k_fm, b_k, kv * GRP)
                ln_early[0]()
                for tt in range(TPG):
                    bk_ = 2 + tt % 2
                    pairs = [(xn1T_s[:, k * GRP + tt * 128: k * GRP + (tt + 1) * 128],
                              wvT_s[:, k * 512:(k + 1) * 512]) for k in range(16)]
                    mm_group(ps[bk_][:, :], bps[bk_], pairs, [b_wv, b_xn1T])
                    vec(lambda v, _tt=tt, _b=bk_: v.tensor_copy(
                        v_s[:, _tt * 512:(_tt + 1) * 512], ps[_b][:, :]),
                        [bps[bk_]], [b_v])
                ln_early[1]()
                for cpair in range(TPG // 2):
                    def goT2(t_, _cp=cpair):
                        inst = None
                        for j in range(2):
                            c_ = _cp * 2 + j
                            for kv in range(HKV):
                                inst = t_.transpose(
                                    psT[:, j * 512 + kv * 128: j * 512 + (kv + 1) * 128],
                                    k_fm[:, kv * GRP + c_ * 128: kv * GRP + (c_ + 1) * 128],
                                    ident_s[:])
                        return inst
                    R.emit("tensor", goT2, reads=[b_k, b_ident], writes=[bpsT],
                           inc=s_pe)
                    act(lambda s_, _cp=cpair: s_.activation(
                        k_tok_s[:, _cp * 1024:(_cp + 1) * 1024], psT[:, :],
                        AF.Copy), [bpsT], [b_ktok])
                ln_early[2]()
                if is_pre:
                    ln_early[3]()
                    ln_early[4]()
                if not is_pre:
                    for eighth in range(8):
                        wqb, bwq = wq_buf[eighth % 2], b_wq[eighth % 2]
                        sync_dma(d_wq, wqb[:], scr_wq.ap()[eighth],
                                 reads=[b_swq], writes=[bwq])
                        for hh in range(2):
                            m = eighth * 2 + hh
                            si = m % 2
                            bk_ = m % 2
                            pairs = [(wqb[:, k * 256 + hh * 128: k * 256 + (hh + 1) * 128],
                                      xn1T_s[:, k * GRP:(k + 1) * GRP]) for k in range(16)]
                            mm_group(ps[bk_][:, 0:GRP], bps[bk_], pairs, [bwq, b_xn1T])
                            rope_dve(si, ps[bk_][:, 0:GRP], bps[bk_], q_fm, b_q,
                                     m * GRP, qdec_h=m)
                        if eighth == 2:
                            ln_early[3]()
                        elif eighth == 5:
                            ln_early[4]()

                ln_late()

                for c in range(TPG):
                    if not is_pre:
                        for bk in range(4):
                            def goS(t_, _bk=bk, _c=c):
                                inst = None
                                for hh in range(4):
                                    h = _bk * 4 + hh
                                    inst = t_.matmul(
                                        ps[_bk][:, hh * 128:(hh + 1) * 128],
                                        k_fm[:, (h // NKVG) * GRP + _c * 128:
                                             (h // NKVG) * GRP + (_c + 1) * 128],
                                        q_fm[:, h * GRP + _c * 128:
                                             h * GRP + (_c + 1) * 128],
                                        start=True, stop=True)
                                return inst
                            R.emit("tensor", goS, reads=[b_k, b_q],
                                   writes=[bps[bk]], inc=s_pe)
                            vec(lambda v, _bk=bk: v.tensor_tensor(
                                st_s[:, _bk * 512:(_bk + 1) * 512], ps[_bk][:, :],
                                ddiag_s[:, _bk * 512:(_bk + 1) * 512], ALU.mult),
                                [bps[bk], b_ddiag], [b_st])
                    ktr_rep = k_tok_s[:, c * 512:(c + 1) * 512].rearrange(
                        "p (g d) -> p g d", g=HKV).rearrange(
                        "p g (r d) -> p g r d", r=1).broadcast_to(
                        [128, HKV, NKVG, 128])
                    kdr = kdecbc_s[:].rearrange("p (g r) -> p g r", g=HKV)\
                        .rearrange("p g (r u) -> p g r u", u=1)\
                        .broadcast_to([128, HKV, NKVG, 128])
                    pool(lambda g, _kr=ktr_rep, _kd=kdr: g.tensor_tensor(
                        kt_s[:].rearrange("p (g r d) -> p g r d", g=HKV, r=NKVG),
                        _kr, _kd, ALU.mult), [b_ktok, b_kdecbc], [b_kt])
                    bank_sets = [(4, 0), (5, 4), (6, 8), (4, 12)]
                    for bk, h0 in bank_sets:
                        brep = blkbc_s[:, h0:h0 + 4].rearrange(
                            "p (h u) -> p h u", u=1).broadcast_to([128, 4, 128])
                        mv_ = M_s[:, h0 * 128:(h0 + 4) * 128].rearrange(
                            "p (h d) -> p h d", h=4)
                        if (h0 // 4) % 2 == 0:
                            vec(lambda v, _b=brep, _m=mv_: v.tensor_tensor(
                                _m, _m, _b, ALU.mult), [b_M, b_blkbc], [b_M])
                        else:
                            pool(lambda g, _b=brep, _m=mv_: g.tensor_tensor(
                                _m, _m, _b, ALU.mult), [b_M, b_blkbc], [b_M])
                        def goA(t_, _bk=bk, _h0=h0, _c=c):
                            inst = None
                            for hh in range(4):
                                h = _h0 + hh
                                inst = t_.matmul(
                                    ps[_bk][:, hh * 128:(hh + 1) * 128],
                                    kt_s[:, h * 128:(h + 1) * 128],
                                    v_s[:, _c * 512 + (h // NKVG) * 128:
                                        _c * 512 + (h // NKVG + 1) * 128],
                                    start=True, stop=True)
                            return inst
                        R.emit("tensor", goA, reads=[b_kt, b_v],
                               writes=[bps[bk]], inc=s_pe)
                        vec(lambda v, _h0=h0, _bk=bk: v.tensor_tensor(
                            M_s[:, _h0 * 128:(_h0 + 4) * 128],
                            M_s[:, _h0 * 128:(_h0 + 4) * 128],
                            ps[_bk][:, :], ALU.add), [b_M, bps[bk]], [b_M])
                    if not is_pre:
                        ov = o_fm[:].rearrange("p (h t) -> p h t", h=H)
                        for bk in range(4):
                            def goO(t_, _bk=bk, _c=c):
                                inst = None
                                for hh in range(4):
                                    h = _bk * 4 + hh
                                    t_.matmul(
                                        ps[_bk][:, hh * 128:(hh + 1) * 128],
                                        v_s[:, _c * 512 + (h // NKVG) * 128:
                                            _c * 512 + (h // NKVG + 1) * 128],
                                        st_s[:, h * 128:(h + 1) * 128],
                                        start=True, stop=False)
                                    inst = t_.matmul(
                                        ps[_bk][:, hh * 128:(hh + 1) * 128],
                                        Mb_s[:, h * 128:(h + 1) * 128],
                                        qt_g[:, h * GRP + _c * 128:
                                             h * GRP + (_c + 1) * 128],
                                        start=False, stop=True)
                                return inst
                            R.emit("tensor", goO,
                                   reads=[b_v, b_st, b_Mb, b_qt],
                                   writes=[bps[bk]], inc=s_pe)
                            act(lambda s_, _bk=bk, _c=c: s_.activation(
                                ov[:, _bk * 4:(_bk + 1) * 4, _c * 128:(_c + 1) * 128],
                                ps[_bk][:].rearrange("p (h t) -> p h t", h=4),
                                AF.Copy), [bps[bk]], [b_o])
                    if (not is_pre) or (ga == cfg.n_pre - 1 and c == TPG - 1):
                        act(lambda s_: s_.activation(Mb_s[:], M_s[:], AF.Copy),
                            [b_M], [b_Mb])

                if not is_pre:
                    sync_dma(d_so, scr_o.ap()[g], o_fm[:], reads=[b_o], writes=[b_so])

                if is_pre:
                    def flush_pend(key):
                        for k_, fn in list(pend_st):
                            if k_ == key:
                                fn()
                                pend_st.remove((k_, fn))
                    for qq in range(8):
                        if qq % cfg.n_pre != ga:
                            continue
                        bi = (qq // 4) % 2
                        wqb, bwq = wq_buf[bi], b_wq[bi]
                        flush_pend(("wq", bi))
                        wt_dma(d_wqc, wqb[:].rearrange("p (k e) -> p k e", k=16),
                               wqT.ap()[:, qq * 256:(qq + 1) * 256]
                               .rearrange("(k p) e -> p k e", p=128), writes=[bwq])
                        pend_st.append((("wq", bi),
                                        lambda _q=qq, _w=wqb, _b=bwq: sync_dma(
                            d_wst, scr_wq.ap()[_q], _w[:], reads=[_b],
                            writes=[b_swq])))
                    for qq in range(4):
                        if qq % cfg.n_pre != ga:
                            continue
                        flush_pend("wo")
                        wt_dma(d_wqc, q_fm[:].rearrange("p (k e) -> p k e", k=16),
                               woT.ap()[:, qq * 512:(qq + 1) * 512]
                               .rearrange("(k p) e -> p k e", p=128), writes=[b_q])
                        pend_st.append(("wo", lambda _q=qq: sync_dma(
                            d_wst, scr_wo.ap()[_q], q_fm[:], reads=[b_q],
                            writes=[b_swo])))

        barrier()

        with ExitStack() as PA2:
            ea = PA2.enter_context
            o_g = ea(nc.sbuf_tensor("o_g", [128, H * GRP], BF16))
            wo_buf = [ea(nc.sbuf_tensor(f"wo_buf{i}", [128, 16 * 512], BF16)) for i in range(2)]
            attn_g = [ea(nc.sbuf_tensor(f"attn_g{i}", [128, TPG * D], F32)) for i in range(3)]
            x2_t = [ea(nc.sbuf_tensor(f"x2_t{i}", [128, D], F32)) for i in range(4)]
            xn2_t = [ea(nc.sbuf_tensor(f"xn2_t{i}", [128, D], BF16)) for i in range(2)]
            xn2T_s = ea(nc.sbuf_tensor("xn2T_s", [128, 16 * GRP], BF16))
            b_og = Buf("og")
            b_ag = [[Buf(f"ag{i}_{t}") for t in range(TPG)] for i in range(3)]
            b_wo = [Buf("wo0"), Buf("wo1")]
            b_x2 = [Buf(f"x2t{i}") for i in range(4)]
            b_xn2 = [Buf("xn2a"), Buf("xn2b")]
            b_xn2T_t = [Buf(f"xn2T{t}") for t in range(TPG)]
            d_og = dsem("d_og"); d_wo = dsem("d_wo")
            d_x20 = dsem("d_x20"); d_x21 = dsem("d_x21")
            d_sx2 = dsem("d_sx2"); d_sxn = dsem("d_sxn")

            def a2_outproj(g):
                ag, bag = attn_g[g % 3], b_ag[g % 3]
                sync_dma(d_og, o_g[:], scr_o.ap()[g], reads=[b_so], writes=[b_og])
                for dp in range(4):
                    wob, bwo = wo_buf[dp % 2], b_wo[dp % 2]
                    sync_dma(d_wo, wob[:], scr_wo.ap()[dp],
                             reads=[b_swo], writes=[bwo])
                    for tt in range(TPG):
                        pairs = [(o_g[:, h * GRP + tt * 128: h * GRP + (tt + 1) * 128],
                                  wob[:, h * 512:(h + 1) * 512]) for h in range(16)]
                        pp = (dp * TPG + tt) % 2
                        mm_group(ps[pp][:, :], bps[pp], pairs, [b_og, bwo])
                        act(lambda s_, _tt=tt, _dp=dp, _pp=pp, _ag=ag: s_.activation(
                            _ag[:, _tt * D + _dp * 512: _tt * D + (_dp + 1) * 512],
                            ps[_pp][:, :], AF.Copy), [bps[pp]], [bag[tt]])

            def a2_tile(g):
                ag, bag = attn_g[g % 3], b_ag[g % 3]
                items = [(ag[:, tt * D:(tt + 1) * D], bag[tt], tt % 4)
                         for tt in range(TPG)]
                for tt in range(TPG):
                    xts, bx = x2_t[tt % 4], b_x2[tt % 4]
                    row0 = g * GRP + tt * 128
                    act_dma(d_x20 if tt % 2 == 0 else d_x21, xts[:],
                            x_in.ap()[row0:row0 + 128, :], writes=[bx])
                ln_stats_staged(items)
                for tt in range(TPG):
                    a_ap = ag[:, tt * D:(tt + 1) * D]
                    ln_apply(a_ap, a_ap, bag[tt], bag[tt], tt % 4)
                    xts, bx = x2_t[tt % 4], b_x2[tt % 4]
                    vec(lambda v, _ap=a_ap, _x=xts: v.tensor_tensor(
                        _x[:], _x[:], _ap, ALU.add), [bx, bag[tt]], [bx])
                    row0 = g * GRP + tt * 128
                    sync_dma(d_sx2, scr_x2.ap()[row0:row0 + 128, :], xts[:],
                             reads=[bx], writes=[b_sx2])
                items2 = [(x2_t[tt % 4][:], b_x2[tt % 4], tt % 4)
                          for tt in range(TPG)]
                ln_stats_staged(items2)
                for tt in range(TPG):
                    xts, bx = x2_t[tt % 4], b_x2[tt % 4]
                    ln_apply(xn2_t[tt % 2][:], xts[:], bx, b_xn2[tt % 2], tt % 4)
                    transpose_tile(xn2_t[tt % 2][:], b_xn2[tt % 2], xn2T_s,
                                   b_xn2T_t[tt], tt, TPG)
                for tt in range(TPG):
                    sl = tt % 2
                    pairs = [(xn2T_s[:, k * GRP + tt * 128: k * GRP + (tt + 1) * 128],
                              gwT_s[:, k * 8:(k + 1) * 8]) for k in range(16)]
                    mm_group(ps[4 + sl][:, 0:8], bps[4 + sl], pairs,
                             [b_xn2T_t[tt], b_gwT])
                    vec(lambda v, lo=44 * sl, _sl=sl: v.tensor_copy(
                        lg_s[:, lo:lo + 8], ps[4 + _sl][:, 0:8]),
                        [bps[4 + sl]], [b_lg[sl]])
                    if sl == 1 or tt == TPG - 1:
                        t0_ = tt - sl
                        gate_chain([(t0_ + j, (t0_ + j) % 2, g * TPG + t0_ + j)
                                    for j in range(sl + 1)])
                sync_dma(d_sxn, scr_xn2T.ap()[g], xn2T_s[:],
                         reads=list(b_xn2T_t), writes=[b_sxn])


            a2_outproj(0)
            for g in range(cfg.n_main):
                R.flush()
                if g + 1 < cfg.n_main:
                    a2_outproj(g + 1)
                a2_tile(g)

        barrier()

        with ExitStack() as PB:
            eb = PB.enter_context
            xn2T_h = eb(nc.sbuf_tensor("xn2T_h", [128, 16 * THF], BF16))
            y_s = eb(nc.sbuf_tensor("y_s", [128, NTH * D], F32))
            h_s_ = [eb(nc.sbuf_tensor(f"h_s{i}", [128, 8 * THF], BF16)) for i in range(2)]
            w1_buf = [eb(nc.sbuf_tensor(f"w1_buf{i}", [128, 16 * 128], BF16)) for i in range(2)]
            w3_buf = [eb(nc.sbuf_tensor(f"w3_buf{i}", [128, 16 * 128], BF16)) for i in range(2)]
            w2_buf = [eb(nc.sbuf_tensor(f"w2_buf{i}", [128, 8 * 512], BF16)) for i in range(2)]
            silu_t = eb(nc.sbuf_tensor("silu_t", [128, 512], F32))
            x2b_t = [eb(nc.sbuf_tensor(f"x2b_t{i}", [128, D], F32)) for i in range(2)]
            b_xnh, b_y = Buf("xnh"), Buf("y")
            b_h_ = [Buf("h0"), Buf("h1")]
            b_w1 = [Buf("w1b0"), Buf("w1b1")]
            b_w3 = [Buf("w3b0"), Buf("w3b1")]
            b_w2 = [Buf("w2b0"), Buf("w2b1")]
            b_silu = Buf("silu")
            b_x2b = [Buf("x2b0"), Buf("x2b1")]
            d_xnh = dsem("d_xnh"); d_w1 = dsem("d_w1"); d_w3 = dsem("d_w3")
            d_w2 = dsem("d_w2"); d_x2b = dsem("d_x2b"); d_x2c = dsem("d_x2c")

            t_pieces = [(i * 512, min(512, THF - i * 512))
                        for i in range((THF + 511) // 512)]

            def load_xnh(hf):
                xv = xn2T_h[:].rearrange("p (k t) -> p k t", k=16)
                t0 = hf * THF
                g0, g1 = t0 // GRP, (t0 + THF - 1) // GRP
                for g in range(g0, g1 + 1):
                    lo = max(t0, g * GRP)
                    hi = min(t0 + THF, (g + 1) * GRP)
                    sync_dma(d_xnh, xv[:, :, lo - t0:hi - t0],
                             scr_xn2T.ap()[g].rearrange("p (k t) -> p k t", k=16)
                             [:, :, lo - g * GRP:hi - g * GRP],
                             reads=[b_sxn], writes=[b_xnh])

            def emit_h(hf, ex, hb, ici_order=None, skips=(), out_cb=None):
                h_s, b_h = h_s_[hb], b_h_[hb]
                R.flush()
                for ici in (ici_order if ici_order is not None else range(8)):
                    if out_cb is not None:
                        out_cb(ici)
                    w1b, bw1 = w1_buf[ici % 2], b_w1[ici % 2]
                    w3b, bw3 = w3_buf[ici % 2], b_w3[ici % 2]
                    if ici not in skips:
                        wt_dma(d_w1, w1b[:].rearrange("p (k i) -> p k i", k=16),
                               w1T.ap()[ex][:, ici * 128:(ici + 1) * 128]
                               .rearrange("(k p) i -> p k i", p=128), writes=[bw1])
                        wt_dma(d_w3, w3b[:].rearrange("p (k i) -> p k i", k=16),
                               w3T.ap()[ex][:, ici * 128:(ici + 1) * 128]
                               .rearrange("(k p) i -> p k i", p=128), writes=[bw3])
                    if True:
                        for tpi, (tp0, tpl) in enumerate(t_pieces):
                            p1 = [(w1b[:, k * 128:(k + 1) * 128],
                                   xn2T_h[:, k * THF + tp0: k * THF + tp0 + tpl])
                                  for k in range(16)]
                            p3 = [(w3b[:, k * 128:(k + 1) * 128],
                                   xn2T_h[:, k * THF + tp0: k * THF + tp0 + tpl])
                                  for k in range(16)]
                            pp = ((ici * len(t_pieces) + tpi) % 2) * 2
                            mm_group(ps[pp][:, 0:tpl], bps[pp], p1, [bw1, b_xnh])
                            mm_group(ps[pp + 1][:, 0:tpl], bps[pp + 1], p3, [bw3, b_xnh])
                            act(lambda s, _pp=pp, _l=tpl: s.activation(
                                silu_t[:, 0:_l], ps[_pp][:, 0:_l], AF.Silu),
                                [bps[pp]], [b_silu])
                            vec(lambda v, _pp=pp, _l=tpl, _i=ici, _t0=tp0:
                                v.tensor_tensor(
                                    h_s[:, _i * THF + _t0: _i * THF + _t0 + _l],
                                    silu_t[:, 0:_l], ps[_pp + 1][:, 0:_l], ALU.mult),
                                [b_silu, bps[pp + 1]], [b_h])

            def emit_y(hf, ex, hb, dq_order=None, skips=(), init=None):
                h_s, b_h = h_s_[hb], b_h_[hb]
                if init is None:
                    init = (ex == 8)
                for dq4 in (dq_order if dq_order is not None else range(4)):
                    w2b, bw2 = w2_buf[dq4 % 2], b_w2[dq4 % 2]
                    if dq4 not in skips:
                        wt_dma(d_w2, w2b[:].rearrange("p (k d) -> p k d", k=8),
                               w2T.ap()[ex][:, dq4 * 512:(dq4 + 1) * 512]
                               .rearrange("(k p) d -> p k d", p=128), writes=[bw2])
                    if True:
                        for tt in range(NTH):
                            dcol = dq4 * 512
                            pairs = [(h_s[:, ic * THF + tt * 128: ic * THF + (tt + 1) * 128],
                                      w2b[:, ic * 512:(ic + 1) * 512])
                                     for ic in range(8)]
                            pp = 4 + (dq4 * NTH + tt) % 2
                            mm_group(ps[pp][:, :], bps[pp], pairs, [b_h, bw2])
                            ysl = y_s[:, tt * D + dcol: tt * D + dcol + 512]
                            tile = hf * NTH + tt
                            if init and ex == 8:
                                act(lambda s, _y=ysl, _pp=pp: s.activation(
                                    _y, ps[_pp][:, :], AF.Copy),
                                    [bps[pp]], [b_y])
                            elif init:
                                act(lambda s, _y=ysl, _pp=pp, _t=tile, _e=ex:
                                    s.activation(
                                        _y, ps[_pp][:, :], AF.Identity,
                                        bias=zro_s[:, 0:1],
                                        scale=dw_s[:, _t * 16 + _e: _t * 16 + _e + 1]),
                                    [bps[pp], b_dw, b_zro], [b_y])
                            elif ex == 8:
                                vec(lambda v, _y=ysl, _pp=pp: v.tensor_tensor(
                                    _y, _y, ps[_pp][:, :], ALU.add),
                                    [bps[pp], b_y], [b_y])
                            else:
                                vec(lambda v, _y=ysl, _pp=pp, _t=tile, _e=ex:
                                    v.scalar_tensor_tensor(
                                        _y, ps[_pp][:, :],
                                        dw_s[:, _t * 16 + _e: _t * 16 + _e + 1],
                                        _y, ALU.mult, ALU.add),
                                    [bps[pp], b_dw, b_y], [b_y])

            def emit_out_stats(hf):
                for tt in range(2):
                    row0 = hf * THF + tt * 128
                    act_dma(d_x2b if tt % 2 == 0 else d_x2c,
                            x2b_t[tt % 2][:], scr_x2.ap()[row0:row0 + 128, :],
                            reads=[b_sx2], writes=[b_x2b[tt % 2]])
                ln_stats_staged([(y_s[:, tt * D:(tt + 1) * D], b_y, tt % 8)
                                 for tt in range(NTH)])

            def emit_out_tile(hf, tt):
                if tt >= NTH:
                    return
                row0 = hf * THF + tt * 128
                x2b, bx2b = x2b_t[tt % 2], b_x2b[tt % 2]
                o = 32 * (tt % 8)
                zbc = zro_s[:, 0:1].broadcast_to([128, D])
                vec(lambda v, _x=x2b, _b=o, _z=zbc: v.scalar_tensor_tensor(
                    _x[:], _x[:], mv_s[:, _b + 2:_b + 3], _z,
                    ALU.add, ALU.add),
                    [bx2b, b_mv[tt % 8], b_zro], [bx2b])
                vec(lambda v, _x=x2b, _t=tt, _b=o: v.scalar_tensor_tensor(
                    _x[:], y_s[:, _t * D:(_t + 1) * D],
                    mv_s[:, _b + 1:_b + 2], _x[:], ALU.mult, ALU.add),
                    [b_y, b_mv[tt % 8], bx2b], [bx2b])
                pw = [(s_out, s_out.v)] if s_out.v else []
                R.emit("sync", lambda s, _r=row0, _x=x2b: s.dma_start(
                    out=out_d.ap()[_r:_r + 128, :], in_=_x[:]),
                    reads=[bx2b], writes=[], inc=s_out, inc_n=16,
                    pre_waits=pw)
                if tt + 2 < NTH:
                    row2 = hf * THF + (tt + 2) * 128
                    act_dma(d_x2b if tt % 2 == 0 else d_x2c,
                            x2b[:], scr_x2.ap()[row2:row2 + 128, :],
                            reads=[b_sx2], writes=[bx2b])

            def emit_out(hf):
                emit_out_stats(hf)
                for tt in range(NTH):
                    emit_out_tile(hf, tt)

            load_xnh(0)
            for ei, ex in enumerate([8] + list(range(8))):
                emit_h(0, ex, ei % 2)
                emit_y(0, ex, ei % 2)
            load_xnh(1)
            emit_h(1, 7, 1, ici_order=[7, 6, 5, 4, 3, 2, 1, 0], skips=(7, 6))
            emit_out_stats(0)
            h_seq = [6, 5, 4, 3, 2, 1, 0, 8]
            emit_h(1, h_seq[0], 0, out_cb=lambda c: emit_out_tile(0, c))
            emit_y(1, 7, 1, dq_order=[3, 2, 1, 0], skips=(3, 2), init=True)
            for i in range(1, 8):
                emit_h(1, h_seq[i], i % 2)
                emit_y(1, h_seq[i - 1], (i - 1) % 2, init=False)
            emit_y(1, 8, 1, init=False)
            emit_out(1)

            R.flush()
            @block.sync
            def _(sy):
                sy.wait_ge(s_out.h, s_out.v)

    return nc


def _stage_inputs(cfg, L0_attn_x, wq, wk, wv, wo, gate_w, gate_b,
                  W1, W2, W3, Ws1, Ws2, Ws3):
    dd, qd, kd, blk, bb, kb = decay_tables()
    cfg.blk_vals = blk
    f32 = np.float32
    shared = {
        "wqT": np.ascontiguousarray(wq.T, f32),
        "wkT": np.ascontiguousarray(wk.T, f32),
        "wvT": np.ascontiguousarray(wv.T, f32),
        "woT": np.ascontiguousarray(wo.T, f32),
        "gwT": np.ascontiguousarray(gate_w.T, f32),
        "gb": np.ascontiguousarray(np.broadcast_to(gate_b, (128, 8)), f32),
        "w1T": np.ascontiguousarray(
            np.concatenate([W1.transpose(0, 2, 1), Ws1.T[None]], 0), f32),
        "w3T": np.ascontiguousarray(
            np.concatenate([W3.transpose(0, 2, 1), Ws3.T[None]], 0), f32),
        "w2T": np.ascontiguousarray(
            np.concatenate([W2.transpose(0, 2, 1), Ws2.T[None]], 0), f32),
        "ddiag": dd, "qdec": qd, "kdec": kd, "blkbc": bb, "kdecbc": kb,
        "ident": np.eye(128, dtype=ml_dtypes.bfloat16),
    }
    cos0, sin0 = rope_tables(0, cfg.t_pre)
    in_maps = []
    for c in range(8):
        b, half = c // 2, c % 2
        m0 = half * cfg.t_main
        cosm, sinm = rope_tables(m0, cfg.t_main)
        im = dict(shared)
        im["x"] = np.ascontiguousarray(L0_attn_x[b, m0:m0 + cfg.t_main, :], f32)
        im["xpre"] = (np.ascontiguousarray(L0_attn_x[b, 0:cfg.t_pre, :], f32)
                      if half == 1 else np.zeros((cfg.t_pre, D), f32))
        im["cosp"], im["sinp"] = cos0, sin0
        im["cosm"], im["sinm"] = cosm, sinm
        in_maps.append(im)
    return in_maps


def kernel(L0_attn_x, L0_attn_wq, L0_attn_wk, L0_attn_wv, L0_attn_wo,
           L0_ffn_gate_w, L0_ffn_gate_b, L0_ffn_W1, L0_ffn_W2, L0_ffn_W3,
           L0_ffn_Ws1, L0_ffn_Ws2, L0_ffn_Ws3):
    cfg = Cfg(4, 4)
    in_maps = _stage_inputs(cfg, np.asarray(L0_attn_x), np.asarray(L0_attn_wq),
                            np.asarray(L0_attn_wk), np.asarray(L0_attn_wv),
                            np.asarray(L0_attn_wo), np.asarray(L0_ffn_gate_w),
                            np.asarray(L0_ffn_gate_b), np.asarray(L0_ffn_W1),
                            np.asarray(L0_ffn_W2), np.asarray(L0_ffn_W3),
                            np.asarray(L0_ffn_Ws1), np.asarray(L0_ffn_Ws2),
                            np.asarray(L0_ffn_Ws3))
    nc = build_program(cfg)
    res = run_bass_kernel_spmd(nc, in_maps, core_ids=list(range(8)))
    out = np.empty((B, S, D), np.float32)
    for c in range(8):
        b, half = c // 2, c % 2
        out[b, half * cfg.t_main:(half + 1) * cfg.t_main, :] = \
            np.asarray(res.results[c]["out"], np.float32)
    return out
